# version 05
# baseline (speedup 1.0000x reference)
import numpy as np
from collections import defaultdict
from contextlib import ExitStack

import concourse.bass as bass
import concourse.mybir as mybir
from concourse.bass_utils import run_bass_kernel_spmd

F32 = mybir.dt.float32
BF16 = mybir.dt.bfloat16
I8 = mybir.dt.int8
AF = mybir.ActivationFunctionType
ALU = mybir.AluOpType
AX = mybir.AxisListType

D = 1024
T = 2048
NCORES = 8
CH = 512
NCH = T // CH
DT = D // 128
NTT = T // 128
EPS = 1e-5

ENG_ATTR = {"pe": "tensor", "act": "scalar", "dve": "vector", "pool": "gpsimd", "sp": "sync"}
ENG_ORDER = ["pe", "act", "dve", "pool", "sp"]


class Prog:

    def __init__(self, nc, es):
        self.nc = nc
        self.es = es
        self.q = {e: [] for e in ENG_ORDER}
        self.cnt = defaultdict(int)
        self.mark = {e: defaultdict(int) for e in ENG_ORDER}
        self.semh = {}

    def _sem(self, name):
        if name not in self.semh:
            self.semh[name] = self.es.enter_context(self.nc.semaphore(name))
        return self.semh[name]

    def op(self, eng, fn, deps=(), sem=None, inc=1):
        dl = []
        for d in deps or ():
            if d is None:
                continue
            s, v = d
            if self.mark[eng][s] < v:
                dl.append((s, v))
                self.mark[eng][s] = v
        tok = None
        if inc:
            if sem is None:
                sem = "s_" + eng
            self.cnt[sem] += inc
            tok = (sem, self.cnt[sem])
            self._sem(sem)
        for s, _v in dl:
            self._sem(s)
        self.q[eng].append((dl, fn, (sem, inc) if inc else None))
        return tok

    def dma(self, eng, out, in_, deps=(), **kw):
        sem = "d_" + eng
        return self.op(eng, lambda E: E.dma_start(out, in_, **kw), deps, sem=sem, inc=16)

    def fence(self, eng):
        if not hasattr(self, "_fence_scr"):
            self._fence_scr = self.nc.alloc_sbuf_tensor("fence_scr", [1, 8], F32)
        tok_dma = ("d_" + eng, self.cnt["d_" + eng])
        scr = self._fence_scr
        return self.op(eng, lambda E: E.memset(scr[0:1, 0:1], 0.0), deps=[tok_dma], inc=1)

    def mm(self, out, lhsT, rhs, start, stop, deps=(), inc=1, tile_position=None):
        def fn(E, out=out, lhsT=lhsT, rhs=rhs, start=start, stop=stop, tp=tile_position):
            kw = {"skip_group_check": True}
            if tp is not None:
                kw["tile_position"] = tp
            return E.matmul(out, lhsT, rhs, start=start, stop=stop, **kw)
        return self.op("pe", fn, deps, inc=inc)

    def now(self):
        return [(s, v) for s, v in self.cnt.items()]

    def finalize(self):
        toks = [(s, v) for s, v in self.cnt.items() if s.startswith("d_")]
        self.op("sp", lambda E: None, toks, inc=0)

    def emit(self):
        self.finalize()
        with self.nc.Block() as blk:
            for e in ENG_ORDER:
                if not self.q[e]:
                    continue
                section = getattr(blk, ENG_ATTR[e])

                def body(E, e=e):
                    for dl, fn, incinfo in self.q[e]:
                        for s, v in dl:
                            E.wait_ge(self.semh[s], v)
                        ins = fn(E)
                        if incinfo is not None:
                            sem, amt = incinfo
                            ins.then_inc(self.semh[sem], amt)

                section(body)


class Arena:

    def __init__(self, nc, name, nbytes):
        self.h = nc.alloc_sbuf_tensor(name, [128, nbytes], I8)
        self.vf = self.h.bitcast(F32)
        self.vb = self.h.bitcast(BF16)
        self.nbytes = nbytes

    def f(self, off, n):
        assert off % 4 == 0 and off + 4 * n <= self.nbytes
        return self.vf[:, off // 4:off // 4 + n]

    def b(self, off, n):
        assert off % 2 == 0 and off + 2 * n <= self.nbytes
        return self.vb[:, off // 2:off // 2 + n]


B_ST1, B_ST2, B_BC1, B_BC2, B_MM0, B_MM1, B_SC0, B_SC1 = range(8)


class Ctx:

    def __init__(self, nc, es):
        self.nc = nc
        self.p = Prog(nc, es)
        p = self.p
        self.ones_row_f = alloc(nc, "ones_row_f", [1, 128], F32)
        self.ones_blk = alloc(nc, "ones_blk", [128, 64], F32)
        self.ones_row_b = alloc(nc, "ones_row_b", [1, 128], BF16)
        self.ones_col_b = alloc(nc, "ones_col_b", [128, 1], BF16)
        self.ones_col_f = alloc(nc, "ones_col_f", [128, 1], F32)
        toks = [p.op("pool", lambda E, t=t: E.memset(t[:], 1.0))
                for t in (self.ones_row_f, self.ones_row_b, self.ones_col_b, self.ones_col_f,
                          self.ones_blk)]
        self.ones_tok = toks
        self.cc_map = {}
        self.xb = alloc(nc, "c_xb", [128, DT * CH], BF16)
        self.as8 = Arena(nc, "c_as8", DT * CH * 2)
        self.sq = self.as8.b(0, DT * CH)
        self.apl = self.as8.f(0, 2 * CH)
        self.exf = alloc(nc, "c_exf", [128, CH], F32)
        self.exf2 = self.xb.bitcast(F32)[:, 0:CH]
        self.rows = alloc(nc, "c_rows", [1, 4 * CH], F32)
        self.srow = alloc(nc, "c_srow", [1, 64], F32)
        self.rrow = alloc(nc, "c_rrow", [1, CH], F32)
        self.rstd_b = alloc(nc, "c_rstdb", [128, CH], F32)
        self.mr_b = alloc(nc, "c_mrb", [128, CH], F32)
        self.ncol = alloc(nc, "c_ncol", [128, 8], F32)
        self.bcols = alloc(nc, "c_bcols", [128, 48], F32)
        self.brow = alloc(nc, "c_brow", [1, 1024], F32)
        self.m128 = alloc(nc, "c_m128", [128, 256], F32)
        self.identf = alloc(nc, "c_identf", [128, 128], F32)
        self.p2t = alloc(nc, "c_p2t", [128, 128], BF16)
        self.lnxt = alloc(nc, "c_lnxt", [128, DT * T], BF16)
        self.aw = Arena(nc, "arena_w", 49152)
        self.aa = Arena(nc, "arena_a", 90112)
        self.bank = [nc.alloc_psum_tensor(f"bank{i}", [128, CH], F32) for i in range(8)]
        self.bank_war = [None] * 8
        self.mxc_war = None
        self.ln_last_use = None
        self.bc_last_use = None
        self.apply_last = None
        self.rows_war = None

    def cc(self, val):
        if val not in self.cc_map:
            t = alloc(self.nc, f"cc_{len(self.cc_map)}", [128, 1], F32)
            tok = self.p.op("pool", lambda E, t=t, v=val: E.memset(t[:], float(v)))
            self.cc_map[val] = (t, tok)
        return self.cc_map[val]


    def bcast_row(self, row_ap, n, bank_i, deps, evac_to=None, evac_eng="act", evac_deps=()):
        p = self.p
        bk = self.bank[bank_i]
        t = p.mm(bk[:, 0:n], self.ones_row_f[0:1, :], row_ap, True, True,
                 deps=list(deps) + self.ones_tok)
        if evac_to is None:
            return bk[:, 0:n], t
        if evac_eng == "act":
            t2 = p.op("act", lambda E: E.copy(evac_to, bk[:, 0:n]), deps=[t] + list(evac_deps))
        else:
            t2 = p.op("dve", lambda E: E.tensor_copy(evac_to, bk[:, 0:n]), deps=[t] + list(evac_deps))
        return evac_to, t2

    def load_w(self, dstv, hbm_ap, ncols, deps=(), eng="pool"):
        ntiles = hbm_ap.shape[0] // 128
        i = hbm_ap.rearrange("(a p) m -> p a m", p=128)
        o = dstv.rearrange("p (a m) -> p a m", m=ncols)
        return self.p.dma(eng, o, i, deps=deps)

    def load_bias_col(self, hbm_ap, ncols, col_off, deps=(), scale=None):
        p = self.p
        dst = self.bcols[:, col_off:col_off + ncols]
        p.dma("pool", dst, hbm_ap, deps=deps)
        t = p.fence("pool")
        if scale is not None:
            t = p.op("dve", lambda E: E.tensor_scalar_mul(dst, dst, float(scale)), deps=[t])
        return t


def alloc(nc, name, shape, dtype):
    return nc.alloc_sbuf_tensor(name, shape, dtype)



def emit_ln_chunk(cx, src, c, keep_deps=()):
    p, nc = cx.p, cx.nc
    R = cx.rows
    st1, st2 = cx.bank[B_ST1], cx.bank[B_ST2]
    prev = list(cx.ln_last_use or []) + list(keep_deps) + cx.ones_tok
    srcr = src.rearrange("(a p) m -> p a m", p=128)
    p.dma("pool", cx.xb.rearrange("p (a m) -> p a m", m=CH),
          srcr[:, :, c * CH:(c + 1) * CH], deps=prev)
    xb_f = p.fence("pool")
    sq_t = [None] * DT
    mmA_t = mmB_t = None
    for dt in range(DT):
        sl = cx.sq[:, dt * CH:(dt + 1) * CH]
        xbs = cx.xb[:, dt * CH:(dt + 1) * CH]
        sq_t[dt] = p.op("dve", lambda E, o=sl, i=xbs: E.tensor_mul(o, i, i),
                        deps=[xb_f] + (prev if dt == 0 else []))
        mmA_t = p.mm(st1[0:1, :], cx.ones_col_b[:, 0:1], xbs,
                     start=(dt == 0), stop=(dt == DT - 1),
                     deps=[xb_f] + (prev if dt == 0 else []),
                     inc=1 if dt == DT - 1 else 0)
        mmB_t = p.mm(st2[0:1, :], cx.ones_col_b[:, 0:1], sl,
                     start=(dt == 0), stop=(dt == DT - 1),
                     deps=[sq_t[dt]], inc=1 if dt == DT - 1 else 0)
    rows_war = list(cx.rows_war or [])
    m_t = p.op("dve", lambda E: E.tensor_scalar_mul(R[0:1, 0:CH], st1[0:1, :], 1.0 / D),
               deps=[mmA_t] + rows_war)
    msq_t = p.op("dve", lambda E: E.tensor_mul(R[0:1, CH:2 * CH], R[0:1, 0:CH], R[0:1, 0:CH]),
                 deps=[m_t])
    v1_t = p.op("dve", lambda E: E.tensor_scalar_mul(R[0:1, 2 * CH:3 * CH], st2[0:1, :], 1.0 / D),
                deps=[mmB_t])
    var_t = p.op("dve", lambda E: E.tensor_sub(R[0:1, 2 * CH:3 * CH], R[0:1, 2 * CH:3 * CH],
                                               R[0:1, CH:2 * CH]),
                 deps=[msq_t, v1_t])
    epsc, epst = cx.cc(EPS)
    ln_t = p.op("act", lambda E: E.activation(R[0:1, CH:2 * CH], R[0:1, 2 * CH:3 * CH], AF.Ln,
                                              bias=epsc[0:1, 0:1]),
                deps=[var_t, epst] + rows_war)
    rstd_t = p.op("act", lambda E: E.activation(R[0:1, 3 * CH:4 * CH], R[0:1, CH:2 * CH], AF.Exp,
                                                scale=-0.5),
                  deps=[ln_t])
    mr_t = p.op("dve", lambda E: E.tensor_mul(R[0:1, 0:CH], R[0:1, 0:CH], R[0:1, 3 * CH:4 * CH]),
                deps=[rstd_t])
    bprev = list(cx.bc_last_use or [])
    apply_war = list(cx.apply_last or [])
    _, e1 = cx.bcast_row(R[0:1, 3 * CH:4 * CH], CH, B_BC1, [rstd_t] + bprev,
                         evac_to=cx.rstd_b[:], evac_deps=apply_war)
    _, e2 = cx.bcast_row(R[0:1, 0:CH], CH, B_BC2, [mr_t] + bprev,
                         evac_to=cx.mr_b[:], evac_deps=apply_war)
    cx.bc_last_use = [e1, e2]
    cx.rows_war = [("s_pe", p.cnt["s_pe"])]
    out_t = []
    ap_t = [None, None]
    for dt in range(DT):
        asl = cx.apl[:, (dt % 2) * CH:((dt % 2) + 1) * CH]
        xbs = cx.xb[:, dt * CH:(dt + 1) * CH]
        t1 = p.op("dve", lambda E, o=asl, i=xbs: E.tensor_mul(o, i, cx.rstd_b[:]),
                  deps=[xb_f, e1, ap_t[dt % 2]])
        osl = cx.lnxt[:, dt * T + c * CH:dt * T + c * CH + CH]
        t2 = p.op("dve", lambda E, o=osl, i=asl: E.tensor_sub(o, i, cx.mr_b[:]),
                  deps=[t1, e2])
        ap_t[dt % 2] = t2
        out_t.append(t2)
    cx.apply_last = out_t
    cx.ln_last_use = out_t + [mmA_t, mmB_t, m_t, v1_t]
    return out_t


def fm_evac(cx, bank_i, dst_ap, bias_col=None, deps=(), scale=None):
    p = cx.p
    bk = cx.bank[bank_i]
    n = dst_ap.shape[-1]
    if bias_col is not None:
        assert scale is None
        fn = lambda E: E.add(dst_ap, bk[:, 0:n], bias_col)
    elif scale is not None:
        fn = lambda E: E.activation(dst_ap, bk[:, 0:n], AF.Copy, scale=float(scale))
    else:
        fn = lambda E: E.copy(dst_ap, bk[:, 0:n])
    return p.op("act", fn, deps=deps)



def emit_l0ffn_prep(cx, hbm):
    p, nc = cx.p, cx.nc
    bar = list(getattr(cx, "sub_bar", []) or [])
    wq = cx.aw.b(0, DT * 1024)
    wstage = cx.aw.f(16384, DT * 1024)
    red = cx.ncol
    R = cx.srow
    ts3 = cx.aa.f(0, 3 * 1024)
    ps = cx.bank[B_ST1]

    cx.load_w(wstage, hbm["w0T"], 1024, deps=bar)
    p.dma("pool", cx.bcols[:, 0:8], hbm["b0col"])
    p.dma("pool", R[0:1, 0:1], hbm["alpha1"])
    ld_f = p.fence("pool")
    rt = []
    for i in range(DT):
        rt.append(p.op("dve", lambda E, i=i: E.tensor_reduce(
            red[:, i:i + 1], wstage[:, i * 1024:(i + 1) * 1024], AX.X, ALU.add,
            apply_absolute_value=True), deps=[ld_f]))
    sm = p.mm(ps[0:1, 8:16], cx.ones_col_f[:, 0:1], red[:, 0:8], True, True,
              deps=rt + cx.ones_tok)
    tot = p.op("dve", lambda E: E.reduce_sum(R[0:1, 1:2], ps[0:1, 8:16], axis=AX.X), deps=[sm])
    d1 = p.op("dve", lambda E: E.tensor_scalar_mul(R[0:1, 2:3], R[0:1, 1:2], 0.7 / (D * 1024.0)), deps=[tot])
    d2 = p.op("dve", lambda E: E.tensor_scalar_mul(R[0:1, 3:4], R[0:1, 2:3], -1.0), deps=[d1])
    d3 = p.op("dve", lambda E: E.tensor_scalar_mul(R[0:1, 4:5], R[0:1, 0:1], 0.5), deps=[ld_f])
    bc = p.mm(ps[:, 0:3], cx.ones_row_f[0:1, :], R[0:1, 2:5], True, True,
              deps=[d1, d2, d3, tot] + cx.ones_tok)
    be = p.op("dve", lambda E: E.tensor_copy(red[:, 1:4], ps[:, 0:3]), deps=[bc])
    toks = []
    tprev = None
    for i in range(DT):
        w_sl = wstage[:, i * 1024:(i + 1) * 1024]
        a_t = p.op("dve", lambda E, w=w_sl: E.tensor_single_scalar(ts3[:, 0:1024], w, red[:, 1:2], ALU.is_gt),
                   deps=[be, tprev])
        b_t = p.op("dve", lambda E, w=w_sl: E.tensor_single_scalar(ts3[:, 1024:2048], w, red[:, 2:3], ALU.is_lt),
                   deps=[be])
        c_t = p.op("dve", lambda E: E.tensor_sub(ts3[:, 2048:3072], ts3[:, 0:1024], ts3[:, 1024:2048]),
                   deps=[a_t, b_t])
        o_t = p.op("dve", lambda E, i=i: E.tensor_single_scalar(
            wq[:, i * 1024:(i + 1) * 1024], ts3[:, 2048:3072], red[:, 3:4], ALU.mult), deps=[c_t])
        tprev = o_t
        toks.append(o_t)
    bh = p.op("dve", lambda E: E.tensor_scalar_mul(cx.bcols[:, 0:8], cx.bcols[:, 0:8], 0.5), deps=[ld_f])
    return wq, toks + [bh, d3]


def emit_l0ffn(cx, wq, src, dst, wtok):
    p = cx.p
    ev_prev = [None, None]
    st_prev = [None, None]
    exf_prev = [None, None]
    extra = list(wtok)
    for c in range(NCH):
        ln_t = emit_ln_chunk(cx, src, c, keep_deps=extra)
        extra = []
        for ot in range(DT):
            pb = B_MM0 + (ot % 2)
            eb = ot % 2
            xo = cx.exf[:, 0:CH]
            lx = p.dma("sp", xo, src[ot * 128:(ot + 1) * 128, c * CH:(c + 1) * CH],
                       deps=[exf_prev[0], exf_prev[1]])
            mt = None
            for dt in range(DT):
                mt = p.mm(cx.bank[pb][:],
                          wq[:, dt * 1024 + ot * 128:dt * 1024 + ot * 128 + 128],
                          cx.lnxt[:, dt * T + c * CH:dt * T + c * CH + CH],
                          start=(dt == 0), stop=(dt == DT - 1),
                          deps=([ln_t[dt]] if dt > 0 else [ln_t[0], ev_prev[ot % 2]]),
                          inc=1 if dt == DT - 1 else 0)
            xsl = cx.apl[:, eb * CH:(eb + 1) * CH]
            e_t = p.op("dve", lambda E, o=xsl, pm=cx.bank[pb], bo=ot, x=xo: E.scalar_tensor_tensor(
                o, pm[:], cx.bcols[:, bo:bo + 1], x, ALU.add, ALU.add),
                deps=[mt, lx, st_prev[eb], cx.apply_last[-1] if cx.apply_last else None])
            ev_prev[ot % 2] = e_t
            exf_prev[eb] = e_t
            extra.append(e_t)
            st_prev[eb] = p.dma("sp", dst[ot * 128:(ot + 1) * 128, c * CH:(c + 1) * CH],
                                xsl, deps=[e_t])
        cx.apply_last = (cx.apply_last or []) + [t for t in st_prev if t]
    return [t for t in st_prev if t]



def emit_l0attn(cx, hbm, src, dst, stop_after=None):
    p, nc = cx.p, cx.nc
    KT = cx.aa.b(0, DT * T)
    V = cx.aa.b(32768, NTT * 1024)
    AT = cx.aa.b(65536, NTT * CH)
    QC = cx.aa.b(81920, DT * CH)
    OC = QC
    wqp = cx.aw.b(0, DT * 1024)
    wkp = cx.aw.b(16384, DT * 1024)
    wvp = cx.aw.b(32768, DT * 1024)
    wproj = cx.aw.b(16384, DT * 1024)

    lastln = []
    for c in range(NCH):
        lastln = emit_ln_chunk(cx, src, c)
    ln_all = list(lastln) + list(cx.ln_last_use or [])

    if stop_after == "A":
        for dtl in range(DT):
            p.dma("sp", dst[dtl * 128:(dtl + 1) * 128, :], src[dtl * 128:(dtl + 1) * 128, :],
                  deps=ln_all)
        return []
    for part, dstv in ((0, wqp), (1, wkp), (2, wvp)):
        cx.load_w(dstv, hbm["wqkv"][:, part * 1024:(part + 1) * 1024], 1024)
    p.dma("pool", cx.bcols[:, 8:32], hbm["bqkv_col"])
    p.dma("pool", cx.brow[0:1, 0:1024], hbm["bv_row"])
    p.dma("pool", cx.m128[:, 0:128], hbm["diffT"])
    p.dma("pool", cx.m128[:, 128:256], hbm["trilT"])
    p.dma("pool", cx.identf[:], hbm["identf"])
    p.dma("pool", cx.srow[0:1, 8:9], hbm["gamma1"])
    w_f = p.fence("pool")
    lg = p.op("act", lambda E: E.activation(cx.srow[0:1, 9:10], cx.srow[0:1, 8:9], AF.Ln), deps=[w_f])
    bcg = p.mm(cx.bank[B_SC0][:, 0:1], cx.ones_row_f[0:1, :], cx.srow[0:1, 9:10], True, True,
               deps=[lg] + cx.ones_tok)
    lgc = p.op("dve", lambda E: E.tensor_copy(cx.ncol[:, 4:5], cx.bank[B_SC0][:, 0:1]), deps=[bcg])
    mk1 = p.op("dve", lambda E: E.tensor_single_scalar(cx.m128[:, 0:128], cx.m128[:, 0:128],
                                                       cx.ncol[:, 4:5], ALU.mult), deps=[lgc, w_f])
    mk2 = p.op("act", lambda E: E.activation(cx.m128[:, 0:128], cx.m128[:, 0:128], AF.Exp), deps=[mk1])
    mask_t = p.op("dve", lambda E: E.tensor_mul(cx.m128[:, 0:128], cx.m128[:, 0:128],
                                                cx.m128[:, 128:256]), deps=[mk2])

    kv_done = []
    for c in range(NCH):
        for ot in range(DT):
            pb = B_MM0 + (ot % 2)
            war = [t for t in kv_done[-2:]]
            mt = None
            for dt in range(DT):
                mt = p.mm(cx.bank[pb][:], wkp[:, dt * 1024 + ot * 128:dt * 1024 + ot * 128 + 128],
                          cx.lnxt[:, dt * T + c * CH:dt * T + c * CH + CH],
                          start=(dt == 0), stop=(dt == DT - 1),
                          deps=[w_f] + (ln_all + war if dt == 0 else []),
                          inc=1 if dt == DT - 1 else 0)
            kv_done.append(fm_evac(cx, pb, KT[:, ot * T + c * CH:ot * T + c * CH + CH],
                                   bias_col=cx.bcols[:, 16 + ot:17 + ot], deps=[mt]))
    v_done = []
    for tt in range(NTT):
        for vo in range(2):
            pb = B_MM0 + (vo % 2)
            war = [v_done[-2] if len(v_done) >= 2 else (kv_done[-2 + vo] if len(kv_done) >= 2 else None)]
            bi = p.mm(cx.bank[pb][:], cx.ones_row_f[0:1, :], cx.brow[0:1, vo * 512:(vo + 1) * 512],
                      True, False, deps=[w_f] + war + cx.ones_tok, inc=0)
            mt = None
            for dt in range(DT):
                mt = p.mm(cx.bank[pb][:],
                          cx.lnxt[:, dt * T + tt * 128:dt * T + tt * 128 + 128],
                          wvp[:, dt * 1024 + vo * 512:dt * 1024 + vo * 512 + 512],
                          start=False, stop=(dt == DT - 1), deps=[],
                          inc=1 if dt == DT - 1 else 0)
            v_done.append(fm_evac(cx, pb, V[:, tt * 1024 + vo * 512:tt * 1024 + vo * 512 + 512],
                                  deps=[mt]))
    wp_war = kv_done
    cx.load_w(wproj, hbm["wproj"], 1024, deps=wp_war)
    p.dma("pool", cx.bcols[:, 32:40], hbm["bproj_col"])
    wp_f = p.fence("pool")

    if stop_after == "B":
        for dtl in range(DT):
            p.dma("sp", dst[dtl * 128:(dtl + 1) * 128, :], src[dtl * 128:(dtl + 1) * 128, :],
                  deps=v_done + kv_done + [wp_f])
        return []
    SCALE = 1.0 / 32.0
    qc_war = []
    at_war = []
    exf_prev = [None, None]
    st_prev = [None, None]
    rb_war = []
    av_evac_war = []
    prev_ex = []
    for c in range(1 if stop_after == "one" else NCH):
        q_done = []
        for ot in range(DT):
            pb = B_MM0 + (ot % 2)
            war = qc_war + [q_done[-2] if len(q_done) >= 2 else None]
            mt = None
            for dt in range(DT):
                mt = p.mm(cx.bank[pb][:], wqp[:, dt * 1024 + ot * 128:dt * 1024 + ot * 128 + 128],
                          cx.lnxt[:, dt * T + c * CH:dt * T + c * CH + CH],
                          start=(dt == 0), stop=(dt == DT - 1),
                          deps=(war + v_done[-2:] if dt == 0 else []),
                          inc=1 if dt == DT - 1 else 0)
            q_done.append(fm_evac(cx, pb, QC[:, ot * CH:(ot + 1) * CH],
                                  bias_col=cx.bcols[:, 8 + ot:9 + ot], deps=[mt]))
        qc_war = []
        if stop_after == "C1":
            break
        mx_t = []
        for tk in range(NTT):
            pb = B_SC0 + (tk % 2)
            mt = None
            for ft in range(DT):
                mt = p.mm(cx.bank[pb][:], KT[:, ft * T + tk * 128:ft * T + tk * 128 + 128],
                          QC[:, ft * CH:(ft + 1) * CH],
                          start=(ft == 0), stop=(ft == DT - 1),
                          deps=([kv_done[-1], q_done[-1], mask_t] if tk < 2 and ft == 0 else
                                ([mx_t[tk - 2]] if ft == 0 else [])),
                          inc=1 if ft == DT - 1 else 0)
            mx_t.append(p.op("dve", lambda E, tk=tk, pb=pb: E.reduce_max(
                cx.m128[:, 128 + tk:129 + tk], cx.bank[pb][:], axis=AX.X),
                deps=[mt, mask_t] if tk == 0 else [mt]))
        mxc = p.op("dve", lambda E: E.reduce_max(cx.ncol[:, 6:7], cx.m128[:, 128:144], axis=AX.X),
                   deps=mx_t)
        tr = p.op("pe", lambda E: E.transpose(cx.bank[B_BC2][0:1, 0:128], cx.ncol[:, 6:7], cx.identf[:]),
                  deps=[mxc] + rb_war)
        gmx = p.op("dve", lambda E: E.reduce_max(cx.srow[0:1, 10:11], cx.bank[B_BC2][0:1, 0:128], axis=AX.X),
                   deps=[tr])
        gm2 = p.op("dve", lambda E: E.tensor_scalar_max(cx.srow[0:1, 10:11], cx.srow[0:1, 10:11], 0.0),
                   deps=[gmx])
        ngm = p.op("dve", lambda E: E.tensor_scalar_mul(cx.srow[0:1, 11:12], cx.srow[0:1, 10:11],
                                                        -SCALE), deps=[gm2])
        bcm = p.mm(cx.bank[B_BC2][:, 0:1], cx.ones_row_f[0:1, :], cx.srow[0:1, 11:12], True, True,
                   deps=[ngm])
        nmc = p.op("dve", lambda E: E.tensor_copy(cx.ncol[:, 7:8], cx.bank[B_BC2][:, 0:1]),
                   deps=[bcm] + prev_ex)
        if stop_after == "C2":
            break
        ex_t = []
        for tk in range(NTT):
            pb = B_SC0 + (tk % 2)
            mt = None
            for ft in range(DT):
                mt = p.mm(cx.bank[pb][:], KT[:, ft * T + tk * 128:ft * T + tk * 128 + 128],
                          QC[:, ft * CH:(ft + 1) * CH],
                          start=(ft == 0), stop=(ft == DT - 1),
                          deps=([nmc] + at_war if ft == 0 and tk < 2 else
                                ([ex_t[tk - 2]] if ft == 0 else [])),
                          inc=1 if ft == DT - 1 else 0)
            dtok = mt
            if c * 4 <= tk < (c + 1) * 4:
                qb = (tk - c * 4) * 128
                dtok = p.op("dve", lambda E, pb=pb, qb=qb: E.tensor_mul(
                    cx.bank[pb][:, qb:qb + 128], cx.bank[pb][:, qb:qb + 128], cx.m128[:, 0:128]),
                    deps=[mt])
            ex_t.append(p.op("act", lambda E, pb=pb, tk=tk: E.activation(
                AT[:, tk * CH:(tk + 1) * CH], cx.bank[pb][:], AF.Exp,
                bias=cx.ncol[:, 7:8], scale=SCALE), deps=[dtok]))
        at_war = []
        prev_ex = [ex_t[-1]]
        if stop_after == "C3":
            break
        rs_t = None
        for tk in range(NTT):
            rs_t = p.mm(cx.bank[B_ST1][0:1, :], cx.ones_col_b[:, 0:1], AT[:, tk * CH:(tk + 1) * CH],
                        start=(tk == 0), stop=(tk == NTT - 1),
                        deps=[ex_t[tk]] + (rb_war if tk == 0 else []),
                        inc=1 if tk == NTT - 1 else 0)
        rc = p.op("dve", lambda E: E.reciprocal(cx.rrow[0:1, 0:CH], cx.bank[B_ST1][0:1, :]),
                  deps=[rs_t] + rb_war)
        _, rbt = cx.bcast_row(cx.rrow[0:1, 0:CH], CH, B_BC1, [rc], evac_to=cx.rstd_b[:],
                              evac_deps=av_evac_war)
        rb_war = [rbt]
        av_done = []
        av_evac_war = []
        for ot in range(DT):
            pb = B_MM0 + (ot % 2)
            war = [av_done[-2] if len(av_done) >= 2 else q_done[-1]]
            mt = None
            for tk in range(NTT):
                mt = p.mm(cx.bank[pb][:], V[:, tk * 1024 + ot * 128:tk * 1024 + ot * 128 + 128],
                          AT[:, tk * CH:(tk + 1) * CH],
                          start=(tk == 0), stop=(tk == NTT - 1),
                          deps=(war + [ex_t[-1]] if tk == 0 else []),
                          inc=1 if tk == NTT - 1 else 0)
            e = p.op("dve", lambda E, pb=pb, ot=ot: E.tensor_mul(
                OC[:, ot * CH:(ot + 1) * CH], cx.bank[pb][:], cx.rstd_b[:]),
                deps=[mt, rbt])
            av_done.append(e)
            at_war.append(e)
            av_evac_war.append(e)
        if stop_after == "C4":
            continue
        proj_ev = [None, None]
        for ot in range(DT):
            pb = B_MM0 + (ot % 2)
            eb = ot % 2
            xo = cx.exf[:, 0:CH] if eb == 0 else cx.exf2
            lx = p.dma("sp", xo, src[ot * 128:(ot + 1) * 128, c * CH:(c + 1) * CH],
                       deps=[exf_prev[eb]])
            mt = None
            for dt in range(DT):
                mt = p.mm(cx.bank[pb][:], wproj[:, dt * 1024 + ot * 128:dt * 1024 + ot * 128 + 128],
                          OC[:, dt * CH:(dt + 1) * CH],
                          start=(dt == 0), stop=(dt == DT - 1),
                          deps=([wp_f, proj_ev[ot % 2]] + av_done if dt == 0 else []),
                          inc=1 if dt == DT - 1 else 0)
            xsl = cx.apl[:, eb * CH:(eb + 1) * CH]
            e_t = p.op("dve", lambda E, o=xsl, pb=pb, bo=32 + ot, x=xo: E.scalar_tensor_tensor(
                o, cx.bank[pb][:], cx.bcols[:, bo:bo + 1], x, ALU.add, ALU.add),
                deps=[mt, lx, st_prev[eb]])
            exf_prev[eb] = e_t
            proj_ev[ot % 2] = e_t
            qc_war.append(e_t)
            st_prev[eb] = p.dma("sp", dst[ot * 128:(ot + 1) * 128, c * CH:(c + 1) * CH],
                                xsl, deps=[e_t])
        cx.apply_last = (cx.apply_last or []) + [t for t in st_prev if t]
    if stop_after in ("one",):
        bar = p.now()
        for dtl in range(DT):
            p.dma("sp", dst[dtl * 128:(dtl + 1) * 128, :], src[dtl * 128:(dtl + 1) * 128, :],
                  deps=bar)
        return []
    if stop_after and stop_after.startswith("C"):
        bar = p.now()
        for dtl in range(DT):
            p.dma("sp", dst[dtl * 128:(dtl + 1) * 128, :], src[dtl * 128:(dtl + 1) * 128, :],
                  deps=bar)
        return []
    return [t for t in st_prev if t]



def col_layout(v):
    v = np.asarray(v).reshape(-1)
    n = v.shape[0]
    return np.ascontiguousarray(v.reshape(n // 128, 128).T)


def prep_l0ffn_inputs(inputs):
    return {
        "w0T": np.ascontiguousarray(inputs["L0_ffn_weight"].T),
        "b0col": col_layout(inputs["L0_ffn_bias"]),
        "alpha1": np.asarray(inputs["L0_ffn_alpha"], np.float32).reshape(1, 1),
    }


def prep_l0attn_inputs(inputs):
    i_ = np.arange(128, dtype=np.float32)
    diffT = i_[None, :] - i_[:, None]
    trilT = (diffT >= 0).astype(np.float32)
    bqkv = np.asarray(inputs["L0_attn_bqkv"], np.float32)
    return {
        "wqkv": np.ascontiguousarray(inputs["L0_attn_Wqkv"]),
        "wproj": np.ascontiguousarray(inputs["L0_attn_Wproj"]),
        "bqkv_col": np.ascontiguousarray(
            np.concatenate([col_layout(bqkv[0:1024]), col_layout(bqkv[1024:2048]),
                            col_layout(bqkv[2048:3072])], axis=1)),
        "bv_row": np.ascontiguousarray(bqkv[2048:3072].reshape(1, 1024)),
        "bproj_col": col_layout(inputs["L0_attn_bproj"]),
        "diffT": np.ascontiguousarray(diffT),
        "trilT": np.ascontiguousarray(trilT),
        "identf": np.eye(128, dtype=np.float32),
        "gamma1": np.asarray(inputs["L0_attn_gamma"], np.float32).reshape(1, 1),
    }



def _mk_io(nc, names_shapes):
    hbm = {"xT": nc.dram_tensor("xT", [D, T], F32, kind="ExternalInput").ap(),
           "yT": nc.dram_tensor("yT", [D, T], F32, kind="ExternalOutput").ap()}
    for name, shape, dt in names_shapes:
        hbm[name] = nc.dram_tensor(name, shape, dt, kind="ExternalInput").ap()
    return hbm


def build_l0ffn_program():
    nc = bass.Bass("TRN2", target_bir_lowering=False)
    es = ExitStack()
    hbm = _mk_io(nc, [("w0T", [D, D], F32), ("b0col", [128, 8], F32), ("alpha1", [1, 1], F32)])
    cx = Ctx(nc, es)
    wq, wtok = emit_l0ffn_prep(cx, hbm)
    emit_l0ffn(cx, wq, hbm["xT"], hbm["yT"], wtok)
    cx.p.emit()
    es.close()
    return nc


def build_l0attn_program(stop_after=None):
    nc = bass.Bass("TRN2", target_bir_lowering=False)
    es = ExitStack()
    hbm = _mk_io(nc, [("wqkv", [D, 3 * D], F32), ("wproj", [D, D], F32),
                      ("bqkv_col", [128, 24], F32), ("bv_row", [1, 1024], F32),
                      ("bproj_col", [128, 8], F32), ("diffT", [128, 128], F32),
                      ("trilT", [128, 128], F32), ("identf", [128, 128], F32),
                      ("gamma1", [1, 1], F32)])
    cx = Ctx(nc, es)
    emit_l0attn(cx, hbm, hbm["xT"], hbm["yT"], stop_after=stop_after)
    cx.p.emit()
    es.close()
    return nc


_prog_cache = {}


def _cached(name, builder):
    if name not in _prog_cache:
        _prog_cache[name] = builder()
    return _prog_cache[name]


def _run_sub(progname, builder, prep, x, inputs):
    nc = _cached(progname, builder)
    pre = prep(inputs)
    in_maps = [dict(pre, xT=np.ascontiguousarray(x[c].T)) for c in range(NCORES)]
    res = run_bass_kernel_spmd(nc, in_maps, core_ids=list(range(NCORES)))
    return np.stack([r["yT"].T for r in res.results])


def run_sublayer_l0ffn(x, inputs):
    return _run_sub("l0ffn", build_l0ffn_program, prep_l0ffn_inputs, x, inputs)


def run_sublayer_l0attn(x, inputs):
    return _run_sub("l0attn", build_l0attn_program, prep_l0attn_inputs, x, inputs)





def mha_core(cx, c, KT, V, QC, OC, AT, scale, causal, start_deps):
    p = cx.p
    bw = cx.bank_war
    oc_toks = []
    tk_hi = min(NTT, 4 * (c + 1)) if causal else NTT
    ms_t = p.op("dve", lambda E: E.memset(cx.m128[:, 128:160], 0.0), deps=[cx.ncol_war, cx.mxc_war])

    def lhsE(hp, tk):
        return KT[0:64, hp * T + tk * 128:hp * T + tk * 128 + 128]

    def lhsO(hp, tk):
        return KT[64:128, hp * T + tk * 128:hp * T + tk * 128 + 128]

    def rhsE(hp):
        return QC[0:64, hp * CH:(hp + 1) * CH]

    def rhsO(hp):
        return QC[64:128, hp * CH:(hp + 1) * CH]

    def pass1_step(hp, tk, extra):
        bE = B_ST1 if tk % 2 == 0 else B_ST2
        mE = p.mm(cx.bank[bE][:], lhsE(hp, tk), rhsE(hp), True, True,
                  deps=extra + [bw[bE]])
        rdE = p.op("dve", lambda E, b=bE, tk=tk: E.reduce_max(
            cx.m128[:, 128 + tk:129 + tk], cx.bank[b][:], axis=AX.X), deps=[mE, ms_t])
        bw[bE] = rdE
        mO = p.mm(cx.bank[bE][:], lhsO(hp, tk), rhsO(hp), True, True,
                  deps=[rdE], tile_position=(64, 0))
        rdO = p.op("dve", lambda E, b=bE, tk=tk: E.reduce_max(
            cx.m128[:, 144 + tk:145 + tk], cx.bank[b][:], axis=AX.X), deps=[mO])
        bw[bE] = rdO
        return rdO

    def combine(hp, rd_last):
        mxc = p.op("dve", lambda E: E.reduce_max(
            cx.ncol[:, 4:5], cx.m128[:, 128:160], axis=AX.X),
            deps=[rd_last, cx.ncol_war])
        cx.mxc_war = mxc
        tr = p.op("pe", lambda E: E.transpose(cx.bank[B_BC2][0:1, 0:128],
                                              cx.ncol[:, 4:5],
                                              cx.identf[:]), deps=[mxc, bw[B_BC2]])
        gmx = p.op("dve", lambda E: E.reduce_max(cx.srow[0:1, 16:17], cx.bank[B_BC2][0:1, 0:128],
                                                 axis=AX.X), deps=[tr])
        bw[B_BC2] = gmx
        ngm = p.op("dve", lambda E: E.tensor_scalar(cx.srow[0:1, 17:18], cx.srow[0:1, 16:17],
                                                    0.0, -float(scale), ALU.max, ALU.mult),
                   deps=[gmx])
        bcm = p.mm(cx.bank[B_BC2][:, 0:1], cx.ones_row_f[0:1, :], cx.srow[0:1, 17:18], True, True,
                   deps=[ngm])
        nmc = p.op("dve", lambda E: E.tensor_copy(cx.ncol[:, 6:7],
                                                  cx.bank[B_BC2][:, 0:1]), deps=[bcm])
        bw[B_BC2] = nmc
        cx.ncol_war = nmc
        return nmc

    rd_last = None
    for tk in range(tk_hi):
        rd_last = pass1_step(0, tk, start_deps if tk < 2 else [])
    prev_rd = rd_last

    for hp in range(DT):
        nmc = combine(hp, prev_rd)
        av_e = av_o = rs_e = rs_o = None
        for tk in range(tk_hi):
            bE = B_SC0 if tk % 2 == 0 else B_SC1
            mE = p.mm(cx.bank[bE][:], lhsE(hp, tk), rhsE(hp), True, True,
                      deps=[bw[bE], nmc])
            mO = None
            if hp + 1 < DT:
                prev_rd = pass1_step(hp + 1, tk, [nmc])
            mO = p.mm(cx.bank[B_SC1 if tk % 2 == 0 else B_SC0][:], lhsO(hp, tk), rhsO(hp),
                      True, True, deps=[bw[B_SC1 if tk % 2 == 0 else B_SC0]],
                      tile_position=(64, 0))
            bO = B_SC1 if tk % 2 == 0 else B_SC0
            if causal and tk >= 4 * c:
                nfull = tk - 4 * c
                for b, mtok in ((bE, mE), (bO, mO)):
                    t = mtok
                    if nfull > 0:
                        t = p.op("dve", lambda E, b=b, n=nfull: E.tensor_scalar_add(
                            cx.bank[b][:, 0:n * 128], cx.bank[b][:, 0:n * 128], -1e6), deps=[t])
                    if nfull < 4:
                        t = p.op("dve", lambda E, b=b, q0=nfull * 128: E.tensor_add(
                            cx.bank[b][:, q0:q0 + 128], cx.bank[b][:, q0:q0 + 128],
                            cx.m128[:, 0:128]), deps=[t])
                    if b == bE:
                        mE = t
                    else:
                        mO = t
            sE = AT[:, (2 * (tk % 2)) * CH:(2 * (tk % 2) + 1) * CH]
            sO = AT[:, (2 * (tk % 2) + 1) * CH:(2 * (tk % 2) + 2) * CH]
            eE = p.op("act", lambda E, b=bE, s=sE: E.activation(
                s, cx.bank[b][:], AF.Exp, bias=cx.ncol[:, 6:7], scale=float(scale)),
                deps=[mE, cx.at_war[2 * (tk % 2)]])
            bw[bE] = eE
            eO = p.op("act", lambda E, b=bO, s=sO: E.activation(
                s, cx.bank[b][:], AF.Exp, bias=cx.ncol[:, 6:7], scale=float(scale)),
                deps=[mO, cx.at_war[2 * (tk % 2) + 1]])
            bw[bO] = eO
            first, last = (tk == 0), (tk == tk_hi - 1)
            av_e = p.mm(cx.bank[B_MM0][0:64, :], V[:, tk * 1024 + 2 * hp * 64:tk * 1024 + 2 * hp * 64 + 64],
                        sE, first, False, deps=[eE] + ([bw[B_MM0]] if first else []),
                        inc=0)
            rs_e = p.mm(cx.bank[B_MM0][64:65, :], cx.ones_col_b[:, 0:1], sE, first, last,
                        deps=[], tile_position=(0, 64), inc=1 if last else 0)
            av_o = p.mm(cx.bank[B_MM1][64:128, :],
                        V[:, tk * 1024 + (2 * hp + 1) * 64:tk * 1024 + (2 * hp + 1) * 64 + 64],
                        sO, first, False, deps=[eO] + ([bw[B_MM1]] if first else []),
                        tile_position=(0, 64), inc=0)
            rs_o = p.mm(cx.bank[B_MM1][0:1, :], cx.ones_col_b[:, 0:1], sO, first, last,
                        deps=[], inc=1 if last else 0)
            cx.at_war[2 * (tk % 2)] = rs_e
            cx.at_war[2 * (tk % 2) + 1] = rs_o
        rcE = p.op("dve", lambda E: E.reciprocal(cx.mr_b[64:65, 0:CH], cx.bank[B_MM0][64:65, :]),
                   deps=[rs_e, cx.rrow_war])
        rcO = p.op("dve", lambda E: E.reciprocal(cx.mr_b[0:1, 0:CH], cx.bank[B_MM1][0:1, :]),
                   deps=[rs_o])
        b1 = p.mm(cx.bank[B_BC1][0:64, 0:CH], cx.ones_blk[64:65, 0:64], cx.mr_b[64:65, 0:CH],
                  True, True, deps=[rcE, bw[B_BC1]], tile_position=(64, 0))
        b2 = p.mm(cx.bank[B_BC1][64:128, 0:CH], cx.ones_blk[0:1, 0:64], cx.mr_b[0:1, 0:CH],
                  True, True, deps=[rcO], tile_position=(0, 64))
        cx.rrow_war = b2
        rbt = p.op("act", lambda E: E.copy(cx.rstd_b[:], cx.bank[B_BC1][:, 0:CH]),
                   deps=[b1, b2, cx.rstdb_war])
        bw[B_BC1] = rbt
        oe = p.op("dve", lambda E, hp=hp: E.tensor_mul(
            OC[0:64, hp * CH:(hp + 1) * CH], cx.bank[B_MM0][0:64, :], cx.rstd_b[0:64, :]),
            deps=[rs_e, rbt])
        bw[B_MM0] = oe
        oo = p.op("dve", lambda E, hp=hp: E.tensor_mul(
            OC[64:128, hp * CH:(hp + 1) * CH], cx.bank[B_MM1][64:128, :], cx.rstd_b[64:128, :]),
            deps=[rs_o, rbt])
        bw[B_MM1] = oo
        cx.rstdb_war = oo
        oc_toks += [oe, oo]
    return oc_toks


def emit_l1attn(cx, hbm, src, dst, debug=False):
    p, nc = cx.p, cx.nc
    KT = cx.aa.b(0, DT * T)
    V = cx.aa.b(32768, NTT * 1024)
    AT = cx.aa.b(65536, 4 * CH)
    QC = cx.aa.b(69632, DT * CH)
    OC = cx.aa.b(77824, DT * CH)
    wq = cx.aw.b(0, DT * 1024)
    wkv = cx.aw.b(16384, DT * 1024)
    wo = cx.aw.b(32768, DT * 1024)
    cx.ncol_war = None
    cx.rrow_war = None
    cx.rstdb_war = None
    cx.at_war = [None] * 4

    lastln = []
    for c in range(NCH):
        lastln = emit_ln_chunk(cx, src, c)
    ln_all = list(lastln) + list(cx.ln_last_use or [])

    for name, dstv in (("wq", wq), ("wkv", wkv), ("wo", wo)):
        cx.load_w(dstv, hbm[name], 1024)
    p.dma("pool", cx.bcols[:, 8:16], hbm["kw_col"])
    p.dma("pool", cx.m128[:, 0:128], hbm["maskC"])
    p.dma("pool", cx.identf[:], hbm["identf"])
    w_f = p.fence("pool")
    kw1 = p.op("dve", lambda E: E.tensor_scalar_add(cx.bcols[:, 8:16], cx.bcols[:, 8:16], 1.0),
               deps=[w_f])

    kv_done = []
    for c in range(NCH):
        for ot in range(DT):
            pb = B_MM0 + (ot % 2)
            mt = None
            for dt in range(DT):
                mt = p.mm(cx.bank[pb][:], wkv[:, dt * 1024 + ot * 128:dt * 1024 + ot * 128 + 128],
                          cx.lnxt[:, dt * T + c * CH:dt * T + c * CH + CH],
                          start=(dt == 0), stop=(dt == DT - 1),
                          deps=[w_f, cx.bank_war[pb]] + (ln_all if c == 0 and ot == 0 else []),
                          inc=1 if dt == DT - 1 else 0)
            e = p.op("act", lambda E, pb=pb, ot=ot, c=c: E.mul(
                KT[:, ot * T + c * CH:ot * T + c * CH + CH], cx.bank[pb][:],
                cx.bcols[:, 8 + ot:9 + ot]), deps=[mt, kw1])
            cx.bank_war[pb] = e
            kv_done.append(e)
    v_done = []
    for tt in range(NTT):
        for vo in range(2):
            pb = B_MM0 + (vo % 2)
            mt = None
            for dt in range(DT):
                mt = p.mm(cx.bank[pb][:],
                          cx.lnxt[:, dt * T + tt * 128:dt * T + tt * 128 + 128],
                          wkv[:, dt * 1024 + vo * 512:dt * 1024 + vo * 512 + 512],
                          start=(dt == 0), stop=(dt == DT - 1),
                          deps=[cx.bank_war[pb]] if dt == 0 else [],
                          inc=1 if dt == DT - 1 else 0)
            e = fm_evac(cx, pb, V[:, tt * 1024 + vo * 512:tt * 1024 + vo * 512 + 512], deps=[mt])
            cx.bank_war[pb] = e
            v_done.append(e)

    exf_prev = [None, None]
    st_prev = [None, None]
    qc_war = []
    for c in range(1 if debug else NCH):
        q_done = []
        for ot in range(DT):
            pb = B_MM0 + (ot % 2)
            mt = None
            for dt in range(DT):
                mt = p.mm(cx.bank[pb][:], wq[:, dt * 1024 + ot * 128:dt * 1024 + ot * 128 + 128],
                          cx.lnxt[:, dt * T + c * CH:dt * T + c * CH + CH],
                          start=(dt == 0), stop=(dt == DT - 1),
                          deps=[cx.bank_war[pb]] + (qc_war + v_done[-2:] if dt == 0 else []),
                          inc=1 if dt == DT - 1 else 0)
            e = fm_evac(cx, pb, QC[:, ot * CH:(ot + 1) * CH], deps=[mt])
            cx.bank_war[pb] = e
            q_done.append(e)
        qc_war = []
        oc_toks = mha_core(cx, c, KT, V, QC, OC, AT, 1.0 / 32.0, True,
                           [kv_done[-1], q_done[-1]])
        if debug:
            bar = p.now()
            for nm, buf, wid in (("d_kt", KT, DT * T), ("d_v", V, NTT * 1024),
                                 ("d_qc", QC, DT * CH), ("d_oc", OC, DT * CH),
                                 ("d_at", AT, 4 * CH)):
                o = nc.dram_tensor(nm, [128, wid], F32, kind="ExternalOutput").ap()
                p.dma("pool", o, buf, deps=bar)
            for nm, buf in (("d_ncol", cx.ncol[:]), ("d_rrow", cx.rrow[:]),
                            ("d_m128", cx.m128[:]), ("d_lnxt", cx.lnxt[:])):
                o = nc.dram_tensor(nm, list(buf.shape), F32, kind="ExternalOutput").ap()
                p.dma("pool", o, buf, deps=bar)
            for dtl in range(DT):
                p.dma("sp", dst[dtl * 128:(dtl + 1) * 128, :], src[dtl * 128:(dtl + 1) * 128, :],
                      deps=bar)
            return []
        for ot in range(DT):
            pb = B_MM0 + (ot % 2)
            eb = ot % 2
            xo = cx.exf[:, 0:CH] if eb == 0 else cx.exf2
            lx = p.dma("sp", xo, src[ot * 128:(ot + 1) * 128, c * CH:(c + 1) * CH],
                       deps=[exf_prev[eb]])
            mt = None
            for dt in range(DT):
                mt = p.mm(cx.bank[pb][:], wo[:, dt * 1024 + ot * 128:dt * 1024 + ot * 128 + 128],
                          OC[:, dt * CH:(dt + 1) * CH],
                          start=(dt == 0), stop=(dt == DT - 1),
                          deps=[cx.bank_war[pb]] + (oc_toks if dt == 0 else []),
                          inc=1 if dt == DT - 1 else 0)
            xsl = cx.apl[:, eb * CH:(eb + 1) * CH]
            e_t = p.op("dve", lambda E, o=xsl, pb=pb, x=xo: E.tensor_add(o, cx.bank[pb][:], x),
                       deps=[mt, lx, st_prev[eb]])
            cx.bank_war[pb] = e_t
            exf_prev[eb] = e_t
            qc_war.append(e_t)
            st_prev[eb] = p.dma("sp", dst[ot * 128:(ot + 1) * 128, c * CH:(c + 1) * CH],
                                xsl, deps=[e_t])
        cx.apply_last = (cx.apply_last or []) + [t for t in st_prev if t]
    return [t for t in st_prev if t]


def prep_l1attn_inputs(inputs):
    i_ = np.arange(128, dtype=np.float32)
    diffT = i_[None, :] - i_[:, None]
    maskC = np.where(diffT >= 0, 0.0, -1e6).astype(np.float32)
    return {
        "wq": np.ascontiguousarray(inputs["L1_attn_w_q"]),
        "wkv": np.ascontiguousarray(inputs["L1_attn_w_kv"]),
        "wo": np.ascontiguousarray(inputs["L1_attn_w_o"]),
        "kw_col": col_layout(np.asarray(inputs["L1_attn_k_weights"]).reshape(-1)),
        "maskC": maskC,
        "identf": np.eye(128, dtype=np.float32),
    }


def build_l1attn_program(debug=False):
    nc = bass.Bass("TRN2", target_bir_lowering=False)
    es = ExitStack()
    hbm = _mk_io(nc, [("wq", [D, D], F32), ("wkv", [D, D], F32), ("wo", [D, D], F32),
                      ("kw_col", [128, 8], F32), ("maskC", [128, 128], F32),
                      ("identf", [128, 128], F32)])
    cx = Ctx(nc, es)
    emit_l1attn(cx, hbm, hbm["xT"], hbm["yT"], debug=debug)
    cx.p.emit()
    es.close()
    return nc


def run_sublayer_l1attn(x, inputs):
    return _run_sub("l1attn", build_l1attn_program, prep_l1attn_inputs, x, inputs)





def _rope_tile(cx, kt_slice, psum_bank, cosb, sinb, deps):
    p = cx.p
    rot = p.mm(cx.bank[psum_bank][:], cx.p2t[:], kt_slice, True, True,
               deps=deps + [cx.bank_war[psum_bank]])
    t1 = p.op("dve", lambda E: E.tensor_mul(cx.apl[:, 0:CH], kt_slice, cosb), deps=deps + [cx.apl_war])
    t2 = p.op("dve", lambda E: E.tensor_mul(cx.apl[:, CH:2 * CH], cx.bank[psum_bank][:], sinb),
              deps=[rot, t1])
    cx.bank_war[psum_bank] = t2
    t3 = p.op("dve", lambda E: E.tensor_add(kt_slice, cx.apl[:, 0:CH], cx.apl[:, CH:2 * CH]),
              deps=[t2])
    cx.apl_war = t3
    return t3


def emit_l2attn(cx, hbm, src, dst):
    p, nc = cx.p, cx.nc
    KT = cx.aa.b(0, DT * T)
    V = cx.aa.b(32768, NTT * 1024)
    QC = cx.aa.b(65536, DT * CH)
    OC = cx.aa.b(73728, DT * CH)
    AT = cx.aa.b(81920, 4 * CH)
    cosb = cx.aa.f(86016, CH)
    sinb = cx.aa.f(88064, CH)
    wq = cx.aw.b(0, DT * 1024)
    wk = cx.aw.b(16384, DT * 1024)
    wv = cx.aw.b(32768, DT * 1024)
    wo = cx.aw.b(16384, DT * 1024)
    cx.ncol_war = None
    cx.rrow_war = None
    cx.rstdb_war = None
    cx.apl_war = None
    cx.at_war = [None] * 4

    lastln = []
    for c in range(NCH):
        lastln = emit_ln_chunk(cx, src, c)
    ln_all = list(lastln) + list(cx.ln_last_use or [])
    cx.apl_war = lastln[-1]

    for part, dstv in ((0, wq), (1, wk), (2, wv)):
        cx.load_w(dstv, hbm["wqkvT"][:, part * 1024:(part + 1) * 1024], 1024)
    p.dma("pool", cx.p2t[:], hbm["p2t"])
    p.dma("pool", cx.identf[:], hbm["identf"])
    p.dma("pool", cx.bcols[:, 32:40], hbm["bout_col"])
    w_f = p.fence("pool")

    kv_done = []
    cs_tok = None
    for c in range(NCH):
        p.dma("pool", cosb, hbm["cosT"][:, c * CH:(c + 1) * CH],
              deps=[cs_tok, cx.apl_war])
        p.dma("pool", sinb, hbm["sinT"][:, c * CH:(c + 1) * CH])
        cs_f = p.fence("pool")
        for ot in range(DT):
            pb = B_MM0 + (ot % 2)
            mt = None
            for dt in range(DT):
                mt = p.mm(cx.bank[pb][:], wk[:, dt * 1024 + ot * 128:dt * 1024 + ot * 128 + 128],
                          cx.lnxt[:, dt * T + c * CH:dt * T + c * CH + CH],
                          start=(dt == 0), stop=(dt == DT - 1),
                          deps=[w_f, cx.bank_war[pb]] + (ln_all if c == 0 and ot == 0 else []),
                          inc=1 if dt == DT - 1 else 0)
            sl = KT[:, ot * T + c * CH:ot * T + c * CH + CH]
            e = fm_evac(cx, pb, sl, deps=[mt])
            cx.bank_war[pb] = e
            rt = _rope_tile(cx, sl, B_SC0 if ot % 2 == 0 else B_SC1, cosb, sinb, [e, cs_f])
            kv_done.append(rt)
        cs_tok = kv_done[-1]
    v_done = []
    for tt in range(NTT):
        for vo in range(2):
            pb = B_MM0 + (vo % 2)
            mt = None
            for dt in range(DT):
                mt = p.mm(cx.bank[pb][:],
                          cx.lnxt[:, dt * T + tt * 128:dt * T + tt * 128 + 128],
                          wv[:, dt * 1024 + vo * 512:dt * 1024 + vo * 512 + 512],
                          start=(dt == 0), stop=(dt == DT - 1),
                          deps=[cx.bank_war[pb]] if dt == 0 else [],
                          inc=1 if dt == DT - 1 else 0)
            e = fm_evac(cx, pb, V[:, tt * 1024 + vo * 512:tt * 1024 + vo * 512 + 512], deps=[mt])
            cx.bank_war[pb] = e
            v_done.append(e)
    cx.load_w(wo, hbm["woutT"], 1024, deps=kv_done)
    wo_f = p.fence("pool")

    exf_prev = [None, None]
    st_prev = [None, None]
    qc_war = []
    for c in range(NCH):
        p.dma("pool", cosb, hbm["cosT"][:, c * CH:(c + 1) * CH], deps=[cs_tok, cx.apl_war])
        p.dma("pool", sinb, hbm["sinT"][:, c * CH:(c + 1) * CH])
        cs_f = p.fence("pool")
        q_done = []
        for ot in range(DT):
            pb = B_MM0 + (ot % 2)
            mt = None
            for dt in range(DT):
                mt = p.mm(cx.bank[pb][:], wq[:, dt * 1024 + ot * 128:dt * 1024 + ot * 128 + 128],
                          cx.lnxt[:, dt * T + c * CH:dt * T + c * CH + CH],
                          start=(dt == 0), stop=(dt == DT - 1),
                          deps=[cx.bank_war[pb]] + (qc_war + v_done[-2:] if dt == 0 else []),
                          inc=1 if dt == DT - 1 else 0)
            sl = QC[:, ot * CH:(ot + 1) * CH]
            e = fm_evac(cx, pb, sl, deps=[mt])
            cx.bank_war[pb] = e
            rt = _rope_tile(cx, sl, B_SC0 if ot % 2 == 0 else B_SC1, cosb, sinb, [e, cs_f])
            q_done.append(rt)
        qc_war = []
        cs_tok = q_done[-1]
        oc_toks = mha_core(cx, c, KT, V, QC, OC, AT, 0.125, False,
                           [kv_done[-1], q_done[-1]])
        for ot in range(DT):
            pb = B_MM0 + (ot % 2)
            mt = None
            for dt in range(DT):
                mt = p.mm(cx.bank[pb][:], wo[:, dt * 1024 + ot * 128:dt * 1024 + ot * 128 + 128],
                          OC[:, dt * CH:(dt + 1) * CH],
                          start=(dt == 0), stop=(dt == DT - 1),
                          deps=[wo_f, cx.bank_war[pb]] + (oc_toks if dt == 0 else []),
                          inc=1 if dt == DT - 1 else 0)
            eb = ot % 2
            xo = cx.exf[:, 0:CH] if eb == 0 else cx.exf2
            lx = p.dma("sp", xo, src[ot * 128:(ot + 1) * 128, c * CH:(c + 1) * CH],
                       deps=[exf_prev[eb]])
            xsl = cx.apl[:, eb * CH:(eb + 1) * CH]
            e_t = p.op("dve", lambda E, o=xsl, pb=pb, bo=32 + ot, x=xo: E.scalar_tensor_tensor(
                o, cx.bank[pb][:], cx.bcols[:, bo:bo + 1], x, ALU.add, ALU.add),
                deps=[mt, lx, st_prev[eb], cx.apl_war])
            cx.bank_war[pb] = e_t
            exf_prev[eb] = e_t
            qc_war.append(e_t)
            st_prev[eb] = p.dma("sp", dst[ot * 128:(ot + 1) * 128, c * CH:(c + 1) * CH],
                                xsl, deps=[e_t])
            cx.apl_war = st_prev[eb]
        cx.apply_last = (cx.apply_last or []) + [t for t in st_prev if t]
    return [t for t in st_prev if t]


def prep_l2attn_inputs(inputs):
    hd = 64
    inv = 1.0 / (10000.0 ** (np.arange(0, hd, 2, dtype=np.float32) / hd))
    t = np.arange(T, dtype=np.float32)
    freqs = np.concatenate([np.outer(inv, t), np.outer(inv, t)], axis=0)
    cosT = np.cos(freqs).astype(np.float32)
    sinT = np.sin(freqs).astype(np.float32)
    cosT2 = np.ascontiguousarray(np.concatenate([cosT, cosT], axis=0))
    sinT2 = np.ascontiguousarray(np.concatenate([sinT, sinT], axis=0))
    P = np.zeros((64, 64), dtype=np.float32)
    for cp in range(32):
        P[cp, 2 * cp + 1] = -1.0
    for cp in range(32, 64):
        P[cp, 2 * (cp - 32)] = 1.0
    P2T = np.zeros((128, 128), dtype=np.float32)
    P2T[0:64, 0:64] = P.T
    P2T[64:128, 64:128] = P.T
    return {
        "wqkvT": np.ascontiguousarray(inputs["L2_attn_w_qkv"].T),
        "woutT": np.ascontiguousarray(inputs["L2_attn_w_out"].T),
        "bout_col": col_layout(inputs["L2_attn_b_out"]),
        "cosT": cosT2, "sinT": sinT2, "p2t": P2T,
        "identf": np.eye(128, dtype=np.float32),
    }


def build_l2attn_program():
    nc = bass.Bass("TRN2", target_bir_lowering=False)
    es = ExitStack()
    hbm = _mk_io(nc, [("wqkvT", [D, 3 * D], F32), ("woutT", [D, D], F32),
                      ("bout_col", [128, 8], F32), ("cosT", [128, T], F32),
                      ("sinT", [128, T], F32), ("p2t", [128, 128], F32),
                      ("identf", [128, 128], F32)])
    cx = Ctx(nc, es)
    emit_l2attn(cx, hbm, hbm["xT"], hbm["yT"])
    cx.p.emit()
    es.close()
    return nc


def run_sublayer_l2attn(x, inputs):
    return _run_sub("l2attn", build_l2attn_program, prep_l2attn_inputs, x, inputs)





def emit_l2ffn(cx, hbm, src, dst):
    p, nc = cx.p, cx.nc
    cx.g8_war = None
    cx.m8_war = None
    cx.combB_war = None
    KT = cx.aa.b(0, DT * T)
    V = cx.aa.b(32768, NTT * 1024)
    QC = cx.aa.b(65536, DT * CH)
    OC = cx.aa.b(73728, DT * CH)
    AT = cx.aa.b(81920, 4 * CH)
    sel = cx.aa.f(86016, 1024)
    acc = cx.aa.f(0, 16 * 1024)
    combT = cx.aa.f(73728, 2 * T)
    wq = cx.aw.b(0, DT * 1024)
    wk = cx.aw.b(16384, DT * 1024)
    wv = cx.aw.b(32768, DT * 1024)
    wg = cx.aw.b(0, DT * 8)
    we = [cx.aw.b(off, DT * 1024) for off in (0, 16384)]
    cx.ncol_war = None
    cx.rrow_war = None
    cx.rstdb_war = None
    cx.apl_war = None
    cx.at_war = [None] * 4

    lastln = []
    for c in range(NCH):
        lastln = emit_ln_chunk(cx, src, c)
    ln_all = list(lastln) + list(cx.ln_last_use or [])
    cx.apl_war = lastln[-1]

    for part, dstv in ((0, wq), (1, wk), (2, wv)):
        cx.load_w(dstv, hbm["wqkv"][:, part * 1024:(part + 1) * 1024], 1024)
    p.dma("pool", cx.bcols[:, 8:32], hbm["bqkv_col"])
    p.dma("pool", cx.brow[0:1, 0:1024], hbm["bv_row"])
    p.dma("pool", cx.identf[:], hbm["identf"])
    w_f = p.fence("pool")

    kv_done = []
    for c in range(NCH):
        for ot in range(DT):
            pb = B_MM0 + (ot % 2)
            mt = None
            for dt in range(DT):
                mt = p.mm(cx.bank[pb][:], wk[:, dt * 1024 + ot * 128:dt * 1024 + ot * 128 + 128],
                          cx.lnxt[:, dt * T + c * CH:dt * T + c * CH + CH],
                          start=(dt == 0), stop=(dt == DT - 1),
                          deps=[w_f, cx.bank_war[pb]] + (ln_all if c == 0 and ot == 0 else []),
                          inc=1 if dt == DT - 1 else 0)
            e = fm_evac(cx, pb, KT[:, ot * T + c * CH:ot * T + c * CH + CH],
                        bias_col=cx.bcols[:, 16 + ot:17 + ot], deps=[mt])
            cx.bank_war[pb] = e
            kv_done.append(e)
    v_done = []
    for tt in range(NTT):
        for vo in range(2):
            pb = B_MM0 + (vo % 2)
            bi = p.mm(cx.bank[pb][:], cx.ones_row_f[0:1, :], cx.brow[0:1, vo * 512:(vo + 1) * 512],
                      True, False, deps=[w_f, cx.bank_war[pb]] + cx.ones_tok, inc=0)
            mt = None
            for dt in range(DT):
                mt = p.mm(cx.bank[pb][:],
                          cx.lnxt[:, dt * T + tt * 128:dt * T + tt * 128 + 128],
                          wv[:, dt * 1024 + vo * 512:dt * 1024 + vo * 512 + 512],
                          start=False, stop=(dt == DT - 1), deps=[],
                          inc=1 if dt == DT - 1 else 0)
            e = fm_evac(cx, pb, V[:, tt * 1024 + vo * 512:tt * 1024 + vo * 512 + 512], deps=[mt])
            cx.bank_war[pb] = e
            v_done.append(e)

    exf_prev = [None, None]
    st_prev = [None, None]
    qc_war = []
    attn_stores = []
    for c in range(NCH):
        q_done = []
        for ot in range(DT):
            pb = B_MM0 + (ot % 2)
            mt = None
            for dt in range(DT):
                mt = p.mm(cx.bank[pb][:], wq[:, dt * 1024 + ot * 128:dt * 1024 + ot * 128 + 128],
                          cx.lnxt[:, dt * T + c * CH:dt * T + c * CH + CH],
                          start=(dt == 0), stop=(dt == DT - 1),
                          deps=[cx.bank_war[pb]] + (qc_war + v_done[-2:] if dt == 0 else []),
                          inc=1 if dt == DT - 1 else 0)
            e = fm_evac(cx, pb, QC[:, ot * CH:(ot + 1) * CH],
                        bias_col=cx.bcols[:, 8 + ot:9 + ot], deps=[mt])
            cx.bank_war[pb] = e
            q_done.append(e)
        qc_war = []
        oc_toks = mha_core(cx, c, KT, V, QC, OC, AT, 0.125, False,
                           [kv_done[-1], q_done[-1]])
        for ot in range(DT):
            eb = ot % 2
            xo = cx.exf[:, 0:CH] if eb == 0 else cx.exf2
            lx = p.dma("sp", xo, src[ot * 128:(ot + 1) * 128, c * CH:(c + 1) * CH],
                       deps=[exf_prev[eb]])
            xsl = cx.apl[:, (ot % 2) * CH:((ot % 2) + 1) * CH]
            e_t = p.op("dve", lambda E, o=xsl, ot=ot, x=xo: E.scalar_tensor_tensor(
                o, OC[:, ot * CH:(ot + 1) * CH], 0.5, x, ALU.mult, ALU.add),
                deps=[oc_toks[2 * ot], oc_toks[2 * ot + 1], lx, st_prev[ot % 2], cx.apl_war])
            exf_prev[eb] = e_t
            cx.apl_war = e_t
            qc_war.append(e_t)
            st_prev[ot % 2] = p.dma("sp", dst[ot * 128:(ot + 1) * 128, c * CH:(c + 1) * CH],
                                    xsl, deps=[e_t])
            attn_stores.append(st_prev[ot % 2])
        cx.apply_last = (cx.apply_last or []) + [t for t in st_prev if t]

    bar = p.now()
    p.dma("pool", cx.aw.b(0, 64)[:, 0:64], hbm["wgate"], deps=bar)
    p.dma("pool", cx.brow[0:1, 0:8], hbm["bg_row"], deps=bar)
    p.dma("pool", sel[0:8, :], hbm["sel"], deps=bar)
    g_f = p.fence("pool")
    comb_done = []
    gscr = cx.rrow
    for tt in range(NTT):
        pb = B_MM0 + (tt % 2)
        bi = p.mm(cx.bank[pb][:, 0:8], cx.ones_row_f[0:1, :], cx.brow[0:1, 0:8],
                  True, False, deps=[g_f, cx.bank_war[pb]] + cx.ones_tok, inc=0)
        mt = None
        for dt in range(DT):
            mt = p.mm(cx.bank[pb][:, 0:8],
                      cx.lnxt[:, dt * T + tt * 128:dt * T + tt * 128 + 128],
                      wg[:, dt * 8:(dt + 1) * 8],
                      start=False, stop=(dt == DT - 1), deps=[],
                      inc=1 if dt == DT - 1 else 0)
        s0 = cx.ncol[:, 0:1]
        g8 = cx.mr_b[:, 0:8]
        t1 = p.op("dve", lambda E, pb=pb: E.reduce_max(s0, cx.bank[pb][:, 0:8], axis=AX.X,
                                                       negate=True), deps=[mt, cx.g8_war])
        t2 = p.op("act", lambda E, pb=pb: E.activation(g8, cx.bank[pb][:, 0:8], AF.Exp, bias=s0),
                  deps=[t1])
        cx.bank_war[pb] = t2
        t3 = p.op("dve", lambda E: E.reduce_sum(cx.ncol[:, 1:2], g8, axis=AX.X), deps=[t2])
        t4 = p.op("dve", lambda E: E.reciprocal(cx.ncol[:, 2:3], cx.ncol[:, 1:2]), deps=[t3])
        t5 = p.op("dve", lambda E: E.tensor_single_scalar(g8, g8, cx.ncol[:, 2:3], ALU.mult),
                  deps=[t4])
        t6 = p.op("dve", lambda E: E.reduce_max(cx.ncol[:, 3:4], g8, axis=AX.X), deps=[t5])
        m8 = cx.rstd_b[:, 0:8]
        t7 = p.op("dve", lambda E: E.tensor_single_scalar(m8, g8, cx.ncol[:, 3:4], ALU.is_ge),
                  deps=[t6, cx.m8_war])
        m8b = cx.rstd_b[:, 8:16]
        t8 = p.op("dve", lambda E: E.scalar_tensor_tensor(m8b, m8, -1e30, g8, ALU.mult, ALU.add),
                  deps=[t7])
        t9 = p.op("dve", lambda E: E.reduce_max(cx.ncol[:, 4:5], m8b, axis=AX.X), deps=[t8])
        t10 = p.op("dve", lambda E: E.tensor_single_scalar(m8b, m8b, cx.ncol[:, 4:5], ALU.is_ge),
                   deps=[t9])
        t11 = p.op("dve", lambda E: E.tensor_add(m8, m8, m8b), deps=[t10])
        t12 = p.op("dve", lambda E: E.tensor_mul(m8, m8, g8), deps=[t11])
        t13 = p.op("dve", lambda E: E.tensor_scalar_mul(m8, m8, 0.5), deps=[t12])
        cx.g8_war = t13
        trb = B_SC0 if tt % 2 == 0 else B_SC1
        tr = p.op("pe", lambda E, trb=trb: E.transpose(cx.bank[trb][0:8, 0:128], m8, cx.identf[:]),
                  deps=[t13, cx.bank_war[trb], w_f])
        ev = p.op("act", lambda E, trb=trb, tt=tt: E.copy(combT[0:8, tt * 128:(tt + 1) * 128],
                                                          cx.bank[trb][0:8, 0:128]), deps=[tr])
        cx.bank_war[trb] = ev
        cx.m8_war = ev
        comb_done.append(ev)

    acc_ld = []
    for dt in range(DT):
        acc_ld.append(p.dma("sp", acc[:, dt * 2048:(dt + 1) * 2048],
                            dst[dt * 128:(dt + 1) * 128, :], deps=attn_stores if dt == 0 else ()))
    we_tok = [None, None]
    for e in range(8):
        wb = e % 2
        cx.load_w(we[wb], hbm["wexp"][e], 1024,
                  deps=[g_f] + (comb_done if e < 2 else []) + ([we_tok[wb]] if we_tok[wb] else []))
        p.dma("pool", cx.bcols[:, 40:48], hbm["bexp_col"][e], deps=[cx.combB_war])
        we_f = p.fence("pool")
        mm_last = []
        for c in range(NCH):
            sb = cx.bank[B_BC2]
            bc = p.mm(sb[:, 0:CH], sel[0:8, e * 128:(e + 1) * 128], combT[0:8, c * CH:(c + 1) * CH],
                      True, True, deps=[comb_done[-1], cx.bank_war[B_BC2]])
            cbe = p.op("act", lambda E: E.copy(cx.rstd_b[:, 0:CH], sb[:, 0:CH]),
                       deps=[bc, cx.combB_war, cx.m8_war])
            cx.bank_war[B_BC2] = cbe
            for ot in range(DT):
                pb = B_MM0 + (ot % 2)
                mt = None
                for dt in range(DT):
                    mt = p.mm(cx.bank[pb][:], we[wb][:, dt * 1024 + ot * 128:dt * 1024 + ot * 128 + 128],
                              cx.lnxt[:, dt * T + c * CH:dt * T + c * CH + CH],
                              start=(dt == 0), stop=(dt == DT - 1),
                              deps=[we_f, cx.bank_war[pb]] + ([acc_ld[-1]] if e == 0 and ot == 0 and c == 0 else []),
                              inc=1 if dt == DT - 1 else 0)
                tmp = cx.apl[:, (ot % 2) * CH:((ot % 2) + 1) * CH]
                u1 = p.op("dve", lambda E, pb=pb, ot=ot, t=tmp: E.scalar_tensor_tensor(
                    t, cx.bank[pb][:], cx.bcols[:, 40 + ot:41 + ot], cx.rstd_b[:, 0:CH],
                    ALU.add, ALU.mult), deps=[mt, cbe, cx.apl_war])
                cx.bank_war[pb] = u1
                asl = acc[:, ot * 2048 + c * CH:ot * 2048 + c * CH + CH]
                u2 = p.op("dve", lambda E, a=asl, t=tmp: E.tensor_add(a, a, t), deps=[u1])
                cx.apl_war = u2
                mm_last.append(u2)
        we_tok[wb] = mm_last[-1]
        cx.combB_war = mm_last[-1]
        cx.m8_war = mm_last[-1]
    fin = []
    bar2 = p.now()
    for dt in range(DT):
        fin.append(p.dma("sp", dst[dt * 128:(dt + 1) * 128, :], acc[:, dt * 2048:(dt + 1) * 2048],
                         deps=bar2))
    return fin


def prep_l2ffn_inputs(inputs):
    bqkv = np.asarray(inputs["L2_ffn_b_qkv"], np.float32)
    sel = np.zeros((8, 8 * 128), dtype=np.float32)
    for e in range(8):
        sel[e, e * 128:(e + 1) * 128] = 1.0
    be = np.stack([col_layout(np.asarray(inputs["L2_ffn_b_experts"])[e]) for e in range(8)])
    return {
        "wqkv": np.ascontiguousarray(inputs["L2_ffn_W_qkv"]),
        "bqkv_col": np.ascontiguousarray(
            np.concatenate([col_layout(bqkv[0:1024]), col_layout(bqkv[1024:2048]),
                            col_layout(bqkv[2048:3072])], axis=1)),
        "bv_row": np.ascontiguousarray(bqkv[2048:3072].reshape(1, 1024)),
        "wgate": np.ascontiguousarray(
            np.asarray(inputs["L2_ffn_W_gate"], np.float32).reshape(8, 128, 8).transpose(1, 0, 2).reshape(128, 64)),
        "bg_row": np.asarray(inputs["L2_ffn_b_gate"], np.float32).reshape(1, 8),
        "wexp": np.ascontiguousarray(inputs["L2_ffn_W_experts"]),
        "bexp_col": be,
        "sel": sel,
        "identf": np.eye(128, dtype=np.float32),
    }


def build_l2ffn_program():
    nc = bass.Bass("TRN2", target_bir_lowering=False)
    es = ExitStack()
    hbm = _mk_io(nc, [("wqkv", [D, 3 * D], F32), ("bqkv_col", [128, 24], F32),
                      ("bv_row", [1, 1024], F32), ("wgate", [128, 64], F32),
                      ("bg_row", [1, 8], F32), ("wexp", [8, D, D], F32),
                      ("bexp_col", [8, 128, 8], F32), ("sel", [8, 8 * 128], F32),
                      ("identf", [128, 128], F32)])
    cx = Ctx(nc, es)
    emit_l2ffn(cx, hbm, hbm["xT"], hbm["yT"])
    cx.p.emit()
    es.close()
    return nc


def run_sublayer_l2ffn(x, inputs):
    return _run_sub("l2ffn", build_l2ffn_program, prep_l2ffn_inputs, x, inputs)





def emit_top2_renorm(cx, wg, bg_row, combT, g_f, identf_tok):
    p = cx.p
    comb_done = []
    for tt in range(NTT):
        pb = B_MM0 + (tt % 2)
        p.mm(cx.bank[pb][:, 0:8], cx.ones_row_f[0:1, :], bg_row,
             True, False, deps=[g_f, cx.bank_war[pb]] + cx.ones_tok, inc=0)
        mt = None
        for dt in range(DT):
            mt = p.mm(cx.bank[pb][:, 0:8],
                      cx.lnxt[:, dt * T + tt * 128:dt * T + tt * 128 + 128],
                      wg[:, dt * 8:(dt + 1) * 8],
                      start=False, stop=(dt == DT - 1), deps=[],
                      inc=1 if dt == DT - 1 else 0)
        g8 = cx.mr_b[:, 0:8]
        m8 = cx.rstd_b[:, 0:8]
        m8b = cx.rstd_b[:, 8:16]
        t0 = p.op("dve", lambda E, pb=pb: E.tensor_copy(g8, cx.bank[pb][:, 0:8]),
                  deps=[mt, cx.g8_war])
        cx.bank_war[pb] = t0
        t1 = p.op("dve", lambda E: E.reduce_max(cx.ncol[:, 0:1], g8, axis=AX.X), deps=[t0])
        t2 = p.op("dve", lambda E: E.tensor_single_scalar(m8, g8, cx.ncol[:, 0:1], ALU.is_ge),
                  deps=[t1, cx.m8_war])
        t3 = p.op("dve", lambda E: E.scalar_tensor_tensor(m8b, m8, -1e30, g8, ALU.mult, ALU.add),
                  deps=[t2])
        t4 = p.op("dve", lambda E: E.reduce_max(cx.ncol[:, 1:2], m8b, axis=AX.X), deps=[t3])
        t5 = p.op("dve", lambda E: E.tensor_single_scalar(m8b, m8b, cx.ncol[:, 1:2], ALU.is_ge),
                  deps=[t4])
        t6 = p.op("dve", lambda E: E.tensor_sub(cx.ncol[:, 2:3], cx.ncol[:, 1:2], cx.ncol[:, 0:1]),
                  deps=[t4])
        t7 = p.op("act", lambda E: E.activation(cx.ncol[:, 3:4], cx.ncol[:, 2:3], AF.Exp),
                  deps=[t6])
        t8 = p.op("dve", lambda E: E.tensor_scalar_add(cx.ncol[:, 3:4], cx.ncol[:, 3:4], 1.0),
                  deps=[t7])
        t9 = p.op("dve", lambda E: E.reciprocal(cx.ncol[:, 2:3], cx.ncol[:, 3:4]), deps=[t8])
        t10 = p.op("dve", lambda E: E.tensor_scalar_mul(cx.ncol[:, 2:3], cx.ncol[:, 2:3], 0.5),
                   deps=[t9])
        t11 = p.op("dve", lambda E: E.tensor_scalar(cx.ncol[:, 3:4], cx.ncol[:, 2:3], -1.0, 0.5,
                                                    ALU.mult, ALU.add), deps=[t10])
        t12 = p.op("dve", lambda E: E.tensor_single_scalar(m8, m8, cx.ncol[:, 2:3], ALU.mult),
                   deps=[t10, t5])
        t13 = p.op("dve", lambda E: E.tensor_single_scalar(m8b, m8b, cx.ncol[:, 3:4], ALU.mult),
                   deps=[t11])
        t14 = p.op("dve", lambda E: E.tensor_add(m8, m8, m8b), deps=[t12, t13])
        cx.g8_war = t14
        trb = B_SC0 if tt % 2 == 0 else B_SC1
        tr = p.op("pe", lambda E, trb=trb: E.transpose(cx.bank[trb][0:8, 0:128], m8, cx.identf[:]),
                  deps=[t14, cx.bank_war[trb], identf_tok])
        ev = p.op("act", lambda E, trb=trb, tt=tt: E.copy(combT[0:8, tt * 128:(tt + 1) * 128],
                                                          cx.bank[trb][0:8, 0:128]), deps=[tr])
        cx.bank_war[trb] = ev
        cx.m8_war = ev
        comb_done.append(ev)
    return comb_done



def emit_l1ffn(cx, hbm, src, dst):
    p, nc = cx.p, cx.nc
    acc = cx.aa.f(0, 16 * 1024)
    H1 = cx.aa.b(65536, DT * CH)
    combT = cx.aa.f(73728, 2 * T)
    sel = cx.aa.f(81920, 1024)
    wgv = cx.aa.b(86016, DT * 8)
    w1b = [cx.aw.b(0, DT * 1024), cx.aw.b(16384, DT * 1024), cx.aw.b(32768, DT * 1024)]
    cx.g8_war = None
    cx.m8_war = None
    cx.combB_war = None
    cx.apl_war = None
    cx.b1_war = None
    cx.h1_war = [None] * DT

    lastln = []
    for c in range(NCH):
        lastln = emit_ln_chunk(cx, src, c)
    ln_all = list(lastln) + list(cx.ln_last_use or [])
    cx.apl_war = lastln[-1]

    p.dma("pool", wgv[:, 0:64], hbm["wg"], deps=ln_all)
    p.dma("pool", cx.brow[0:1, 0:8], hbm["bg_row"])
    p.dma("pool", sel[0:8, :], hbm["sel"])
    p.dma("pool", cx.identf[:], hbm["identf"])
    g_f = p.fence("pool")
    comb_done = emit_top2_renorm(cx, wgv, cx.brow[0:1, 0:8], combT, g_f, g_f)

    acc_ld = []
    for dt in range(DT):
        acc_ld.append(p.dma("sp", acc[:, dt * 2048:(dt + 1) * 2048],
                            src[dt * 128:(dt + 1) * 128, :],
                            deps=(cx.apply_last or []) if dt == 0 else ()))
    iters = [(e, h) for e in range(8) for h in range(2)]
    s_w1, s_w2, s_sp = 0, 1, 2
    cx.load_w(w1b[s_w1], hbm["w1T"][0][:, 0:1024], 1024, deps=[g_f])
    cx.load_w(w1b[s_w2], hbm["w2T"][0][0:1024, :], 1024)
    p.dma("pool", cx.bcols[:, 8:24], hbm["b1col"][0])
    p.dma("pool", cx.bcols[:, 24:32], hbm["b2col"][0])
    we_f = p.fence("pool")
    spare_free = g_f
    w2slot_free = None
    for it, (e, half) in enumerate(iters):
        if True:
            w1s = w1b[s_w1]
            w2s = w1b[s_w2]
            mm_last = None
            mm1_last = None
            for c in range(NCH):
                sb = cx.bank[B_BC2]
                bc = p.mm(sb[:, 0:CH], sel[0:8, e * 128:(e + 1) * 128],
                          combT[0:8, c * CH:(c + 1) * CH], True, True,
                          deps=[comb_done[-1], cx.bank_war[B_BC2]])
                cbe = p.op("act", lambda E: E.copy(cx.rstd_b[:, 0:CH], sb[:, 0:CH]),
                           deps=[bc, cx.combB_war, cx.m8_war])
                cx.bank_war[B_BC2] = cbe
                cbe_t = cbe
                h1_done = []
                for ht in range(DT):
                    pb = B_SC0 + (ht % 2)
                    mt = None
                    for dt in range(DT):
                        mt = p.mm(cx.bank[pb][:],
                                  w1s[:, dt * 1024 + ht * 128:dt * 1024 + ht * 128 + 128],
                                  cx.lnxt[:, dt * T + c * CH:dt * T + c * CH + CH],
                                  start=(dt == 0), stop=(dt == DT - 1),
                                  deps=[we_f, cx.bank_war[pb]],
                                  inc=1 if dt == DT - 1 else 0)
                    he = p.op("act", lambda E, pb=pb, ht=ht: E.activation(
                        H1[:, ht * CH:(ht + 1) * CH], cx.bank[pb][:], AF.Relu,
                        bias=cx.bcols[:, 8 + half * 8 + ht:9 + half * 8 + ht]),
                        deps=[mt, cx.h1_war[ht]])
                    cx.bank_war[pb] = he
                    h1_done.append(he)
                for ot in range(DT):
                    pb = B_MM0 + (ot % 2)
                    mt = None
                    for ht in range(DT):
                        mt = p.mm(cx.bank[pb][:],
                                  w2s[:, ht * 1024 + ot * 128:ht * 1024 + ot * 128 + 128],
                                  H1[:, ht * CH:(ht + 1) * CH],
                                  start=(ht == 0), stop=(ht == DT - 1),
                                  deps=[cx.bank_war[pb]] + ([h1_done[-1]] if ht == 0 else []),
                                  inc=1 if ht == DT - 1 else 0)
                    tmp = cx.apl[:, (ot % 2) * CH:((ot % 2) + 1) * CH]
                    if half == 0:
                        u1 = p.op("dve", lambda E, pb=pb, ot=ot, t=tmp: E.scalar_tensor_tensor(
                            t, cx.bank[pb][:], cx.bcols[:, 24 + ot:25 + ot], cx.rstd_b[:, 0:CH],
                            ALU.add, ALU.mult), deps=[mt, cbe_t, cx.apl_war])
                    else:
                        u1 = p.op("dve", lambda E, pb=pb, t=tmp: E.tensor_mul(
                            t, cx.bank[pb][:], cx.rstd_b[:, 0:CH]), deps=[mt, cbe_t, cx.apl_war])
                    cx.bank_war[pb] = u1
                    asl = acc[:, ot * 2048 + c * CH:ot * 2048 + c * CH + CH]
                    u2 = p.op("dve", lambda E, a=asl, t=tmp: E.tensor_add(a, a, t),
                              deps=[u1] + ([acc_ld[-1]] if e == 0 and half == 0 and c == 0 and ot == 0 else []))
                    cx.apl_war = u2
                    mm_last = u2
                cx.combB_war = mm_last
                cx.h1_war = [cx.bank_war[B_MM0]] * DT
                if c == NCH - 1:
                    mm1_last = h1_done[-1]
            if it + 1 < len(iters):
                ne, nh = iters[it + 1]
                cx.load_w(w1b[s_sp], hbm["w1T"][ne][:, nh * 1024:(nh + 1) * 1024], 1024,
                          deps=[spare_free])
                cx.load_w(w1b[s_w1], hbm["w2T"][ne][nh * 1024:(nh + 1) * 1024, :], 1024,
                          deps=[mm1_last])
                p.dma("pool", cx.bcols[:, 8:24], hbm["b1col"][ne], deps=[mm_last])
                p.dma("pool", cx.bcols[:, 24:32], hbm["b2col"][ne])
                we_f = p.fence("pool")
            spare_free = mm_last
            cx.b1_war = mm_last
            s_w1, s_w2, s_sp = s_sp, s_w1, s_w2
    fin = []
    bar2 = p.now()
    for dt in range(DT):
        fin.append(p.dma("sp", dst[dt * 128:(dt + 1) * 128, :], acc[:, dt * 2048:(dt + 1) * 2048],
                         deps=bar2))
    return fin


def prep_l1ffn_inputs(inputs):
    sel = np.zeros((8, 8 * 128), dtype=np.float32)
    for e in range(8):
        sel[e, e * 128:(e + 1) * 128] = 1.0
    W1 = np.asarray(inputs["L1_ffn_W1"])
    W2 = np.asarray(inputs["L1_ffn_W2"])
    b1 = np.asarray(inputs["L1_ffn_b1"])
    b2 = np.asarray(inputs["L1_ffn_b2"])
    return {
        "w1T": np.ascontiguousarray(W1.transpose(0, 2, 1)),
        "w2T": np.ascontiguousarray(W2.transpose(0, 2, 1)),
        "b1col": np.stack([col_layout(b1[e]) for e in range(8)]),
        "b2col": np.stack([col_layout(b2[e]) for e in range(8)]),
        "wg": np.ascontiguousarray(
            np.asarray(inputs["L1_ffn_Wg"], np.float32).T.reshape(8, 128, 8).transpose(1, 0, 2).reshape(128, 64)),
        "bg_row": np.asarray(inputs["L1_ffn_bg"], np.float32).reshape(1, 8),
        "sel": sel,
        "identf": np.eye(128, dtype=np.float32),
    }


def build_l1ffn_program():
    nc = bass.Bass("TRN2", target_bir_lowering=False)
    es = ExitStack()
    hbm = _mk_io(nc, [("w1T", [8, D, 2048], F32), ("w2T", [8, 2048, D], F32),
                      ("b1col", [8, 128, 16], F32), ("b2col", [8, 128, 8], F32),
                      ("wg", [128, 64], F32), ("bg_row", [1, 8], F32),
                      ("sel", [8, 8 * 128], F32), ("identf", [128, 128], F32)])
    cx = Ctx(nc, es)
    emit_l1ffn(cx, hbm, hbm["xT"], hbm["yT"])
    cx.p.emit()
    es.close()
    return nc


def run_sublayer_l1ffn(x, inputs):
    return _run_sub("l1ffn", build_l1ffn_program, prep_l1ffn_inputs, x, inputs)





def emit_l3ffn(cx, hbm, src, dst):
    p, nc = cx.p, cx.nc
    acc = cx.aa.f(0, 16 * 1024)
    G = cx.aa.b(65536, DT * CH)
    utmp = cx.aa.b(73728, 2 * CH)
    combT = cx.aa.f(75776, T)
    sel = cx.aa.f(83968, 1024)
    wgv = cx.aa.b(88064, DT * 8)
    w1s = cx.aw.b(0, DT * 1024)
    w3s = cx.aw.b(16384, DT * 1024)
    w2s = cx.aw.b(32768, DT * 1024)
    cx.g8_war = None
    cx.m8_war = None
    cx.combB_war = None
    cx.apl_war = None
    cx.g_war = [None] * DT
    cx.ut_war = [None, None]

    lastln = []
    for c in range(NCH):
        lastln = emit_ln_chunk(cx, src, c)
    ln_all = list(lastln) + list(cx.ln_last_use or [])
    cx.apl_war = lastln[-1]

    p.dma("pool", wgv[:, 0:64], hbm["wg"], deps=ln_all)
    p.dma("pool", cx.brow[0:1, 0:8], hbm["bg_row"])
    p.dma("pool", sel[0:8, :], hbm["sel"])
    p.dma("pool", cx.identf[:], hbm["identf"])
    g_f = p.fence("pool")
    comb_done = emit_top2_renorm(cx, wgv, cx.brow[0:1, 0:8], combT, g_f, g_f)

    acc_ld = []
    for dt in range(DT):
        acc_ld.append(p.dma("sp", acc[:, dt * 2048:(dt + 1) * 2048],
                            src[dt * 128:(dt + 1) * 128, :],
                            deps=(cx.apply_last or []) if dt == 0 else ()))
    w_tok = None
    for e in range(9):
        shared = (e == 8)
        for half in range(2):
            cx.load_w(w1s, (hbm["sw1"][0] if shared else hbm["w1"][e])[:, half * 1024:(half + 1) * 1024],
                      1024, deps=[g_f, w_tok])
            cx.load_w(w3s, (hbm["sw3"][0] if shared else hbm["w3"][e])[:, half * 1024:(half + 1) * 1024],
                      1024)
            cx.load_w(w2s, (hbm["sw2"][0] if shared else hbm["w2"][e])[half * 1024:(half + 1) * 1024, :],
                      1024)
            we_f = p.fence("pool")
            mm_last = None
            for c in range(NCH):
                if not shared:
                    sb = cx.bank[B_BC2]
                    bc = p.mm(sb[:, 0:CH], sel[0:8, e * 128:(e + 1) * 128],
                              combT[0:8, c * CH:(c + 1) * CH], True, True,
                              deps=[comb_done[-1], cx.bank_war[B_BC2]])
                    cbe = p.op("act", lambda E: E.copy(cx.rstd_b[:, 0:CH], sb[:, 0:CH]),
                               deps=[bc, cx.combB_war, cx.m8_war])
                    cx.bank_war[B_BC2] = cbe
                else:
                    cbe = None
                eo_done = []
                for ht in range(DT):
                    pb = B_SC0 + (ht % 2)
                    mt = None
                    for dt in range(DT):
                        mt = p.mm(cx.bank[pb][:],
                                  w1s[:, dt * 1024 + ht * 128:dt * 1024 + ht * 128 + 128],
                                  cx.lnxt[:, dt * T + c * CH:dt * T + c * CH + CH],
                                  start=(dt == 0), stop=(dt == DT - 1),
                                  deps=[we_f, cx.bank_war[pb]],
                                  inc=1 if dt == DT - 1 else 0)
                    ge = p.op("act", lambda E, pb=pb, ht=ht: E.activation(
                        G[:, ht * CH:(ht + 1) * CH], cx.bank[pb][:], AF.Silu),
                        deps=[mt, cx.g_war[ht]])
                    cx.bank_war[pb] = ge
                    mt2 = None
                    for dt in range(DT):
                        mt2 = p.mm(cx.bank[pb][:],
                                   w3s[:, dt * 1024 + ht * 128:dt * 1024 + ht * 128 + 128],
                                   cx.lnxt[:, dt * T + c * CH:dt * T + c * CH + CH],
                                   start=(dt == 0), stop=(dt == DT - 1),
                                   deps=[ge] if dt == 0 else [],
                                   inc=1 if dt == DT - 1 else 0)
                    ue = p.op("act", lambda E, pb=pb, ht=ht: E.copy(
                        utmp[:, (ht % 2) * CH:((ht % 2) + 1) * CH], cx.bank[pb][:]),
                        deps=[mt2, cx.ut_war[ht % 2]])
                    cx.bank_war[pb] = ue
                    me = p.op("dve", lambda E, ht=ht: E.tensor_mul(
                        G[:, ht * CH:(ht + 1) * CH], G[:, ht * CH:(ht + 1) * CH],
                        utmp[:, (ht % 2) * CH:((ht % 2) + 1) * CH]), deps=[ue, ge])
                    cx.ut_war[ht % 2] = me
                    eo_done.append(me)
                for ot in range(DT):
                    pb = B_MM0 + (ot % 2)
                    mt = None
                    for ht in range(DT):
                        mt = p.mm(cx.bank[pb][:],
                                  w2s[:, ht * 1024 + ot * 128:ht * 1024 + ot * 128 + 128],
                                  G[:, ht * CH:(ht + 1) * CH],
                                  start=(ht == 0), stop=(ht == DT - 1),
                                  deps=[cx.bank_war[pb]] + ([eo_done[-1]] if ht == 0 else []),
                                  inc=1 if ht == DT - 1 else 0)
                    asl = acc[:, ot * 2048 + c * CH:ot * 2048 + c * CH + CH]
                    if shared:
                        u2 = p.op("dve", lambda E, pb=pb, a=asl: E.scalar_tensor_tensor(
                            a, cx.bank[pb][:], 0.5, a, ALU.mult, ALU.add),
                            deps=[mt, cx.apl_war] + ([acc_ld[-1]] if e == 0 else []))
                        cx.bank_war[pb] = u2
                    else:
                        tmp = cx.apl[:, (ot % 2) * CH:((ot % 2) + 1) * CH]
                        u1 = p.op("dve", lambda E, pb=pb, t=tmp: E.tensor_mul(
                            t, cx.bank[pb][:], cx.rstd_b[:, 0:CH]), deps=[mt, cbe, cx.apl_war])
                        cx.bank_war[pb] = u1
                        u2 = p.op("dve", lambda E, a=asl, t=tmp: E.tensor_add(a, a, t),
                                  deps=[u1] + ([acc_ld[-1]] if e == 0 and half == 0 and c == 0 and ot == 0 else []))
                        cx.apl_war = u2
                    mm_last = u2
                cx.combB_war = mm_last
                cx.g_war = [cx.bank_war[B_MM0]] * DT
            w_tok = mm_last
    fin = []
    bar2 = p.now()
    for dt in range(DT):
        fin.append(p.dma("sp", dst[dt * 128:(dt + 1) * 128, :], acc[:, dt * 2048:(dt + 1) * 2048],
                         deps=bar2))
    return fin


def prep_l3ffn_inputs(inputs):
    sel = np.zeros((8, 8 * 128), dtype=np.float32)
    for e in range(8):
        sel[e, e * 128:(e + 1) * 128] = 1.0
    return {
        "w1": np.ascontiguousarray(inputs["L3_ffn_w1"]),
        "w3": np.ascontiguousarray(inputs["L3_ffn_w3"]),
        "w2": np.ascontiguousarray(inputs["L3_ffn_w2"]),
        "sw1": np.ascontiguousarray(inputs["L3_ffn_sw1"]),
        "sw3": np.ascontiguousarray(inputs["L3_ffn_sw3"]),
        "sw2": np.ascontiguousarray(inputs["L3_ffn_sw2"]),
        "wg": np.ascontiguousarray(
            np.asarray(inputs["L3_ffn_router_w"], np.float32).T.reshape(8, 128, 8).transpose(1, 0, 2).reshape(128, 64)),
        "bg_row": np.asarray(inputs["L3_ffn_experts_bias"], np.float32).reshape(1, 8),
        "sel": sel,
        "identf": np.eye(128, dtype=np.float32),
    }


def build_l3ffn_program():
    nc = bass.Bass("TRN2", target_bir_lowering=False)
    es = ExitStack()
    hbm = _mk_io(nc, [("w1", [8, D, 2048], F32), ("w3", [8, D, 2048], F32),
                      ("w2", [8, 2048, D], F32), ("sw1", [1, D, 2048], F32),
                      ("sw3", [1, D, 2048], F32), ("sw2", [1, 2048, D], F32),
                      ("wg", [128, 64], F32), ("bg_row", [1, 8], F32),
                      ("sel", [8, 8 * 128], F32), ("identf", [128, 128], F32)])
    cx = Ctx(nc, es)
    emit_l3ffn(cx, hbm, hbm["xT"], hbm["yT"])
    cx.p.emit()
    es.close()
    return nc


def run_sublayer_l3ffn(x, inputs):
    return _run_sub("l3ffn", build_l3ffn_program, prep_l3ffn_inputs, x, inputs)





def emit_l3attn(cx, hbm, src, dst, stop_after=None):
    p, nc = cx.p, cx.nc
    NKV = 4
    KT = cx.aa.b(0, NKV * T)
    V = cx.aa.b(16384, NTT * 512)
    QC = cx.aa.b(32768, DT * CH)
    OC = cx.aa.b(40960, DT * CH)
    AT = cx.aa.b(49152, 2 * CH)
    cosb = cx.aa.f(51200, CH)
    sinb = cx.aa.f(53248, CH)
    wq3 = cx.aa.b(55296, CH)
    hsqp = cx.aa.b(56320, CH)
    maskW = cx.aa.f(57344, 128)
    dirsrow = cx.aa.b(57856, 32)
    dirscol = cx.aa.b(57920, 8)
    wqp = cx.aw.b(0, DT * 1024)
    wkp = cx.aw.b(16384, DT * 512)
    wvp = cx.aw.b(24576, DT * 512)
    wop = cx.aw.b(32768, DT * 1024)
    SCALE = 1.0 / float(np.sqrt(128.0))
    cx.apl_war = None
    cx.hsqp_war = None
    cx.wq3_war = None

    lastln = []
    for c in range(NCH):
        lastln = emit_ln_chunk(cx, src, c)
    ln_all = list(lastln) + list(cx.ln_last_use or [])
    cx.apl_war = lastln[-1]

    cx.load_w(wqp, hbm["wq"], 1024)
    cx.load_w(wkp, hbm["wk"], 512)
    cx.load_w(wvp, hbm["wv"], 512)
    cx.load_w(wop, hbm["wo"], 1024)
    p.dma("pool", cx.p2t[:], hbm["p3t"])
    p.dma("pool", cx.identf[:], hbm["identf"])
    p.dma("pool", cx.m128[:, 0:128], hbm["maskC"])
    p.dma("pool", maskW, hbm["maskW"])
    p.dma("pool", dirsrow[0:1, 0:24], hbm["dirsrow"])
    p.dma("pool", dirscol[0:3, 0:8], hbm["dirscol"])
    p.dma("pool", cx.srow[0:1, 20:28], hbm["hs_row"])
    w_f = p.fence("pool")
    hs2 = p.op("dve", lambda E: E.tensor_scalar_mul(cx.srow[0:1, 28:36], cx.srow[0:1, 20:28],
                                                    float(np.sqrt(128.0))), deps=[w_f])

    kv_done = []
    cs_tok = None
    for c in range(NCH):
        p.dma("pool", cosb, hbm["cosT"][:, c * CH:(c + 1) * CH], deps=[cs_tok, cx.apl_war])
        p.dma("pool", sinb, hbm["sinT"][:, c * CH:(c + 1) * CH])
        cs_f = p.fence("pool")
        for kh in range(NKV):
            pb = B_MM0 + (kh % 2)
            mt = None
            for dt in range(DT):
                mt = p.mm(cx.bank[pb][:], wkp[:, dt * 512 + kh * 128:dt * 512 + kh * 128 + 128],
                          cx.lnxt[:, dt * T + c * CH:dt * T + c * CH + CH],
                          start=(dt == 0), stop=(dt == DT - 1),
                          deps=[w_f, cx.bank_war[pb]] + (ln_all if c == 0 and kh == 0 else []),
                          inc=1 if dt == DT - 1 else 0)
            sl = KT[:, kh * T + c * CH:kh * T + c * CH + CH]
            e = fm_evac(cx, pb, sl, deps=[mt])
            cx.bank_war[pb] = e
            rt = _rope_tile(cx, sl, B_SC0 if kh % 2 == 0 else B_SC1, cosb, sinb, [e, cs_f])
            kv_done.append(rt)
        cs_tok = kv_done[-1]
    v_done = []
    for tt in range(NTT):
        pb = B_MM0 + (tt % 2)
        mt = None
        for dt in range(DT):
            mt = p.mm(cx.bank[pb][:],
                      cx.lnxt[:, dt * T + tt * 128:dt * T + tt * 128 + 128],
                      wvp[:, dt * 512:(dt + 1) * 512],
                      start=(dt == 0), stop=(dt == DT - 1),
                      deps=[cx.bank_war[pb]] if dt == 0 else [],
                      inc=1 if dt == DT - 1 else 0)
        e = fm_evac(cx, pb, V[:, tt * 512:(tt + 1) * 512], deps=[mt])
        cx.bank_war[pb] = e
        v_done.append(e)

    if stop_after == "B":
        bar = p.now()
        for dtl in range(DT):
            p.dma("sp", dst[dtl * 128:(dtl + 1) * 128, :], src[dtl * 128:(dtl + 1) * 128, :],
                  deps=bar)
        return []
    exf_prev = [None, None]
    st_prev = [None, None]
    qc_war = []
    at_war = [None, None]
    ncol_war = None
    rrow_war = None
    rstdb_war = None
    for c in range(1 if stop_after in ("one", "C1", "C2") else (2 if stop_after == "two" else NCH)):
        p.dma("pool", cosb, hbm["cosT"][:, c * CH:(c + 1) * CH], deps=[cs_tok, cx.apl_war])
        p.dma("pool", sinb, hbm["sinT"][:, c * CH:(c + 1) * CH])
        cs_f = p.fence("pool")
        q_done = []
        for ot in range(DT):
            pb = B_MM0 + (ot % 2)
            mt = None
            for dt in range(DT):
                mt = p.mm(cx.bank[pb][:], wqp[:, dt * 1024 + ot * 128:dt * 1024 + ot * 128 + 128],
                          cx.lnxt[:, dt * T + c * CH:dt * T + c * CH + CH],
                          start=(dt == 0), stop=(dt == DT - 1),
                          deps=[cx.bank_war[pb]] + (qc_war + v_done[-2:] if dt == 0 else []),
                          inc=1 if dt == DT - 1 else 0)
            sl = QC[:, ot * CH:(ot + 1) * CH]
            e = fm_evac(cx, pb, sl, deps=[mt])
            cx.bank_war[pb] = e
            rt = _rope_tile(cx, sl, B_SC0 if ot % 2 == 0 else B_SC1, cosb, sinb, [e, cs_f])
            q_done.append(rt)
        qc_war = []
        cs_tok = q_done[-1]
        if stop_after == "C1":
            break
        tk_lo = max(0, 4 * c - 4)
        tk_hi = 4 * (c + 1)
        tks = list(range(tk_lo, tk_hi))
        oc_toks = []
        for h in range(8):
            kh = h // 2
            qp = p.mm(cx.bank[B_ST1][0:1, :], dirscol[0:3, h:h + 1], QC[0:3, h * CH:(h + 1) * CH],
                      True, True, deps=[q_done[-1], cx.bank_war[B_ST1], w_f])
            hq = p.op("act", lambda E, h=h: E.mul(hsqp[0:1, 0:CH], cx.bank[B_ST1][0:1, :],
                                                  cx.srow[0:1, 28 + h:29 + h]),
                      deps=[qp, hs2, cx.hsqp_war])
            cx.bank_war[B_ST1] = hq
            w3m = p.mm(cx.bank[B_ST2][0:3, :], dirsrow[0:1, 3 * h:3 * h + 3], hsqp[0:1, 0:CH],
                       True, True, deps=[hq, cx.bank_war[B_ST2]])
            w3e = p.op("act", lambda E: E.copy(wq3[0:3, 0:CH], cx.bank[B_ST2][0:3, :]),
                       deps=[w3m, cx.wq3_war])
            cx.bank_war[B_ST2] = w3e

            def score_mm(tk, pb, extra_deps):
                m1 = p.mm(cx.bank[pb][:], KT[:, kh * T + tk * 128:kh * T + tk * 128 + 128],
                          QC[:, h * CH:(h + 1) * CH], True, False,
                          deps=extra_deps + [cx.bank_war[pb]], inc=0)
                return p.mm(cx.bank[pb][:], KT[0:3, kh * T + tk * 128:kh * T + tk * 128 + 128],
                            wq3[0:3, 0:CH], False, True, deps=[w3e])

            mx_t = []
            for i, tk in enumerate(tks):
                pb = B_SC0 + (i % 2)
                mt = score_mm(tk, pb, [kv_done[-1]] if i < 2 else [])
                rd = p.op("dve", lambda E, pb=pb, i=i: E.reduce_max(
                    cx.m128[:, 128 + i:129 + i], cx.bank[pb][:], axis=AX.X), deps=[mt, ncol_war])
                cx.bank_war[pb] = rd
                mx_t.append(rd)
            mxc = p.op("dve", lambda E: E.reduce_max(cx.ncol[:, 5:6],
                                                     cx.m128[:, 128:128 + len(tks)], axis=AX.X),
                       deps=[mx_t[-1]])
            tr = p.op("pe", lambda E: E.transpose(cx.bank[B_BC2][0:1, 0:128], cx.ncol[:, 5:6],
                                                  cx.identf[:]), deps=[mxc, cx.bank_war[B_BC2]])
            gmx = p.op("dve", lambda E: E.reduce_max(cx.srow[0:1, 16:17],
                                                     cx.bank[B_BC2][0:1, 0:128], axis=AX.X), deps=[tr])
            cx.bank_war[B_BC2] = gmx
            gm2 = p.op("dve", lambda E: E.tensor_scalar_max(cx.srow[0:1, 16:17], cx.srow[0:1, 16:17],
                                                            0.0), deps=[gmx])
            ngm = p.op("dve", lambda E: E.tensor_scalar_mul(cx.srow[0:1, 17:18], cx.srow[0:1, 16:17],
                                                            -SCALE), deps=[gm2])
            bcm = p.mm(cx.bank[B_BC2][:, 0:1], cx.ones_row_f[0:1, :], cx.srow[0:1, 17:18], True, True,
                       deps=[ngm])
            nmc = p.op("dve", lambda E: E.tensor_copy(cx.ncol[:, 6:7], cx.bank[B_BC2][:, 0:1]),
                       deps=[bcm, ncol_war])
            cx.bank_war[B_BC2] = nmc
            ncol_war = nmc
            ex_last = None
            av_t = rs_t = None
            for i, tk in enumerate(tks):
                pb = B_SC0 + (i % 2)
                mt = score_mm(tk, pb, [nmc])
                d0 = tk - 4 * c
                t = mt
                if d0 > 0:
                    t = p.op("dve", lambda E, pb=pb, n=d0: E.tensor_scalar_add(
                        cx.bank[pb][:, 0:n * 128], cx.bank[pb][:, 0:n * 128], -1e6), deps=[t])
                if 0 <= d0 <= 3:
                    t = p.op("dve", lambda E, pb=pb, q0=d0 * 128: E.tensor_add(
                        cx.bank[pb][:, q0:q0 + 128], cx.bank[pb][:, q0:q0 + 128],
                        cx.m128[:, 0:128]), deps=[t])
                if 0 <= d0 + 4 <= 3:
                    t = p.op("dve", lambda E, pb=pb, q0=(d0 + 4) * 128: E.tensor_add(
                        cx.bank[pb][:, q0:q0 + 128], cx.bank[pb][:, q0:q0 + 128], maskW), deps=[t])
                if d0 + 5 <= 3:
                    t = p.op("dve", lambda E, pb=pb, q0=(d0 + 5) * 128: E.tensor_scalar_add(
                        cx.bank[pb][:, q0:q0 + 512 - q0], cx.bank[pb][:, q0:512], -1e6), deps=[t])
                sE = AT[:, (i % 2) * CH:((i % 2) + 1) * CH]
                ex = p.op("act", lambda E, pb=pb, s=sE: E.activation(
                    s, cx.bank[pb][:], AF.Exp, bias=cx.ncol[:, 6:7], scale=SCALE),
                    deps=[t, at_war[i % 2]])
                cx.bank_war[pb] = ex
                ex_last = ex
                first, last = (i == 0), (i == len(tks) - 1)
                av_t = p.mm(cx.bank[B_MM0][:], V[:, tk * 512 + kh * 128:tk * 512 + kh * 128 + 128],
                            sE, first, last, deps=[ex] + ([cx.bank_war[B_MM0]] if first else []),
                            inc=1 if last else 0)
                rs_t = p.mm(cx.bank[B_ST1][0:1, :], cx.ones_col_b[:, 0:1], sE, first, last,
                            deps=[ex] + ([cx.bank_war[B_ST1]] if first else []),
                            inc=1 if last else 0)
                at_war[i % 2] = av_t
            rc = p.op("dve", lambda E: E.reciprocal(cx.rrow[0:1, 0:CH], cx.bank[B_ST1][0:1, :]),
                      deps=[rs_t, rrow_war])
            cx.bank_war[B_ST1] = rc
            b1 = p.mm(cx.bank[B_BC1][:, 0:CH], cx.ones_row_f[0:1, :], cx.rrow[0:1, 0:CH],
                      True, True, deps=[rc, cx.bank_war[B_BC1]])
            rrow_war = b1
            rbt = p.op("act", lambda E: E.copy(cx.rstd_b[:], cx.bank[B_BC1][:, 0:CH]),
                       deps=[b1, rstdb_war])
            cx.bank_war[B_BC1] = rbt
            oe = p.op("dve", lambda E, h=h: E.tensor_mul(
                OC[:, h * CH:(h + 1) * CH], cx.bank[B_MM0][:], cx.rstd_b[:]),
                deps=[av_t, rbt])
            cx.bank_war[B_MM0] = oe
            rstdb_war = oe
            oc_toks.append(oe)
            cx.hsqp_war = oe
            cx.wq3_war = oe
        if stop_after in ("C2", "C2all"):
            continue
        for ot in range(DT):
            pb = B_MM0 + (ot % 2)
            mt = None
            for dt in range(DT):
                mt = p.mm(cx.bank[pb][:], wop[:, dt * 1024 + ot * 128:dt * 1024 + ot * 128 + 128],
                          OC[:, dt * CH:(dt + 1) * CH],
                          start=(dt == 0), stop=(dt == DT - 1),
                          deps=[cx.bank_war[pb]] + (oc_toks if dt == 0 else []),
                          inc=1 if dt == DT - 1 else 0)
            eb = ot % 2
            xo = cx.exf[:, 0:CH] if eb == 0 else cx.exf2
            lx = p.dma("sp", xo, src[ot * 128:(ot + 1) * 128, c * CH:(c + 1) * CH],
                       deps=[exf_prev[eb]])
            xsl = cx.apl[:, eb * CH:(eb + 1) * CH]
            e_t = p.op("dve", lambda E, o=xsl, pb=pb, x=xo: E.tensor_add(o, cx.bank[pb][:], x),
                       deps=[mt, lx, st_prev[eb], cx.apl_war])
            cx.bank_war[pb] = e_t
            exf_prev[eb] = e_t
            qc_war.append(e_t)
            st_prev[eb] = p.dma("sp", dst[ot * 128:(ot + 1) * 128, c * CH:(c + 1) * CH],
                                xsl, deps=[e_t])
            cx.apl_war = st_prev[eb]
        cx.apply_last = (cx.apply_last or []) + [t for t in st_prev if t]
    if stop_after in ("one", "C1", "C2", "two", "C2all"):
        bar = p.now()
        for dtl in range(DT):
            p.dma("sp", dst[dtl * 128:(dtl + 1) * 128, :], src[dtl * 128:(dtl + 1) * 128, :],
                  deps=bar)
        return []
    return [t for t in st_prev if t]


def prep_l3attn_inputs(inputs):
    import itertools
    hd = 128
    inv = 1.0 / (10000.0 ** (np.arange(0, hd, 2, dtype=np.float32) / hd))
    t = np.arange(T, dtype=np.float32)
    freqs = np.concatenate([np.outer(inv, t), np.outer(inv, t)], axis=0)
    cosT = np.cos(freqs).astype(np.float32)
    sinT = np.sin(freqs).astype(np.float32)
    P = np.zeros((128, 128), dtype=np.float32)
    for cp in range(64):
        P[cp, cp + 64] = -1.0
    for cp in range(64, 128):
        P[cp, cp - 64] = 1.0
    i_ = np.arange(128, dtype=np.float32)
    diffT = i_[None, :] - i_[:, None]
    maskC = np.where(diffT >= 0, 0.0, -1e6).astype(np.float32)
    maskW = np.where(diffT < 0, 0.0, -1e6).astype(np.float32)
    tg = np.array(list(itertools.product([-1.0, 1.0], repeat=3)), dtype=np.float32)
    tg = tg / np.linalg.norm(tg, axis=1, keepdims=True)
    dirs = tg[:8]
    return {
        "wq": np.ascontiguousarray(inputs["L3_attn_Wq"]),
        "wk": np.ascontiguousarray(inputs["L3_attn_Wk"]),
        "wv": np.ascontiguousarray(inputs["L3_attn_Wv"]),
        "wo": np.ascontiguousarray(inputs["L3_attn_Wo"]),
        "p3t": np.ascontiguousarray(P.T),
        "cosT": cosT, "sinT": sinT,
        "maskC": maskC, "maskW": maskW,
        "dirsrow": np.ascontiguousarray(dirs.reshape(1, 24)),
        "dirscol": np.ascontiguousarray(dirs.T),
        "hs_row": np.asarray(inputs["L3_attn_head_scales"], np.float32).reshape(1, 8),
        "identf": np.eye(128, dtype=np.float32),
    }


def build_l3attn_program(stop_after=None):
    nc = bass.Bass("TRN2", target_bir_lowering=False)
    es = ExitStack()
    hbm = _mk_io(nc, [("wq", [D, D], F32), ("wk", [D, 512], F32), ("wv", [D, 512], F32),
                      ("wo", [D, D], F32), ("p3t", [128, 128], F32),
                      ("cosT", [128, T], F32), ("sinT", [128, T], F32),
                      ("maskC", [128, 128], F32), ("maskW", [128, 128], F32),
                      ("dirsrow", [1, 24], F32), ("dirscol", [3, 8], F32),
                      ("hs_row", [1, 8], F32), ("identf", [128, 128], F32)])
    cx = Ctx(nc, es)
    emit_l3attn(cx, hbm, hbm["xT"], hbm["yT"], stop_after=stop_after)
    cx.p.emit()
    es.close()
    return nc


def run_sublayer_l3attn(x, inputs):
    return _run_sub("l3attn", build_l3attn_program, prep_l3attn_inputs, x, inputs)


_pre_cache_store = {}


def _pre(name, prep, inputs):
    key = (name, id(inputs))
    if key not in _pre_cache_store:
        _pre_cache_store.clear()
        _pre_cache_store[key] = prep(inputs)
    return _pre_cache_store[key]


CHAIN = [
    ("l0ffn", build_l0ffn_program, prep_l0ffn_inputs),
    ("l0attn", build_l0attn_program, prep_l0attn_inputs),
    ("l0ffn", build_l0ffn_program, prep_l0ffn_inputs),
    ("l1ffn", build_l1ffn_program, prep_l1ffn_inputs),
    ("l1attn", build_l1attn_program, prep_l1attn_inputs),
    ("l1ffn", build_l1ffn_program, prep_l1ffn_inputs),
    ("l2ffn", build_l2ffn_program, prep_l2ffn_inputs),
    ("l2attn", build_l2attn_program, prep_l2attn_inputs),
    ("l2ffn", build_l2ffn_program, prep_l2ffn_inputs),
    ("l3ffn", build_l3ffn_program, prep_l3ffn_inputs),
    ("l3attn", build_l3attn_program, prep_l3attn_inputs),
    ("l3ffn", build_l3ffn_program, prep_l3ffn_inputs),
]





_SUBLAYER_SPECS = {
    "l0ffn": ([("w0T", [D, D], F32), ("b0col", [128, 8], F32), ("alpha1", [1, 1], F32)],
              prep_l0ffn_inputs),
    "l0attn": ([("wqkv", [D, 3 * D], F32), ("wproj", [D, D], F32),
                ("bqkv_col", [128, 24], F32), ("bv_row", [1, 1024], F32),
                ("bproj_col", [128, 8], F32), ("diffT", [128, 128], F32),
                ("trilT", [128, 128], F32), ("identf", [128, 128], F32),
                ("gamma1", [1, 1], F32)], prep_l0attn_inputs),
    "l1ffn": ([("w1T", [8, D, 2048], F32), ("w2T", [8, 2048, D], F32),
               ("b1col", [8, 128, 16], F32), ("b2col", [8, 128, 8], F32),
               ("wg", [128, 64], F32), ("bg_row", [1, 8], F32),
               ("sel", [8, 8 * 128], F32), ("identf", [128, 128], F32)], prep_l1ffn_inputs),
    "l1attn": ([("wq", [D, D], F32), ("wkv", [D, D], F32), ("wo", [D, D], F32),
                ("kw_col", [128, 8], F32), ("maskC", [128, 128], F32),
                ("identf", [128, 128], F32)], prep_l1attn_inputs),
    "l2ffn": ([("wqkv", [D, 3 * D], F32), ("bqkv_col", [128, 24], F32),
               ("bv_row", [1, 1024], F32), ("wgate", [128, 64], F32),
               ("bg_row", [1, 8], F32), ("wexp", [8, D, D], F32),
               ("bexp_col", [8, 128, 8], F32), ("sel", [8, 8 * 128], F32),
               ("identf", [128, 128], F32)], prep_l2ffn_inputs),
    "l2attn": ([("wqkvT", [D, 3 * D], F32), ("woutT", [D, D], F32),
                ("bout_col", [128, 8], F32), ("cosT", [128, T], F32),
                ("sinT", [128, T], F32), ("p2t", [128, 128], F32),
                ("identf", [128, 128], F32)], prep_l2attn_inputs),
    "l3ffn": ([("w1", [8, D, 2048], F32), ("w3", [8, D, 2048], F32),
               ("w2", [8, 2048, D], F32), ("sw1", [1, D, 2048], F32),
               ("sw3", [1, D, 2048], F32), ("sw2", [1, 2048, D], F32),
               ("wg", [128, 64], F32), ("bg_row", [1, 8], F32),
               ("sel", [8, 8 * 128], F32), ("identf", [128, 128], F32)], prep_l3ffn_inputs),
    "l3attn": ([("wq", [D, D], F32), ("wk", [D, 512], F32), ("wv", [D, 512], F32),
                ("wo", [D, D], F32), ("p3t", [128, 128], F32),
                ("cosT", [128, T], F32), ("sinT", [128, T], F32),
                ("maskC", [128, 128], F32), ("maskW", [128, 128], F32),
                ("dirsrow", [1, 24], F32), ("dirscol", [3, 8], F32),
                ("hs_row", [1, 8], F32), ("identf", [128, 128], F32)], prep_l3attn_inputs),
}

_FUSED_SEQ = ["l0ffn", "l0attn", "l0ffn", "l1ffn", "l1attn", "l1ffn",
              "l2ffn", "l2attn", "l2ffn", "l3ffn", "l3attn", "l3ffn"]

_EMITTERS = {}


def _emit_sublayer(name, cx, hbm, src, dst):
    if name == "l0ffn":
        wq, wtok = emit_l0ffn_prep(cx, hbm)
        emit_l0ffn(cx, wq, src, dst, wtok)
    elif name == "l0attn":
        emit_l0attn(cx, hbm, src, dst)
    elif name == "l1ffn":
        emit_l1ffn(cx, hbm, src, dst)
    elif name == "l1attn":
        emit_l1attn(cx, hbm, src, dst)
    elif name == "l2ffn":
        emit_l2ffn(cx, hbm, src, dst)
    elif name == "l2attn":
        emit_l2attn(cx, hbm, src, dst)
    elif name == "l3ffn":
        emit_l3ffn(cx, hbm, src, dst)
    elif name == "l3attn":
        emit_l3attn(cx, hbm, src, dst)
    else:
        raise KeyError(name)


def build_fused_program():
    nc = bass.Bass("TRN2", target_bir_lowering=False)
    es = ExitStack()
    xT = nc.dram_tensor("xT", [D, T], F32, kind="ExternalInput").ap()
    yT = nc.dram_tensor("yT", [D, T], F32, kind="ExternalOutput").ap()
    xs = nc.dram_tensor("xs", [D, T], F32, kind="Internal").ap()
    hbms = {}
    for name, (specs, _prep) in _SUBLAYER_SPECS.items():
        hv = {}
        for key, shape, dt in specs:
            hv[key] = nc.dram_tensor(f"{name}_{key}", shape, dt, kind="ExternalInput").ap()
        hbms[name] = hv
    cx = Ctx(nc, es)
    n = len(_FUSED_SEQ)
    for i, name in enumerate(_FUSED_SEQ):
        src = xT if i == 0 else xs
        dst = yT if i == n - 1 else xs
        bar = cx.p.now()
        cx.sub_bar = bar
        cx.ln_last_use = bar
        _emit_sublayer(name, cx, hbms[name], src, dst)
    cx.p.emit()
    es.close()
    return nc


def kernel(**inputs):
    x = np.asarray(inputs["L0_attn_x"], dtype=np.float32)
    nc = _cached("fused", build_fused_program)
    pre = {}
    for name, (_specs, prep) in _SUBLAYER_SPECS.items():
        pv = prep(inputs)
        for k, v in pv.items():
            pre[f"{name}_{k}"] = np.ascontiguousarray(np.asarray(v, dtype=np.float32))
    in_maps = [dict(pre, xT=np.ascontiguousarray(x[c].T)) for c in range(NCORES)]
    res = run_bass_kernel_spmd(nc, in_maps, core_ids=list(range(NCORES)))
    return np.ascontiguousarray(
        np.stack([r["yT"].T for r in res.results])).astype(np.float32)


def kernel_unfused(**inputs):
    x = np.asarray(inputs["L0_attn_x"], dtype=np.float32)
    xT = [np.ascontiguousarray(x[c].T) for c in range(NCORES)]
    pres = {}
    for name, build, prep in CHAIN:
        if name not in pres:
            pres[name] = prep(inputs)
    for name, build, prep in CHAIN:
        nc = _cached(name, build)
        in_maps = [dict(pres[name], xT=xT[c]) for c in range(NCORES)]
        res = run_bass_kernel_spmd(nc, in_maps, core_ids=list(range(NCORES)))
        xT = [np.ascontiguousarray(r["yT"]) for r in res.results]
    return np.ascontiguousarray(np.stack([t.T for t in xT])).astype(np.float32)


# version 06
# speedup vs baseline: 1.1178x; 1.1178x over previous
import numpy as np
from collections import defaultdict
from contextlib import ExitStack

import concourse.bass as bass
import concourse.mybir as mybir
from concourse.bass_utils import run_bass_kernel_spmd

F32 = mybir.dt.float32
BF16 = mybir.dt.bfloat16
I8 = mybir.dt.int8
AF = mybir.ActivationFunctionType
ALU = mybir.AluOpType
AX = mybir.AxisListType

D = 1024
T = 2048
NCORES = 8
CH = 512
NCH = T // CH
DT = D // 128
NTT = T // 128
EPS = 1e-5

ENG_ATTR = {"pe": "tensor", "act": "scalar", "dve": "vector", "pool": "gpsimd", "sp": "sync"}
ENG_ORDER = ["pe", "act", "dve", "pool", "sp"]


class Prog:

    def __init__(self, nc, es):
        self.nc = nc
        self.es = es
        self.q = {e: [] for e in ENG_ORDER}
        self.cnt = defaultdict(int)
        self.mark = {e: defaultdict(int) for e in ENG_ORDER}
        self.semh = {}

    def _sem(self, name):
        if name not in self.semh:
            self.semh[name] = self.es.enter_context(self.nc.semaphore(name))
        return self.semh[name]

    def op(self, eng, fn, deps=(), sem=None, inc=1):
        dl = []
        for d in deps or ():
            if d is None:
                continue
            s, v = d
            if self.mark[eng][s] < v:
                dl.append((s, v))
                self.mark[eng][s] = v
        tok = None
        if inc:
            if sem is None:
                sem = "s_" + eng
            self.cnt[sem] += inc
            tok = (sem, self.cnt[sem])
            self._sem(sem)
        for s, _v in dl:
            self._sem(s)
        self.q[eng].append((dl, fn, (sem, inc) if inc else None))
        return tok

    def dma(self, eng, out, in_, deps=(), **kw):
        sem = "d_" + eng
        return self.op(eng, lambda E: E.dma_start(out, in_, **kw), deps, sem=sem, inc=16)

    def fence(self, eng):
        if not hasattr(self, "_fence_scr"):
            self._fence_scr = self.nc.alloc_sbuf_tensor("fence_scr", [1, 8], F32)
        tok_dma = ("d_" + eng, self.cnt["d_" + eng])
        scr = self._fence_scr
        return self.op(eng, lambda E: E.memset(scr[0:1, 0:1], 0.0), deps=[tok_dma], inc=1)

    def mm(self, out, lhsT, rhs, start, stop, deps=(), inc=1, tile_position=None):
        def fn(E, out=out, lhsT=lhsT, rhs=rhs, start=start, stop=stop, tp=tile_position):
            kw = {"skip_group_check": True}
            if tp is not None:
                kw["tile_position"] = tp
            return E.matmul(out, lhsT, rhs, start=start, stop=stop, **kw)
        return self.op("pe", fn, deps, inc=inc)

    def now(self):
        return [(s, v) for s, v in self.cnt.items()]

    def finalize(self):
        toks = [(s, v) for s, v in self.cnt.items() if s.startswith("d_")]
        self.op("sp", lambda E: None, toks, inc=0)

    def emit(self):
        self.finalize()
        with self.nc.Block() as blk:
            for e in ENG_ORDER:
                if not self.q[e]:
                    continue
                section = getattr(blk, ENG_ATTR[e])

                def body(E, e=e):
                    for dl, fn, incinfo in self.q[e]:
                        for s, v in dl:
                            E.wait_ge(self.semh[s], v)
                        ins = fn(E)
                        if incinfo is not None:
                            sem, amt = incinfo
                            ins.then_inc(self.semh[sem], amt)

                section(body)


class Arena:

    def __init__(self, nc, name, nbytes):
        self.h = nc.alloc_sbuf_tensor(name, [128, nbytes], I8)
        self.vf = self.h.bitcast(F32)
        self.vb = self.h.bitcast(BF16)
        self.nbytes = nbytes

    def f(self, off, n):
        assert off % 4 == 0 and off + 4 * n <= self.nbytes
        return self.vf[:, off // 4:off // 4 + n]

    def b(self, off, n):
        assert off % 2 == 0 and off + 2 * n <= self.nbytes
        return self.vb[:, off // 2:off // 2 + n]


B_ST1, B_ST2, B_BC1, B_BC2, B_MM0, B_MM1, B_SC0, B_SC1 = range(8)


class Ctx:

    def __init__(self, nc, es):
        self.nc = nc
        self.p = Prog(nc, es)
        p = self.p
        self.ones_row_f = alloc(nc, "ones_row_f", [1, 128], F32)
        self.ones_blk = alloc(nc, "ones_blk", [128, 64], F32)
        self.ones_row_b = alloc(nc, "ones_row_b", [1, 128], BF16)
        self.ones_col_b = alloc(nc, "ones_col_b", [128, 1], BF16)
        self.ones_col_f = alloc(nc, "ones_col_f", [128, 1], F32)
        toks = [p.op("pool", lambda E, t=t: E.memset(t[:], 1.0))
                for t in (self.ones_row_f, self.ones_row_b, self.ones_col_b, self.ones_col_f,
                          self.ones_blk)]
        self.ones_tok = toks
        self.cc_map = {}
        self.xb = alloc(nc, "c_xb", [128, DT * CH], BF16)
        self.as8 = Arena(nc, "c_as8", DT * CH * 2)
        self.sq = self.as8.b(0, DT * CH)
        self.apl = self.as8.f(0, 2 * CH)
        self.exf = alloc(nc, "c_exf", [128, CH], F32)
        self.exf2 = self.xb.bitcast(F32)[:, 0:CH]
        self.rows = alloc(nc, "c_rows", [1, 4 * CH], F32)
        self.srow = alloc(nc, "c_srow", [1, 64], F32)
        self.rrow = alloc(nc, "c_rrow", [1, CH], F32)
        self.rstd_b = alloc(nc, "c_rstdb", [128, CH], F32)
        self.mr_b = alloc(nc, "c_mrb", [128, CH], F32)
        self.ncol = alloc(nc, "c_ncol", [128, 8], F32)
        self.bcols = alloc(nc, "c_bcols", [128, 48], F32)
        self.brow = alloc(nc, "c_brow", [1, 1024], F32)
        self.m128 = alloc(nc, "c_m128", [128, 256], F32)
        self.identf = alloc(nc, "c_identf", [128, 128], F32)
        self.p2t = alloc(nc, "c_p2t", [128, 128], BF16)
        self.lnxt = alloc(nc, "c_lnxt", [128, DT * T], BF16)
        self.aw = Arena(nc, "arena_w", 49152)
        self.aa = Arena(nc, "arena_a", 90112)
        self.bank = [nc.alloc_psum_tensor(f"bank{i}", [128, CH], F32) for i in range(8)]
        self.bank_war = [None] * 8
        self.mxc_war = None
        self.ln_last_use = None
        self.bc_last_use = None
        self.apply_last = None
        self.rows_war = None

    def cc(self, val):
        if val not in self.cc_map:
            t = alloc(self.nc, f"cc_{len(self.cc_map)}", [128, 1], F32)
            tok = self.p.op("pool", lambda E, t=t, v=val: E.memset(t[:], float(v)))
            self.cc_map[val] = (t, tok)
        return self.cc_map[val]


    def bcast_row(self, row_ap, n, bank_i, deps, evac_to=None, evac_eng="act", evac_deps=()):
        p = self.p
        bk = self.bank[bank_i]
        t = p.mm(bk[:, 0:n], self.ones_row_f[0:1, :], row_ap, True, True,
                 deps=list(deps) + self.ones_tok)
        if evac_to is None:
            return bk[:, 0:n], t
        if evac_eng == "act":
            t2 = p.op("act", lambda E: E.copy(evac_to, bk[:, 0:n]), deps=[t] + list(evac_deps))
        else:
            t2 = p.op("dve", lambda E: E.tensor_copy(evac_to, bk[:, 0:n]), deps=[t] + list(evac_deps))
        return evac_to, t2

    def load_w(self, dstv, hbm_ap, ncols, deps=(), eng="pool"):
        ntiles = hbm_ap.shape[0] // 128
        i = hbm_ap.rearrange("(a p) m -> p a m", p=128)
        o = dstv.rearrange("p (a m) -> p a m", m=ncols)
        return self.p.dma(eng, o, i, deps=deps)

    def load_bias_col(self, hbm_ap, ncols, col_off, deps=(), scale=None):
        p = self.p
        dst = self.bcols[:, col_off:col_off + ncols]
        p.dma("pool", dst, hbm_ap, deps=deps)
        t = p.fence("pool")
        if scale is not None:
            t = p.op("dve", lambda E: E.tensor_scalar_mul(dst, dst, float(scale)), deps=[t])
        return t


def alloc(nc, name, shape, dtype):
    return nc.alloc_sbuf_tensor(name, shape, dtype)



def emit_ln_chunk(cx, src, c, keep_deps=()):
    p, nc = cx.p, cx.nc
    R = cx.rows
    st1, st2 = cx.bank[B_ST1], cx.bank[B_ST2]
    prev = list(cx.ln_last_use or []) + list(keep_deps) + cx.ones_tok
    srcr = src.rearrange("(a p) m -> p a m", p=128)
    p.dma("pool", cx.xb.rearrange("p (a m) -> p a m", m=CH),
          srcr[:, :, c * CH:(c + 1) * CH], deps=prev)
    xb_f = p.fence("pool")
    sq_t = [None] * DT
    mmA_t = mmB_t = None
    for dt in range(DT):
        sl = cx.sq[:, dt * CH:(dt + 1) * CH]
        xbs = cx.xb[:, dt * CH:(dt + 1) * CH]
        sq_t[dt] = p.op("dve", lambda E, o=sl, i=xbs: E.tensor_mul(o, i, i),
                        deps=[xb_f] + (prev if dt == 0 else []))
        mmA_t = p.mm(st1[0:1, :], cx.ones_col_b[:, 0:1], xbs,
                     start=(dt == 0), stop=(dt == DT - 1),
                     deps=[xb_f] + (prev if dt == 0 else []),
                     inc=1 if dt == DT - 1 else 0)
        mmB_t = p.mm(st2[0:1, :], cx.ones_col_b[:, 0:1], sl,
                     start=(dt == 0), stop=(dt == DT - 1),
                     deps=[sq_t[dt]], inc=1 if dt == DT - 1 else 0)
    rows_war = list(cx.rows_war or [])
    m_t = p.op("dve", lambda E: E.tensor_scalar_mul(R[0:1, 0:CH], st1[0:1, :], 1.0 / D),
               deps=[mmA_t] + rows_war)
    msq_t = p.op("dve", lambda E: E.tensor_mul(R[0:1, CH:2 * CH], R[0:1, 0:CH], R[0:1, 0:CH]),
                 deps=[m_t])
    v1_t = p.op("dve", lambda E: E.tensor_scalar_mul(R[0:1, 2 * CH:3 * CH], st2[0:1, :], 1.0 / D),
                deps=[mmB_t])
    var_t = p.op("dve", lambda E: E.tensor_sub(R[0:1, 2 * CH:3 * CH], R[0:1, 2 * CH:3 * CH],
                                               R[0:1, CH:2 * CH]),
                 deps=[msq_t, v1_t])
    epsc, epst = cx.cc(EPS)
    ln_t = p.op("act", lambda E: E.activation(R[0:1, CH:2 * CH], R[0:1, 2 * CH:3 * CH], AF.Ln,
                                              bias=epsc[0:1, 0:1]),
                deps=[var_t, epst] + rows_war)
    rstd_t = p.op("act", lambda E: E.activation(R[0:1, 3 * CH:4 * CH], R[0:1, CH:2 * CH], AF.Exp,
                                                scale=-0.5),
                  deps=[ln_t])
    mr_t = p.op("dve", lambda E: E.tensor_mul(R[0:1, 0:CH], R[0:1, 0:CH], R[0:1, 3 * CH:4 * CH]),
                deps=[rstd_t])
    bprev = list(cx.bc_last_use or [])
    apply_war = list(cx.apply_last or [])
    _, e1 = cx.bcast_row(R[0:1, 3 * CH:4 * CH], CH, B_BC1, [rstd_t] + bprev,
                         evac_to=cx.rstd_b[:], evac_deps=apply_war)
    _, e2 = cx.bcast_row(R[0:1, 0:CH], CH, B_BC2, [mr_t] + bprev,
                         evac_to=cx.mr_b[:], evac_deps=apply_war)
    cx.bc_last_use = [e1, e2]
    cx.rows_war = [("s_pe", p.cnt["s_pe"])]
    out_t = []
    ap_t = [None, None]
    for dt in range(DT):
        asl = cx.apl[:, (dt % 2) * CH:((dt % 2) + 1) * CH]
        xbs = cx.xb[:, dt * CH:(dt + 1) * CH]
        t1 = p.op("dve", lambda E, o=asl, i=xbs: E.tensor_mul(o, i, cx.rstd_b[:]),
                  deps=[xb_f, e1, ap_t[dt % 2]])
        osl = cx.lnxt[:, dt * T + c * CH:dt * T + c * CH + CH]
        t2 = p.op("dve", lambda E, o=osl, i=asl: E.tensor_sub(o, i, cx.mr_b[:]),
                  deps=[t1, e2])
        ap_t[dt % 2] = t2
        out_t.append(t2)
    cx.apply_last = out_t
    cx.ln_last_use = out_t + [mmA_t, mmB_t, m_t, v1_t]
    return out_t


def fm_evac(cx, bank_i, dst_ap, bias_col=None, deps=(), scale=None):
    p = cx.p
    bk = cx.bank[bank_i]
    n = dst_ap.shape[-1]
    if bias_col is not None:
        assert scale is None
        fn = lambda E: E.add(dst_ap, bk[:, 0:n], bias_col)
    elif scale is not None:
        fn = lambda E: E.activation(dst_ap, bk[:, 0:n], AF.Copy, scale=float(scale))
    else:
        fn = lambda E: E.copy(dst_ap, bk[:, 0:n])
    return p.op("act", fn, deps=deps)



def emit_l0ffn_prep(cx, hbm):
    p, nc = cx.p, cx.nc
    bar = list(getattr(cx, "sub_bar", []) or [])
    wq = cx.aw.b(0, DT * 1024)
    wstage = cx.aw.f(16384, DT * 1024)
    red = cx.ncol
    R = cx.srow
    ts3 = cx.aa.f(0, 3 * 1024)
    ps = cx.bank[B_ST1]

    cx.load_w(wstage, hbm["w0T"], 1024, deps=bar)
    p.dma("pool", cx.bcols[:, 0:8], hbm["b0col"])
    p.dma("pool", R[0:1, 0:1], hbm["alpha1"])
    ld_f = p.fence("pool")
    rt = []
    for i in range(DT):
        rt.append(p.op("dve", lambda E, i=i: E.tensor_reduce(
            red[:, i:i + 1], wstage[:, i * 1024:(i + 1) * 1024], AX.X, ALU.add,
            apply_absolute_value=True), deps=[ld_f]))
    sm = p.mm(ps[0:1, 8:16], cx.ones_col_f[:, 0:1], red[:, 0:8], True, True,
              deps=rt + cx.ones_tok)
    tot = p.op("dve", lambda E: E.reduce_sum(R[0:1, 1:2], ps[0:1, 8:16], axis=AX.X), deps=[sm])
    d1 = p.op("dve", lambda E: E.tensor_scalar_mul(R[0:1, 2:3], R[0:1, 1:2], 0.7 / (D * 1024.0)), deps=[tot])
    d2 = p.op("dve", lambda E: E.tensor_scalar_mul(R[0:1, 3:4], R[0:1, 2:3], -1.0), deps=[d1])
    d3 = p.op("dve", lambda E: E.tensor_scalar_mul(R[0:1, 4:5], R[0:1, 0:1], 0.5), deps=[ld_f])
    bc = p.mm(ps[:, 0:3], cx.ones_row_f[0:1, :], R[0:1, 2:5], True, True,
              deps=[d1, d2, d3, tot] + cx.ones_tok)
    be = p.op("dve", lambda E: E.tensor_copy(red[:, 1:4], ps[:, 0:3]), deps=[bc])
    toks = []
    tprev = None
    for i in range(DT):
        w_sl = wstage[:, i * 1024:(i + 1) * 1024]
        a_t = p.op("dve", lambda E, w=w_sl: E.tensor_single_scalar(ts3[:, 0:1024], w, red[:, 1:2], ALU.is_gt),
                   deps=[be, tprev])
        b_t = p.op("dve", lambda E, w=w_sl: E.tensor_single_scalar(ts3[:, 1024:2048], w, red[:, 2:3], ALU.is_lt),
                   deps=[be])
        c_t = p.op("dve", lambda E: E.tensor_sub(ts3[:, 2048:3072], ts3[:, 0:1024], ts3[:, 1024:2048]),
                   deps=[a_t, b_t])
        o_t = p.op("dve", lambda E, i=i: E.tensor_single_scalar(
            wq[:, i * 1024:(i + 1) * 1024], ts3[:, 2048:3072], red[:, 3:4], ALU.mult), deps=[c_t])
        tprev = o_t
        toks.append(o_t)
    bh = p.op("dve", lambda E: E.tensor_scalar_mul(cx.bcols[:, 0:8], cx.bcols[:, 0:8], 0.5), deps=[ld_f])
    return wq, toks + [bh, d3]


def emit_l0ffn(cx, wq, src, dst, wtok):
    p = cx.p
    ev_prev = [None, None]
    st_prev = [None, None]
    exf_prev = [None, None]
    extra = list(wtok)
    for c in range(NCH):
        ln_t = emit_ln_chunk(cx, src, c, keep_deps=extra)
        extra = []
        for ot in range(DT):
            pb = B_MM0 + (ot % 2)
            eb = ot % 2
            xo = cx.exf[:, 0:CH]
            lx = p.dma("sp", xo, src[ot * 128:(ot + 1) * 128, c * CH:(c + 1) * CH],
                       deps=[exf_prev[0], exf_prev[1]])
            mt = None
            for dt in range(DT):
                mt = p.mm(cx.bank[pb][:],
                          wq[:, dt * 1024 + ot * 128:dt * 1024 + ot * 128 + 128],
                          cx.lnxt[:, dt * T + c * CH:dt * T + c * CH + CH],
                          start=(dt == 0), stop=(dt == DT - 1),
                          deps=([ln_t[dt]] if dt > 0 else [ln_t[0], ev_prev[ot % 2]]),
                          inc=1 if dt == DT - 1 else 0)
            xsl = cx.apl[:, eb * CH:(eb + 1) * CH]
            e_t = p.op("dve", lambda E, o=xsl, pm=cx.bank[pb], bo=ot, x=xo: E.scalar_tensor_tensor(
                o, pm[:], cx.bcols[:, bo:bo + 1], x, ALU.add, ALU.add),
                deps=[mt, lx, st_prev[eb], cx.apply_last[-1] if cx.apply_last else None])
            ev_prev[ot % 2] = e_t
            exf_prev[eb] = e_t
            extra.append(e_t)
            st_prev[eb] = p.dma("sp", dst[ot * 128:(ot + 1) * 128, c * CH:(c + 1) * CH],
                                xsl, deps=[e_t])
        cx.apply_last = (cx.apply_last or []) + [t for t in st_prev if t]
    return [t for t in st_prev if t]



def emit_l0attn(cx, hbm, src, dst, stop_after=None):
    p, nc = cx.p, cx.nc
    KT = cx.aa.b(0, DT * T)
    V = cx.aa.b(32768, NTT * 1024)
    AT = cx.aa.b(65536, NTT * CH)
    QC = cx.aa.b(81920, DT * CH)
    OC = QC
    wqp = cx.aw.b(0, DT * 1024)
    wkp = cx.aw.b(16384, DT * 1024)
    wvp = cx.aw.b(32768, DT * 1024)
    wproj = cx.aw.b(16384, DT * 1024)

    lastln = []
    for c in range(NCH):
        lastln = emit_ln_chunk(cx, src, c)
    ln_all = list(lastln) + list(cx.ln_last_use or [])

    if stop_after == "A":
        for dtl in range(DT):
            p.dma("sp", dst[dtl * 128:(dtl + 1) * 128, :], src[dtl * 128:(dtl + 1) * 128, :],
                  deps=ln_all)
        return []
    for part, dstv in ((0, wqp), (1, wkp), (2, wvp)):
        cx.load_w(dstv, hbm["wqkv"][:, part * 1024:(part + 1) * 1024], 1024)
    p.dma("pool", cx.bcols[:, 8:32], hbm["bqkv_col"])
    p.dma("pool", cx.brow[0:1, 0:1024], hbm["bv_row"])
    p.dma("pool", cx.m128[:, 0:128], hbm["diffT"])
    p.dma("pool", cx.m128[:, 128:256], hbm["trilT"])
    p.dma("pool", cx.identf[:], hbm["identf"])
    p.dma("pool", cx.srow[0:1, 8:9], hbm["gamma1"])
    w_f = p.fence("pool")
    lg = p.op("act", lambda E: E.activation(cx.srow[0:1, 9:10], cx.srow[0:1, 8:9], AF.Ln), deps=[w_f])
    bcg = p.mm(cx.bank[B_SC0][:, 0:1], cx.ones_row_f[0:1, :], cx.srow[0:1, 9:10], True, True,
               deps=[lg] + cx.ones_tok)
    lgc = p.op("dve", lambda E: E.tensor_copy(cx.ncol[:, 4:5], cx.bank[B_SC0][:, 0:1]), deps=[bcg])
    mk1 = p.op("dve", lambda E: E.tensor_single_scalar(cx.m128[:, 0:128], cx.m128[:, 0:128],
                                                       cx.ncol[:, 4:5], ALU.mult), deps=[lgc, w_f])
    mk2 = p.op("act", lambda E: E.activation(cx.m128[:, 0:128], cx.m128[:, 0:128], AF.Exp), deps=[mk1])
    mask_t = p.op("dve", lambda E: E.tensor_mul(cx.m128[:, 0:128], cx.m128[:, 0:128],
                                                cx.m128[:, 128:256]), deps=[mk2])

    kv_done = []
    for c in range(NCH):
        for ot in range(DT):
            pb = B_MM0 + (ot % 2)
            war = [t for t in kv_done[-2:]]
            mt = None
            for dt in range(DT):
                mt = p.mm(cx.bank[pb][:], wkp[:, dt * 1024 + ot * 128:dt * 1024 + ot * 128 + 128],
                          cx.lnxt[:, dt * T + c * CH:dt * T + c * CH + CH],
                          start=(dt == 0), stop=(dt == DT - 1),
                          deps=[w_f] + (ln_all + war if dt == 0 else []),
                          inc=1 if dt == DT - 1 else 0)
            kv_done.append(fm_evac(cx, pb, KT[:, ot * T + c * CH:ot * T + c * CH + CH],
                                   bias_col=cx.bcols[:, 16 + ot:17 + ot], deps=[mt]))
    v_done = []
    for tt in range(NTT):
        for vo in range(2):
            pb = B_MM0 + (vo % 2)
            war = [v_done[-2] if len(v_done) >= 2 else (kv_done[-2 + vo] if len(kv_done) >= 2 else None)]
            bi = p.mm(cx.bank[pb][:], cx.ones_row_f[0:1, :], cx.brow[0:1, vo * 512:(vo + 1) * 512],
                      True, False, deps=[w_f] + war + cx.ones_tok, inc=0)
            mt = None
            for dt in range(DT):
                mt = p.mm(cx.bank[pb][:],
                          cx.lnxt[:, dt * T + tt * 128:dt * T + tt * 128 + 128],
                          wvp[:, dt * 1024 + vo * 512:dt * 1024 + vo * 512 + 512],
                          start=False, stop=(dt == DT - 1), deps=[],
                          inc=1 if dt == DT - 1 else 0)
            v_done.append(fm_evac(cx, pb, V[:, tt * 1024 + vo * 512:tt * 1024 + vo * 512 + 512],
                                  deps=[mt]))
    wp_war = kv_done
    cx.load_w(wproj, hbm["wproj"], 1024, deps=wp_war)
    p.dma("pool", cx.bcols[:, 32:40], hbm["bproj_col"])
    wp_f = p.fence("pool")

    if stop_after == "B":
        for dtl in range(DT):
            p.dma("sp", dst[dtl * 128:(dtl + 1) * 128, :], src[dtl * 128:(dtl + 1) * 128, :],
                  deps=v_done + kv_done + [wp_f])
        return []
    SCALE = 1.0 / 32.0
    qc_war = []
    at_war = []
    exf_prev = [None, None]
    st_prev = [None, None]
    rb_war = []
    av_evac_war = []
    prev_ex = []
    for c in range(1 if stop_after == "one" else NCH):
        q_done = []
        for ot in range(DT):
            pb = B_MM0 + (ot % 2)
            war = qc_war + [q_done[-2] if len(q_done) >= 2 else None]
            mt = None
            for dt in range(DT):
                mt = p.mm(cx.bank[pb][:], wqp[:, dt * 1024 + ot * 128:dt * 1024 + ot * 128 + 128],
                          cx.lnxt[:, dt * T + c * CH:dt * T + c * CH + CH],
                          start=(dt == 0), stop=(dt == DT - 1),
                          deps=(war + v_done[-2:] if dt == 0 else []),
                          inc=1 if dt == DT - 1 else 0)
            q_done.append(fm_evac(cx, pb, QC[:, ot * CH:(ot + 1) * CH],
                                  bias_col=cx.bcols[:, 8 + ot:9 + ot], deps=[mt]))
        qc_war = []
        if stop_after == "C1":
            break
        mx_t = []
        for tk in range(NTT):
            pb = B_SC0 + (tk % 2)
            mt = None
            for ft in range(DT):
                mt = p.mm(cx.bank[pb][:], KT[:, ft * T + tk * 128:ft * T + tk * 128 + 128],
                          QC[:, ft * CH:(ft + 1) * CH],
                          start=(ft == 0), stop=(ft == DT - 1),
                          deps=([kv_done[-1], q_done[-1], mask_t] if tk < 2 and ft == 0 else
                                ([mx_t[tk - 2]] if ft == 0 else [])),
                          inc=1 if ft == DT - 1 else 0)
            mx_t.append(p.op("dve", lambda E, tk=tk, pb=pb: E.reduce_max(
                cx.m128[:, 128 + tk:129 + tk], cx.bank[pb][:], axis=AX.X),
                deps=[mt, mask_t] if tk == 0 else [mt]))
        mxc = p.op("dve", lambda E: E.reduce_max(cx.ncol[:, 6:7], cx.m128[:, 128:144], axis=AX.X),
                   deps=mx_t)
        tr = p.op("pe", lambda E: E.transpose(cx.bank[B_BC2][0:1, 0:128], cx.ncol[:, 6:7], cx.identf[:]),
                  deps=[mxc] + rb_war)
        gmx = p.op("dve", lambda E: E.reduce_max(cx.srow[0:1, 10:11], cx.bank[B_BC2][0:1, 0:128], axis=AX.X),
                   deps=[tr])
        gm2 = p.op("dve", lambda E: E.tensor_scalar_max(cx.srow[0:1, 10:11], cx.srow[0:1, 10:11], 0.0),
                   deps=[gmx])
        ngm = p.op("dve", lambda E: E.tensor_scalar_mul(cx.srow[0:1, 11:12], cx.srow[0:1, 10:11],
                                                        -SCALE), deps=[gm2])
        bcm = p.mm(cx.bank[B_BC2][:, 0:1], cx.ones_row_f[0:1, :], cx.srow[0:1, 11:12], True, True,
                   deps=[ngm])
        nmc = p.op("dve", lambda E: E.tensor_copy(cx.ncol[:, 7:8], cx.bank[B_BC2][:, 0:1]),
                   deps=[bcm] + prev_ex)
        if stop_after == "C2":
            break
        ex_t = []
        for tk in range(NTT):
            pb = B_SC0 + (tk % 2)
            mt = None
            for ft in range(DT):
                mt = p.mm(cx.bank[pb][:], KT[:, ft * T + tk * 128:ft * T + tk * 128 + 128],
                          QC[:, ft * CH:(ft + 1) * CH],
                          start=(ft == 0), stop=(ft == DT - 1),
                          deps=([nmc] + at_war if ft == 0 and tk < 2 else
                                ([ex_t[tk - 2]] if ft == 0 else [])),
                          inc=1 if ft == DT - 1 else 0)
            dtok = mt
            if c * 4 <= tk < (c + 1) * 4:
                qb = (tk - c * 4) * 128
                dtok = p.op("dve", lambda E, pb=pb, qb=qb: E.tensor_mul(
                    cx.bank[pb][:, qb:qb + 128], cx.bank[pb][:, qb:qb + 128], cx.m128[:, 0:128]),
                    deps=[mt])
            ex_t.append(p.op("act", lambda E, pb=pb, tk=tk: E.activation(
                AT[:, tk * CH:(tk + 1) * CH], cx.bank[pb][:], AF.Exp,
                bias=cx.ncol[:, 7:8], scale=SCALE), deps=[dtok]))
        at_war = []
        prev_ex = [ex_t[-1]]
        if stop_after == "C3":
            break
        rs_t = None
        for tk in range(NTT):
            rs_t = p.mm(cx.bank[B_ST1][0:1, :], cx.ones_col_b[:, 0:1], AT[:, tk * CH:(tk + 1) * CH],
                        start=(tk == 0), stop=(tk == NTT - 1),
                        deps=[ex_t[tk]] + (rb_war if tk == 0 else []),
                        inc=1 if tk == NTT - 1 else 0)
        rc = p.op("dve", lambda E: E.reciprocal(cx.rrow[0:1, 0:CH], cx.bank[B_ST1][0:1, :]),
                  deps=[rs_t] + rb_war)
        _, rbt = cx.bcast_row(cx.rrow[0:1, 0:CH], CH, B_BC1, [rc], evac_to=cx.rstd_b[:],
                              evac_deps=av_evac_war)
        rb_war = [rbt]
        av_done = []
        av_evac_war = []
        for ot in range(DT):
            pb = B_MM0 + (ot % 2)
            war = [av_done[-2] if len(av_done) >= 2 else q_done[-1]]
            mt = None
            for tk in range(NTT):
                mt = p.mm(cx.bank[pb][:], V[:, tk * 1024 + ot * 128:tk * 1024 + ot * 128 + 128],
                          AT[:, tk * CH:(tk + 1) * CH],
                          start=(tk == 0), stop=(tk == NTT - 1),
                          deps=(war + [ex_t[-1]] if tk == 0 else []),
                          inc=1 if tk == NTT - 1 else 0)
            e = p.op("dve", lambda E, pb=pb, ot=ot: E.tensor_mul(
                OC[:, ot * CH:(ot + 1) * CH], cx.bank[pb][:], cx.rstd_b[:]),
                deps=[mt, rbt])
            av_done.append(e)
            at_war.append(e)
            av_evac_war.append(e)
        if stop_after == "C4":
            continue
        proj_ev = [None, None]
        for ot in range(DT):
            pb = B_MM0 + (ot % 2)
            eb = ot % 2
            xo = cx.exf[:, 0:CH] if eb == 0 else cx.exf2
            lx = p.dma("sp", xo, src[ot * 128:(ot + 1) * 128, c * CH:(c + 1) * CH],
                       deps=[exf_prev[eb]])
            mt = None
            for dt in range(DT):
                mt = p.mm(cx.bank[pb][:], wproj[:, dt * 1024 + ot * 128:dt * 1024 + ot * 128 + 128],
                          OC[:, dt * CH:(dt + 1) * CH],
                          start=(dt == 0), stop=(dt == DT - 1),
                          deps=([wp_f, proj_ev[ot % 2]] + av_done if dt == 0 else []),
                          inc=1 if dt == DT - 1 else 0)
            xsl = cx.apl[:, eb * CH:(eb + 1) * CH]
            e_t = p.op("dve", lambda E, o=xsl, pb=pb, bo=32 + ot, x=xo: E.scalar_tensor_tensor(
                o, cx.bank[pb][:], cx.bcols[:, bo:bo + 1], x, ALU.add, ALU.add),
                deps=[mt, lx, st_prev[eb]])
            exf_prev[eb] = e_t
            proj_ev[ot % 2] = e_t
            qc_war.append(e_t)
            st_prev[eb] = p.dma("sp", dst[ot * 128:(ot + 1) * 128, c * CH:(c + 1) * CH],
                                xsl, deps=[e_t])
        cx.apply_last = (cx.apply_last or []) + [t for t in st_prev if t]
    if stop_after in ("one",):
        bar = p.now()
        for dtl in range(DT):
            p.dma("sp", dst[dtl * 128:(dtl + 1) * 128, :], src[dtl * 128:(dtl + 1) * 128, :],
                  deps=bar)
        return []
    if stop_after and stop_after.startswith("C"):
        bar = p.now()
        for dtl in range(DT):
            p.dma("sp", dst[dtl * 128:(dtl + 1) * 128, :], src[dtl * 128:(dtl + 1) * 128, :],
                  deps=bar)
        return []
    return [t for t in st_prev if t]



def col_layout(v):
    v = np.asarray(v).reshape(-1)
    n = v.shape[0]
    return np.ascontiguousarray(v.reshape(n // 128, 128).T)


def prep_l0ffn_inputs(inputs):
    return {
        "w0T": np.ascontiguousarray(inputs["L0_ffn_weight"].T),
        "b0col": col_layout(inputs["L0_ffn_bias"]),
        "alpha1": np.asarray(inputs["L0_ffn_alpha"], np.float32).reshape(1, 1),
    }


def prep_l0attn_inputs(inputs):
    i_ = np.arange(128, dtype=np.float32)
    diffT = i_[None, :] - i_[:, None]
    trilT = (diffT >= 0).astype(np.float32)
    bqkv = np.asarray(inputs["L0_attn_bqkv"], np.float32)
    return {
        "wqkv": np.ascontiguousarray(inputs["L0_attn_Wqkv"]),
        "wproj": np.ascontiguousarray(inputs["L0_attn_Wproj"]),
        "bqkv_col": np.ascontiguousarray(
            np.concatenate([col_layout(bqkv[0:1024]), col_layout(bqkv[1024:2048]),
                            col_layout(bqkv[2048:3072])], axis=1)),
        "bv_row": np.ascontiguousarray(bqkv[2048:3072].reshape(1, 1024)),
        "bproj_col": col_layout(inputs["L0_attn_bproj"]),
        "diffT": np.ascontiguousarray(diffT),
        "trilT": np.ascontiguousarray(trilT),
        "identf": np.eye(128, dtype=np.float32),
        "gamma1": np.asarray(inputs["L0_attn_gamma"], np.float32).reshape(1, 1),
    }



def _mk_io(nc, names_shapes):
    hbm = {"xT": nc.dram_tensor("xT", [D, T], F32, kind="ExternalInput").ap(),
           "yT": nc.dram_tensor("yT", [D, T], F32, kind="ExternalOutput").ap()}
    for name, shape, dt in names_shapes:
        hbm[name] = nc.dram_tensor(name, shape, dt, kind="ExternalInput").ap()
    return hbm


def build_l0ffn_program():
    nc = bass.Bass("TRN2", target_bir_lowering=False)
    es = ExitStack()
    hbm = _mk_io(nc, [("w0T", [D, D], F32), ("b0col", [128, 8], F32), ("alpha1", [1, 1], F32)])
    cx = Ctx(nc, es)
    wq, wtok = emit_l0ffn_prep(cx, hbm)
    emit_l0ffn(cx, wq, hbm["xT"], hbm["yT"], wtok)
    cx.p.emit()
    es.close()
    return nc


def build_l0attn_program(stop_after=None):
    nc = bass.Bass("TRN2", target_bir_lowering=False)
    es = ExitStack()
    hbm = _mk_io(nc, [("wqkv", [D, 3 * D], F32), ("wproj", [D, D], F32),
                      ("bqkv_col", [128, 24], F32), ("bv_row", [1, 1024], F32),
                      ("bproj_col", [128, 8], F32), ("diffT", [128, 128], F32),
                      ("trilT", [128, 128], F32), ("identf", [128, 128], F32),
                      ("gamma1", [1, 1], F32)])
    cx = Ctx(nc, es)
    emit_l0attn(cx, hbm, hbm["xT"], hbm["yT"], stop_after=stop_after)
    cx.p.emit()
    es.close()
    return nc


_prog_cache = {}


def _cached(name, builder):
    if name not in _prog_cache:
        _prog_cache[name] = builder()
    return _prog_cache[name]


def _run_sub(progname, builder, prep, x, inputs):
    nc = _cached(progname, builder)
    pre = prep(inputs)
    in_maps = [dict(pre, xT=np.ascontiguousarray(x[c].T)) for c in range(NCORES)]
    res = run_bass_kernel_spmd(nc, in_maps, core_ids=list(range(NCORES)))
    return np.stack([r["yT"].T for r in res.results])


def run_sublayer_l0ffn(x, inputs):
    return _run_sub("l0ffn", build_l0ffn_program, prep_l0ffn_inputs, x, inputs)


def run_sublayer_l0attn(x, inputs):
    return _run_sub("l0attn", build_l0attn_program, prep_l0attn_inputs, x, inputs)





def mha_core(cx, c, KT, V, QC, OC, AT, scale, causal, start_deps):
    p = cx.p
    bw = cx.bank_war
    oc_toks = []
    tk_hi = min(NTT, 4 * (c + 1)) if causal else NTT
    ms_t = p.op("dve", lambda E: E.memset(cx.m128[:, 128:160], 0.0), deps=[cx.ncol_war, cx.mxc_war])

    def lhsE(hp, tk):
        return KT[0:64, hp * T + tk * 128:hp * T + tk * 128 + 128]

    def lhsO(hp, tk):
        return KT[64:128, hp * T + tk * 128:hp * T + tk * 128 + 128]

    def rhsE(hp):
        return QC[0:64, hp * CH:(hp + 1) * CH]

    def rhsO(hp):
        return QC[64:128, hp * CH:(hp + 1) * CH]

    def pass1_step(hp, tk, extra):
        mE = p.mm(cx.bank[B_ST1][:], lhsE(hp, tk), rhsE(hp), True, True,
                  deps=extra + [bw[B_ST1]])
        rdE = p.op("dve", lambda E, tk=tk: E.reduce_max(
            cx.m128[:, 128 + tk:129 + tk], cx.bank[B_ST1][:], axis=AX.X), deps=[mE, ms_t])
        bw[B_ST1] = rdE
        mO = p.mm(cx.bank[B_ST2][:], lhsO(hp, tk), rhsO(hp), True, True,
                  deps=extra + [bw[B_ST2]], tile_position=(64, 0))
        rdO = p.op("dve", lambda E, tk=tk: E.reduce_max(
            cx.m128[:, 144 + tk:145 + tk], cx.bank[B_ST2][:], axis=AX.X), deps=[mO])
        bw[B_ST2] = rdO
        return rdO

    def combine(hp, rd_last):
        mxc = p.op("dve", lambda E: E.reduce_max(
            cx.ncol[:, 4:5], cx.m128[:, 128:160], axis=AX.X),
            deps=[rd_last, cx.ncol_war])
        cx.mxc_war = mxc
        tr = p.op("pe", lambda E: E.transpose(cx.bank[B_BC2][0:1, 0:128],
                                              cx.ncol[:, 4:5],
                                              cx.identf[:]), deps=[mxc, bw[B_BC2]])
        gmx = p.op("dve", lambda E: E.reduce_max(cx.srow[0:1, 16:17], cx.bank[B_BC2][0:1, 0:128],
                                                 axis=AX.X), deps=[tr])
        bw[B_BC2] = gmx
        ngm = p.op("dve", lambda E: E.tensor_scalar(cx.srow[0:1, 17:18], cx.srow[0:1, 16:17],
                                                    0.0, -float(scale), ALU.max, ALU.mult),
                   deps=[gmx])
        bcm = p.mm(cx.bank[B_BC2][:, 0:1], cx.ones_row_f[0:1, :], cx.srow[0:1, 17:18], True, True,
                   deps=[ngm])
        nmc = p.op("dve", lambda E: E.tensor_copy(cx.ncol[:, 6:7],
                                                  cx.bank[B_BC2][:, 0:1]), deps=[bcm])
        bw[B_BC2] = nmc
        cx.ncol_war = nmc
        return nmc

    rd_last = None
    for tk in range(tk_hi):
        rd_last = pass1_step(0, tk, start_deps if tk < 2 else [])
    prev_rd = rd_last

    for hp in range(DT):
        nmc = combine(hp, prev_rd)
        av_e = av_o = rs_e = rs_o = None
        for tk in range(tk_hi):
            bE = B_SC0 if tk % 2 == 0 else B_SC1
            mE = p.mm(cx.bank[bE][:], lhsE(hp, tk), rhsE(hp), True, True,
                      deps=[bw[bE], nmc])
            mO = None
            if hp + 1 < DT:
                prev_rd = pass1_step(hp + 1, tk, [nmc])
            mO = p.mm(cx.bank[B_SC1 if tk % 2 == 0 else B_SC0][:], lhsO(hp, tk), rhsO(hp),
                      True, True, deps=[bw[B_SC1 if tk % 2 == 0 else B_SC0]],
                      tile_position=(64, 0))
            bO = B_SC1 if tk % 2 == 0 else B_SC0
            if causal and tk >= 4 * c:
                nfull = tk - 4 * c
                for b, mtok in ((bE, mE), (bO, mO)):
                    t = mtok
                    if nfull > 0:
                        t = p.op("dve", lambda E, b=b, n=nfull: E.tensor_scalar_add(
                            cx.bank[b][:, 0:n * 128], cx.bank[b][:, 0:n * 128], -1e6), deps=[t])
                    if nfull < 4:
                        t = p.op("dve", lambda E, b=b, q0=nfull * 128: E.tensor_add(
                            cx.bank[b][:, q0:q0 + 128], cx.bank[b][:, q0:q0 + 128],
                            cx.m128[:, 0:128]), deps=[t])
                    if b == bE:
                        mE = t
                    else:
                        mO = t
            sE = AT[:, (2 * (tk % 2)) * CH:(2 * (tk % 2) + 1) * CH]
            sO = AT[:, (2 * (tk % 2) + 1) * CH:(2 * (tk % 2) + 2) * CH]
            eE = p.op("act", lambda E, b=bE, s=sE: E.activation(
                s, cx.bank[b][:], AF.Exp, bias=cx.ncol[:, 6:7], scale=float(scale)),
                deps=[mE, cx.at_war[2 * (tk % 2)]])
            bw[bE] = eE
            eO = p.op("act", lambda E, b=bO, s=sO: E.activation(
                s, cx.bank[b][:], AF.Exp, bias=cx.ncol[:, 6:7], scale=float(scale)),
                deps=[mO, cx.at_war[2 * (tk % 2) + 1]])
            bw[bO] = eO
            first, last = (tk == 0), (tk == tk_hi - 1)
            av_e = p.mm(cx.bank[B_MM0][0:64, :], V[:, tk * 1024 + 2 * hp * 64:tk * 1024 + 2 * hp * 64 + 64],
                        sE, first, False, deps=[eE] + ([bw[B_MM0]] if first else []),
                        inc=0)
            rs_e = p.mm(cx.bank[B_MM0][64:65, :], cx.ones_col_b[:, 0:1], sE, first, last,
                        deps=[], tile_position=(0, 64), inc=1 if last else 0)
            av_o = p.mm(cx.bank[B_MM1][64:128, :],
                        V[:, tk * 1024 + (2 * hp + 1) * 64:tk * 1024 + (2 * hp + 1) * 64 + 64],
                        sO, first, False, deps=[eO] + ([bw[B_MM1]] if first else []),
                        tile_position=(0, 64), inc=0)
            rs_o = p.mm(cx.bank[B_MM1][0:1, :], cx.ones_col_b[:, 0:1], sO, first, last,
                        deps=[], inc=1 if last else 0)
            cx.at_war[2 * (tk % 2)] = rs_e
            cx.at_war[2 * (tk % 2) + 1] = rs_o
        rcE = p.op("dve", lambda E: E.reciprocal(cx.mr_b[64:65, 0:CH], cx.bank[B_MM0][64:65, :]),
                   deps=[rs_e, cx.rrow_war])
        rcO = p.op("dve", lambda E: E.reciprocal(cx.mr_b[0:1, 0:CH], cx.bank[B_MM1][0:1, :]),
                   deps=[rs_o])
        b1 = p.mm(cx.bank[B_BC1][0:64, 0:CH], cx.ones_blk[64:65, 0:64], cx.mr_b[64:65, 0:CH],
                  True, True, deps=[rcE, bw[B_BC1]], tile_position=(64, 0))
        b2 = p.mm(cx.bank[B_BC1][64:128, 0:CH], cx.ones_blk[0:1, 0:64], cx.mr_b[0:1, 0:CH],
                  True, True, deps=[rcO], tile_position=(0, 64))
        cx.rrow_war = b2
        rbt = p.op("act", lambda E: E.copy(cx.rstd_b[:], cx.bank[B_BC1][:, 0:CH]),
                   deps=[b1, b2, cx.rstdb_war])
        bw[B_BC1] = rbt
        oe = p.op("dve", lambda E, hp=hp: E.tensor_mul(
            OC[0:64, hp * CH:(hp + 1) * CH], cx.bank[B_MM0][0:64, :], cx.rstd_b[0:64, :]),
            deps=[rs_e, rbt])
        bw[B_MM0] = oe
        oo = p.op("dve", lambda E, hp=hp: E.tensor_mul(
            OC[64:128, hp * CH:(hp + 1) * CH], cx.bank[B_MM1][64:128, :], cx.rstd_b[64:128, :]),
            deps=[rs_o, rbt])
        bw[B_MM1] = oo
        cx.rstdb_war = oo
        oc_toks += [oe, oo]
    return oc_toks


def emit_l1attn(cx, hbm, src, dst, debug=False):
    p, nc = cx.p, cx.nc
    KT = cx.aa.b(0, DT * T)
    V = cx.aa.b(32768, NTT * 1024)
    AT = cx.aa.b(65536, 4 * CH)
    QC = cx.aa.b(69632, DT * CH)
    OC = cx.aa.b(77824, DT * CH)
    wq = cx.aw.b(0, DT * 1024)
    wkv = cx.aw.b(16384, DT * 1024)
    wo = cx.aw.b(32768, DT * 1024)
    cx.ncol_war = None
    cx.rrow_war = None
    cx.rstdb_war = None
    cx.at_war = [None] * 4

    lastln = []
    for c in range(NCH):
        lastln = emit_ln_chunk(cx, src, c)
    ln_all = list(lastln) + list(cx.ln_last_use or [])

    for name, dstv in (("wq", wq), ("wkv", wkv), ("wo", wo)):
        cx.load_w(dstv, hbm[name], 1024)
    p.dma("pool", cx.bcols[:, 8:16], hbm["kw_col"])
    p.dma("pool", cx.m128[:, 0:128], hbm["maskC"])
    p.dma("pool", cx.identf[:], hbm["identf"])
    w_f = p.fence("pool")
    kw1 = p.op("dve", lambda E: E.tensor_scalar_add(cx.bcols[:, 8:16], cx.bcols[:, 8:16], 1.0),
               deps=[w_f])

    kv_done = []
    for c in range(NCH):
        for ot in range(DT):
            pb = B_MM0 + (ot % 2)
            mt = None
            for dt in range(DT):
                mt = p.mm(cx.bank[pb][:], wkv[:, dt * 1024 + ot * 128:dt * 1024 + ot * 128 + 128],
                          cx.lnxt[:, dt * T + c * CH:dt * T + c * CH + CH],
                          start=(dt == 0), stop=(dt == DT - 1),
                          deps=[w_f, cx.bank_war[pb]] + (ln_all if c == 0 and ot == 0 else []),
                          inc=1 if dt == DT - 1 else 0)
            e = p.op("act", lambda E, pb=pb, ot=ot, c=c: E.mul(
                KT[:, ot * T + c * CH:ot * T + c * CH + CH], cx.bank[pb][:],
                cx.bcols[:, 8 + ot:9 + ot]), deps=[mt, kw1])
            cx.bank_war[pb] = e
            kv_done.append(e)
    v_done = []
    for tt in range(NTT):
        for vo in range(2):
            pb = B_MM0 + (vo % 2)
            mt = None
            for dt in range(DT):
                mt = p.mm(cx.bank[pb][:],
                          cx.lnxt[:, dt * T + tt * 128:dt * T + tt * 128 + 128],
                          wkv[:, dt * 1024 + vo * 512:dt * 1024 + vo * 512 + 512],
                          start=(dt == 0), stop=(dt == DT - 1),
                          deps=[cx.bank_war[pb]] if dt == 0 else [],
                          inc=1 if dt == DT - 1 else 0)
            e = fm_evac(cx, pb, V[:, tt * 1024 + vo * 512:tt * 1024 + vo * 512 + 512], deps=[mt])
            cx.bank_war[pb] = e
            v_done.append(e)

    exf_prev = [None, None]
    st_prev = [None, None]
    qc_war = []
    for c in range(1 if debug else NCH):
        q_done = []
        for ot in range(DT):
            pb = B_MM0 + (ot % 2)
            mt = None
            for dt in range(DT):
                mt = p.mm(cx.bank[pb][:], wq[:, dt * 1024 + ot * 128:dt * 1024 + ot * 128 + 128],
                          cx.lnxt[:, dt * T + c * CH:dt * T + c * CH + CH],
                          start=(dt == 0), stop=(dt == DT - 1),
                          deps=[cx.bank_war[pb]] + (qc_war + v_done[-2:] if dt == 0 else []),
                          inc=1 if dt == DT - 1 else 0)
            e = fm_evac(cx, pb, QC[:, ot * CH:(ot + 1) * CH], deps=[mt])
            cx.bank_war[pb] = e
            q_done.append(e)
        qc_war = []
        oc_toks = mha_core(cx, c, KT, V, QC, OC, AT, 1.0 / 32.0, True,
                           [kv_done[-1], q_done[-1]])
        if debug:
            bar = p.now()
            for nm, buf, wid in (("d_kt", KT, DT * T), ("d_v", V, NTT * 1024),
                                 ("d_qc", QC, DT * CH), ("d_oc", OC, DT * CH),
                                 ("d_at", AT, 4 * CH)):
                o = nc.dram_tensor(nm, [128, wid], F32, kind="ExternalOutput").ap()
                p.dma("pool", o, buf, deps=bar)
            for nm, buf in (("d_ncol", cx.ncol[:]), ("d_rrow", cx.rrow[:]),
                            ("d_m128", cx.m128[:]), ("d_lnxt", cx.lnxt[:])):
                o = nc.dram_tensor(nm, list(buf.shape), F32, kind="ExternalOutput").ap()
                p.dma("pool", o, buf, deps=bar)
            for dtl in range(DT):
                p.dma("sp", dst[dtl * 128:(dtl + 1) * 128, :], src[dtl * 128:(dtl + 1) * 128, :],
                      deps=bar)
            return []
        for ot in range(DT):
            pb = B_MM0 + (ot % 2)
            eb = ot % 2
            xo = cx.exf[:, 0:CH] if eb == 0 else cx.exf2
            lx = p.dma("sp", xo, src[ot * 128:(ot + 1) * 128, c * CH:(c + 1) * CH],
                       deps=[exf_prev[eb]])
            mt = None
            for dt in range(DT):
                mt = p.mm(cx.bank[pb][:], wo[:, dt * 1024 + ot * 128:dt * 1024 + ot * 128 + 128],
                          OC[:, dt * CH:(dt + 1) * CH],
                          start=(dt == 0), stop=(dt == DT - 1),
                          deps=[cx.bank_war[pb]] + (oc_toks if dt == 0 else []),
                          inc=1 if dt == DT - 1 else 0)
            xsl = cx.apl[:, eb * CH:(eb + 1) * CH]
            e_t = p.op("dve", lambda E, o=xsl, pb=pb, x=xo: E.tensor_add(o, cx.bank[pb][:], x),
                       deps=[mt, lx, st_prev[eb]])
            cx.bank_war[pb] = e_t
            exf_prev[eb] = e_t
            qc_war.append(e_t)
            st_prev[eb] = p.dma("sp", dst[ot * 128:(ot + 1) * 128, c * CH:(c + 1) * CH],
                                xsl, deps=[e_t])
        cx.apply_last = (cx.apply_last or []) + [t for t in st_prev if t]
    return [t for t in st_prev if t]


def prep_l1attn_inputs(inputs):
    i_ = np.arange(128, dtype=np.float32)
    diffT = i_[None, :] - i_[:, None]
    maskC = np.where(diffT >= 0, 0.0, -1e6).astype(np.float32)
    return {
        "wq": np.ascontiguousarray(inputs["L1_attn_w_q"]),
        "wkv": np.ascontiguousarray(inputs["L1_attn_w_kv"]),
        "wo": np.ascontiguousarray(inputs["L1_attn_w_o"]),
        "kw_col": col_layout(np.asarray(inputs["L1_attn_k_weights"]).reshape(-1)),
        "maskC": maskC,
        "identf": np.eye(128, dtype=np.float32),
    }


def build_l1attn_program(debug=False):
    nc = bass.Bass("TRN2", target_bir_lowering=False)
    es = ExitStack()
    hbm = _mk_io(nc, [("wq", [D, D], F32), ("wkv", [D, D], F32), ("wo", [D, D], F32),
                      ("kw_col", [128, 8], F32), ("maskC", [128, 128], F32),
                      ("identf", [128, 128], F32)])
    cx = Ctx(nc, es)
    emit_l1attn(cx, hbm, hbm["xT"], hbm["yT"], debug=debug)
    cx.p.emit()
    es.close()
    return nc


def run_sublayer_l1attn(x, inputs):
    return _run_sub("l1attn", build_l1attn_program, prep_l1attn_inputs, x, inputs)





def _rope_tile(cx, kt_slice, psum_bank, cosb, sinb, deps):
    p = cx.p
    rot = p.mm(cx.bank[psum_bank][:], cx.p2t[:], kt_slice, True, True,
               deps=deps + [cx.bank_war[psum_bank]])
    t1 = p.op("dve", lambda E: E.tensor_mul(cx.apl[:, 0:CH], kt_slice, cosb), deps=deps + [cx.apl_war])
    t2 = p.op("dve", lambda E: E.tensor_mul(cx.apl[:, CH:2 * CH], cx.bank[psum_bank][:], sinb),
              deps=[rot, t1])
    cx.bank_war[psum_bank] = t2
    t3 = p.op("dve", lambda E: E.tensor_add(kt_slice, cx.apl[:, 0:CH], cx.apl[:, CH:2 * CH]),
              deps=[t2])
    cx.apl_war = t3
    return t3


def emit_l2attn(cx, hbm, src, dst):
    p, nc = cx.p, cx.nc
    KT = cx.aa.b(0, DT * T)
    V = cx.aa.b(32768, NTT * 1024)
    QC = cx.aa.b(65536, DT * CH)
    OC = cx.aa.b(73728, DT * CH)
    AT = cx.aa.b(81920, 4 * CH)
    cosb = cx.aa.f(86016, CH)
    sinb = cx.aa.f(88064, CH)
    wq = cx.aw.b(0, DT * 1024)
    wk = cx.aw.b(16384, DT * 1024)
    wv = cx.aw.b(32768, DT * 1024)
    wo = cx.aw.b(16384, DT * 1024)
    cx.ncol_war = None
    cx.rrow_war = None
    cx.rstdb_war = None
    cx.apl_war = None
    cx.at_war = [None] * 4

    lastln = []
    for c in range(NCH):
        lastln = emit_ln_chunk(cx, src, c)
    ln_all = list(lastln) + list(cx.ln_last_use or [])
    cx.apl_war = lastln[-1]

    for part, dstv in ((0, wq), (1, wk), (2, wv)):
        cx.load_w(dstv, hbm["wqkvT"][:, part * 1024:(part + 1) * 1024], 1024)
    p.dma("pool", cx.p2t[:], hbm["p2t"])
    p.dma("pool", cx.identf[:], hbm["identf"])
    p.dma("pool", cx.bcols[:, 32:40], hbm["bout_col"])
    w_f = p.fence("pool")

    kv_done = []
    cs_tok = None
    for c in range(NCH):
        p.dma("pool", cosb, hbm["cosT"][:, c * CH:(c + 1) * CH],
              deps=[cs_tok, cx.apl_war])
        p.dma("pool", sinb, hbm["sinT"][:, c * CH:(c + 1) * CH])
        cs_f = p.fence("pool")
        for ot in range(DT):
            pb = B_MM0 + (ot % 2)
            mt = None
            for dt in range(DT):
                mt = p.mm(cx.bank[pb][:], wk[:, dt * 1024 + ot * 128:dt * 1024 + ot * 128 + 128],
                          cx.lnxt[:, dt * T + c * CH:dt * T + c * CH + CH],
                          start=(dt == 0), stop=(dt == DT - 1),
                          deps=[w_f, cx.bank_war[pb]] + (ln_all if c == 0 and ot == 0 else []),
                          inc=1 if dt == DT - 1 else 0)
            sl = KT[:, ot * T + c * CH:ot * T + c * CH + CH]
            e = fm_evac(cx, pb, sl, deps=[mt])
            cx.bank_war[pb] = e
            rt = _rope_tile(cx, sl, B_SC0 if ot % 2 == 0 else B_SC1, cosb, sinb, [e, cs_f])
            kv_done.append(rt)
        cs_tok = kv_done[-1]
    v_done = []
    for tt in range(NTT):
        for vo in range(2):
            pb = B_MM0 + (vo % 2)
            mt = None
            for dt in range(DT):
                mt = p.mm(cx.bank[pb][:],
                          cx.lnxt[:, dt * T + tt * 128:dt * T + tt * 128 + 128],
                          wv[:, dt * 1024 + vo * 512:dt * 1024 + vo * 512 + 512],
                          start=(dt == 0), stop=(dt == DT - 1),
                          deps=[cx.bank_war[pb]] if dt == 0 else [],
                          inc=1 if dt == DT - 1 else 0)
            e = fm_evac(cx, pb, V[:, tt * 1024 + vo * 512:tt * 1024 + vo * 512 + 512], deps=[mt])
            cx.bank_war[pb] = e
            v_done.append(e)
    cx.load_w(wo, hbm["woutT"], 1024, deps=kv_done)
    wo_f = p.fence("pool")

    exf_prev = [None, None]
    st_prev = [None, None]
    qc_war = []
    for c in range(NCH):
        p.dma("pool", cosb, hbm["cosT"][:, c * CH:(c + 1) * CH], deps=[cs_tok, cx.apl_war])
        p.dma("pool", sinb, hbm["sinT"][:, c * CH:(c + 1) * CH])
        cs_f = p.fence("pool")
        q_done = []
        for ot in range(DT):
            pb = B_MM0 + (ot % 2)
            mt = None
            for dt in range(DT):
                mt = p.mm(cx.bank[pb][:], wq[:, dt * 1024 + ot * 128:dt * 1024 + ot * 128 + 128],
                          cx.lnxt[:, dt * T + c * CH:dt * T + c * CH + CH],
                          start=(dt == 0), stop=(dt == DT - 1),
                          deps=[cx.bank_war[pb]] + (qc_war + v_done[-2:] if dt == 0 else []),
                          inc=1 if dt == DT - 1 else 0)
            sl = QC[:, ot * CH:(ot + 1) * CH]
            e = fm_evac(cx, pb, sl, deps=[mt])
            cx.bank_war[pb] = e
            rt = _rope_tile(cx, sl, B_SC0 if ot % 2 == 0 else B_SC1, cosb, sinb, [e, cs_f])
            q_done.append(rt)
        qc_war = []
        cs_tok = q_done[-1]
        oc_toks = mha_core(cx, c, KT, V, QC, OC, AT, 0.125, False,
                           [kv_done[-1], q_done[-1]])
        for ot in range(DT):
            pb = B_MM0 + (ot % 2)
            mt = None
            for dt in range(DT):
                mt = p.mm(cx.bank[pb][:], wo[:, dt * 1024 + ot * 128:dt * 1024 + ot * 128 + 128],
                          OC[:, dt * CH:(dt + 1) * CH],
                          start=(dt == 0), stop=(dt == DT - 1),
                          deps=[wo_f, cx.bank_war[pb]] + (oc_toks if dt == 0 else []),
                          inc=1 if dt == DT - 1 else 0)
            eb = ot % 2
            xo = cx.exf[:, 0:CH] if eb == 0 else cx.exf2
            lx = p.dma("sp", xo, src[ot * 128:(ot + 1) * 128, c * CH:(c + 1) * CH],
                       deps=[exf_prev[eb]])
            xsl = cx.apl[:, eb * CH:(eb + 1) * CH]
            e_t = p.op("dve", lambda E, o=xsl, pb=pb, bo=32 + ot, x=xo: E.scalar_tensor_tensor(
                o, cx.bank[pb][:], cx.bcols[:, bo:bo + 1], x, ALU.add, ALU.add),
                deps=[mt, lx, st_prev[eb], cx.apl_war])
            cx.bank_war[pb] = e_t
            exf_prev[eb] = e_t
            qc_war.append(e_t)
            st_prev[eb] = p.dma("sp", dst[ot * 128:(ot + 1) * 128, c * CH:(c + 1) * CH],
                                xsl, deps=[e_t])
            cx.apl_war = st_prev[eb]
        cx.apply_last = (cx.apply_last or []) + [t for t in st_prev if t]
    return [t for t in st_prev if t]


def prep_l2attn_inputs(inputs):
    hd = 64
    inv = 1.0 / (10000.0 ** (np.arange(0, hd, 2, dtype=np.float32) / hd))
    t = np.arange(T, dtype=np.float32)
    freqs = np.concatenate([np.outer(inv, t), np.outer(inv, t)], axis=0)
    cosT = np.cos(freqs).astype(np.float32)
    sinT = np.sin(freqs).astype(np.float32)
    cosT2 = np.ascontiguousarray(np.concatenate([cosT, cosT], axis=0))
    sinT2 = np.ascontiguousarray(np.concatenate([sinT, sinT], axis=0))
    P = np.zeros((64, 64), dtype=np.float32)
    for cp in range(32):
        P[cp, 2 * cp + 1] = -1.0
    for cp in range(32, 64):
        P[cp, 2 * (cp - 32)] = 1.0
    P2T = np.zeros((128, 128), dtype=np.float32)
    P2T[0:64, 0:64] = P.T
    P2T[64:128, 64:128] = P.T
    return {
        "wqkvT": np.ascontiguousarray(inputs["L2_attn_w_qkv"].T),
        "woutT": np.ascontiguousarray(inputs["L2_attn_w_out"].T),
        "bout_col": col_layout(inputs["L2_attn_b_out"]),
        "cosT": cosT2, "sinT": sinT2, "p2t": P2T,
        "identf": np.eye(128, dtype=np.float32),
    }


def build_l2attn_program():
    nc = bass.Bass("TRN2", target_bir_lowering=False)
    es = ExitStack()
    hbm = _mk_io(nc, [("wqkvT", [D, 3 * D], F32), ("woutT", [D, D], F32),
                      ("bout_col", [128, 8], F32), ("cosT", [128, T], F32),
                      ("sinT", [128, T], F32), ("p2t", [128, 128], F32),
                      ("identf", [128, 128], F32)])
    cx = Ctx(nc, es)
    emit_l2attn(cx, hbm, hbm["xT"], hbm["yT"])
    cx.p.emit()
    es.close()
    return nc


def run_sublayer_l2attn(x, inputs):
    return _run_sub("l2attn", build_l2attn_program, prep_l2attn_inputs, x, inputs)





def emit_l2ffn(cx, hbm, src, dst):
    p, nc = cx.p, cx.nc
    cx.g8_war = None
    cx.m8_war = None
    cx.combB_war = None
    KT = cx.aa.b(0, DT * T)
    V = cx.aa.b(32768, NTT * 1024)
    QC = cx.aa.b(65536, DT * CH)
    OC = cx.aa.b(73728, DT * CH)
    AT = cx.aa.b(81920, 4 * CH)
    sel = cx.aa.f(86016, 1024)
    acc = cx.aa.f(0, 16 * 1024)
    combT = cx.aa.f(73728, 2 * T)
    wq = cx.aw.b(0, DT * 1024)
    wk = cx.aw.b(16384, DT * 1024)
    wv = cx.aw.b(32768, DT * 1024)
    wg = cx.aw.b(0, DT * 8)
    we = [cx.aw.b(off, DT * 1024) for off in (0, 16384)]
    cx.ncol_war = None
    cx.rrow_war = None
    cx.rstdb_war = None
    cx.apl_war = None
    cx.at_war = [None] * 4

    lastln = []
    for c in range(NCH):
        lastln = emit_ln_chunk(cx, src, c)
    ln_all = list(lastln) + list(cx.ln_last_use or [])
    cx.apl_war = lastln[-1]

    for part, dstv in ((0, wq), (1, wk), (2, wv)):
        cx.load_w(dstv, hbm["wqkv"][:, part * 1024:(part + 1) * 1024], 1024)
    p.dma("pool", cx.bcols[:, 8:32], hbm["bqkv_col"])
    p.dma("pool", cx.brow[0:1, 0:1024], hbm["bv_row"])
    p.dma("pool", cx.identf[:], hbm["identf"])
    w_f = p.fence("pool")

    kv_done = []
    for c in range(NCH):
        for ot in range(DT):
            pb = B_MM0 + (ot % 2)
            mt = None
            for dt in range(DT):
                mt = p.mm(cx.bank[pb][:], wk[:, dt * 1024 + ot * 128:dt * 1024 + ot * 128 + 128],
                          cx.lnxt[:, dt * T + c * CH:dt * T + c * CH + CH],
                          start=(dt == 0), stop=(dt == DT - 1),
                          deps=[w_f, cx.bank_war[pb]] + (ln_all if c == 0 and ot == 0 else []),
                          inc=1 if dt == DT - 1 else 0)
            e = fm_evac(cx, pb, KT[:, ot * T + c * CH:ot * T + c * CH + CH],
                        bias_col=cx.bcols[:, 16 + ot:17 + ot], deps=[mt])
            cx.bank_war[pb] = e
            kv_done.append(e)
    v_done = []
    for tt in range(NTT):
        for vo in range(2):
            pb = B_MM0 + (vo % 2)
            bi = p.mm(cx.bank[pb][:], cx.ones_row_f[0:1, :], cx.brow[0:1, vo * 512:(vo + 1) * 512],
                      True, False, deps=[w_f, cx.bank_war[pb]] + cx.ones_tok, inc=0)
            mt = None
            for dt in range(DT):
                mt = p.mm(cx.bank[pb][:],
                          cx.lnxt[:, dt * T + tt * 128:dt * T + tt * 128 + 128],
                          wv[:, dt * 1024 + vo * 512:dt * 1024 + vo * 512 + 512],
                          start=False, stop=(dt == DT - 1), deps=[],
                          inc=1 if dt == DT - 1 else 0)
            e = fm_evac(cx, pb, V[:, tt * 1024 + vo * 512:tt * 1024 + vo * 512 + 512], deps=[mt])
            cx.bank_war[pb] = e
            v_done.append(e)

    exf_prev = [None, None]
    st_prev = [None, None]
    qc_war = []
    attn_stores = []
    for c in range(NCH):
        q_done = []
        for ot in range(DT):
            pb = B_MM0 + (ot % 2)
            mt = None
            for dt in range(DT):
                mt = p.mm(cx.bank[pb][:], wq[:, dt * 1024 + ot * 128:dt * 1024 + ot * 128 + 128],
                          cx.lnxt[:, dt * T + c * CH:dt * T + c * CH + CH],
                          start=(dt == 0), stop=(dt == DT - 1),
                          deps=[cx.bank_war[pb]] + (qc_war + v_done[-2:] if dt == 0 else []),
                          inc=1 if dt == DT - 1 else 0)
            e = fm_evac(cx, pb, QC[:, ot * CH:(ot + 1) * CH],
                        bias_col=cx.bcols[:, 8 + ot:9 + ot], deps=[mt])
            cx.bank_war[pb] = e
            q_done.append(e)
        qc_war = []
        oc_toks = mha_core(cx, c, KT, V, QC, OC, AT, 0.125, False,
                           [kv_done[-1], q_done[-1]])
        for ot in range(DT):
            eb = ot % 2
            xo = cx.exf[:, 0:CH] if eb == 0 else cx.exf2
            lx = p.dma("sp", xo, src[ot * 128:(ot + 1) * 128, c * CH:(c + 1) * CH],
                       deps=[exf_prev[eb]])
            xsl = cx.apl[:, (ot % 2) * CH:((ot % 2) + 1) * CH]
            e_t = p.op("dve", lambda E, o=xsl, ot=ot, x=xo: E.scalar_tensor_tensor(
                o, OC[:, ot * CH:(ot + 1) * CH], 0.5, x, ALU.mult, ALU.add),
                deps=[oc_toks[2 * ot], oc_toks[2 * ot + 1], lx, st_prev[ot % 2], cx.apl_war])
            exf_prev[eb] = e_t
            cx.apl_war = e_t
            qc_war.append(e_t)
            st_prev[ot % 2] = p.dma("sp", dst[ot * 128:(ot + 1) * 128, c * CH:(c + 1) * CH],
                                    xsl, deps=[e_t])
            attn_stores.append(st_prev[ot % 2])
        cx.apply_last = (cx.apply_last or []) + [t for t in st_prev if t]

    bar = p.now()
    p.dma("pool", cx.aw.b(0, 64)[:, 0:64], hbm["wgate"], deps=bar)
    p.dma("pool", cx.brow[0:1, 0:8], hbm["bg_row"], deps=bar)
    p.dma("pool", sel[0:8, :], hbm["sel"], deps=bar)
    g_f = p.fence("pool")
    comb_done = []
    gscr = cx.rrow
    for tt in range(NTT):
        pb = B_MM0 + (tt % 2)
        bi = p.mm(cx.bank[pb][:, 0:8], cx.ones_row_f[0:1, :], cx.brow[0:1, 0:8],
                  True, False, deps=[g_f, cx.bank_war[pb]] + cx.ones_tok, inc=0)
        mt = None
        for dt in range(DT):
            mt = p.mm(cx.bank[pb][:, 0:8],
                      cx.lnxt[:, dt * T + tt * 128:dt * T + tt * 128 + 128],
                      wg[:, dt * 8:(dt + 1) * 8],
                      start=False, stop=(dt == DT - 1), deps=[],
                      inc=1 if dt == DT - 1 else 0)
        s0 = cx.ncol[:, 0:1]
        g8 = cx.mr_b[:, 0:8]
        t1 = p.op("dve", lambda E, pb=pb: E.reduce_max(s0, cx.bank[pb][:, 0:8], axis=AX.X,
                                                       negate=True), deps=[mt, cx.g8_war])
        t2 = p.op("act", lambda E, pb=pb: E.activation(g8, cx.bank[pb][:, 0:8], AF.Exp, bias=s0),
                  deps=[t1])
        cx.bank_war[pb] = t2
        t3 = p.op("dve", lambda E: E.reduce_sum(cx.ncol[:, 1:2], g8, axis=AX.X), deps=[t2])
        t4 = p.op("dve", lambda E: E.reciprocal(cx.ncol[:, 2:3], cx.ncol[:, 1:2]), deps=[t3])
        t5 = p.op("dve", lambda E: E.tensor_single_scalar(g8, g8, cx.ncol[:, 2:3], ALU.mult),
                  deps=[t4])
        t6 = p.op("dve", lambda E: E.reduce_max(cx.ncol[:, 3:4], g8, axis=AX.X), deps=[t5])
        m8 = cx.rstd_b[:, 0:8]
        t7 = p.op("dve", lambda E: E.tensor_single_scalar(m8, g8, cx.ncol[:, 3:4], ALU.is_ge),
                  deps=[t6, cx.m8_war])
        m8b = cx.rstd_b[:, 8:16]
        t8 = p.op("dve", lambda E: E.scalar_tensor_tensor(m8b, m8, -1e30, g8, ALU.mult, ALU.add),
                  deps=[t7])
        t9 = p.op("dve", lambda E: E.reduce_max(cx.ncol[:, 4:5], m8b, axis=AX.X), deps=[t8])
        t10 = p.op("dve", lambda E: E.tensor_single_scalar(m8b, m8b, cx.ncol[:, 4:5], ALU.is_ge),
                   deps=[t9])
        t11 = p.op("dve", lambda E: E.tensor_add(m8, m8, m8b), deps=[t10])
        t12 = p.op("dve", lambda E: E.tensor_mul(m8, m8, g8), deps=[t11])
        t13 = p.op("dve", lambda E: E.tensor_scalar_mul(m8, m8, 0.5), deps=[t12])
        cx.g8_war = t13
        trb = B_SC0 if tt % 2 == 0 else B_SC1
        tr = p.op("pe", lambda E, trb=trb: E.transpose(cx.bank[trb][0:8, 0:128], m8, cx.identf[:]),
                  deps=[t13, cx.bank_war[trb], w_f])
        ev = p.op("act", lambda E, trb=trb, tt=tt: E.copy(combT[0:8, tt * 128:(tt + 1) * 128],
                                                          cx.bank[trb][0:8, 0:128]), deps=[tr])
        cx.bank_war[trb] = ev
        cx.m8_war = ev
        comb_done.append(ev)

    acc_ld = []
    for dt in range(DT):
        acc_ld.append(p.dma("sp", acc[:, dt * 2048:(dt + 1) * 2048],
                            dst[dt * 128:(dt + 1) * 128, :], deps=attn_stores if dt == 0 else ()))
    we_tok = [None, None]
    for e in range(8):
        wb = e % 2
        cx.load_w(we[wb], hbm["wexp"][e], 1024,
                  deps=[g_f] + (comb_done if e < 2 else []) + ([we_tok[wb]] if we_tok[wb] else []))
        p.dma("pool", cx.bcols[:, 40:48], hbm["bexp_col"][e], deps=[cx.combB_war])
        we_f = p.fence("pool")
        mm_last = []
        for c in range(NCH):
            sb = cx.bank[B_BC2]
            bc = p.mm(sb[:, 0:CH], sel[0:8, e * 128:(e + 1) * 128], combT[0:8, c * CH:(c + 1) * CH],
                      True, True, deps=[comb_done[-1], cx.bank_war[B_BC2]])
            cbe = p.op("act", lambda E: E.copy(cx.rstd_b[:, 0:CH], sb[:, 0:CH]),
                       deps=[bc, cx.combB_war, cx.m8_war])
            cx.bank_war[B_BC2] = cbe
            for ot in range(DT):
                pb = B_MM0 + (ot % 2)
                mt = None
                for dt in range(DT):
                    mt = p.mm(cx.bank[pb][:], we[wb][:, dt * 1024 + ot * 128:dt * 1024 + ot * 128 + 128],
                              cx.lnxt[:, dt * T + c * CH:dt * T + c * CH + CH],
                              start=(dt == 0), stop=(dt == DT - 1),
                              deps=[we_f, cx.bank_war[pb]] + ([acc_ld[-1]] if e == 0 and ot == 0 and c == 0 else []),
                              inc=1 if dt == DT - 1 else 0)
                tmp = cx.apl[:, (ot % 2) * CH:((ot % 2) + 1) * CH]
                u1 = p.op("dve", lambda E, pb=pb, ot=ot, t=tmp: E.scalar_tensor_tensor(
                    t, cx.bank[pb][:], cx.bcols[:, 40 + ot:41 + ot], cx.rstd_b[:, 0:CH],
                    ALU.add, ALU.mult), deps=[mt, cbe, cx.apl_war])
                cx.bank_war[pb] = u1
                asl = acc[:, ot * 2048 + c * CH:ot * 2048 + c * CH + CH]
                u2 = p.op("dve", lambda E, a=asl, t=tmp: E.tensor_add(a, a, t), deps=[u1])
                cx.apl_war = u2
                mm_last.append(u2)
        we_tok[wb] = mm_last[-1]
        cx.combB_war = mm_last[-1]
        cx.m8_war = mm_last[-1]
    fin = []
    bar2 = p.now()
    for dt in range(DT):
        fin.append(p.dma("sp", dst[dt * 128:(dt + 1) * 128, :], acc[:, dt * 2048:(dt + 1) * 2048],
                         deps=bar2))
    return fin


def prep_l2ffn_inputs(inputs):
    bqkv = np.asarray(inputs["L2_ffn_b_qkv"], np.float32)
    sel = np.zeros((8, 8 * 128), dtype=np.float32)
    for e in range(8):
        sel[e, e * 128:(e + 1) * 128] = 1.0
    be = np.stack([col_layout(np.asarray(inputs["L2_ffn_b_experts"])[e]) for e in range(8)])
    return {
        "wqkv": np.ascontiguousarray(inputs["L2_ffn_W_qkv"]),
        "bqkv_col": np.ascontiguousarray(
            np.concatenate([col_layout(bqkv[0:1024]), col_layout(bqkv[1024:2048]),
                            col_layout(bqkv[2048:3072])], axis=1)),
        "bv_row": np.ascontiguousarray(bqkv[2048:3072].reshape(1, 1024)),
        "wgate": np.ascontiguousarray(
            np.asarray(inputs["L2_ffn_W_gate"], np.float32).reshape(8, 128, 8).transpose(1, 0, 2).reshape(128, 64)),
        "bg_row": np.asarray(inputs["L2_ffn_b_gate"], np.float32).reshape(1, 8),
        "wexp": np.ascontiguousarray(inputs["L2_ffn_W_experts"]),
        "bexp_col": be,
        "sel": sel,
        "identf": np.eye(128, dtype=np.float32),
    }


def build_l2ffn_program():
    nc = bass.Bass("TRN2", target_bir_lowering=False)
    es = ExitStack()
    hbm = _mk_io(nc, [("wqkv", [D, 3 * D], F32), ("bqkv_col", [128, 24], F32),
                      ("bv_row", [1, 1024], F32), ("wgate", [128, 64], F32),
                      ("bg_row", [1, 8], F32), ("wexp", [8, D, D], F32),
                      ("bexp_col", [8, 128, 8], F32), ("sel", [8, 8 * 128], F32),
                      ("identf", [128, 128], F32)])
    cx = Ctx(nc, es)
    emit_l2ffn(cx, hbm, hbm["xT"], hbm["yT"])
    cx.p.emit()
    es.close()
    return nc


def run_sublayer_l2ffn(x, inputs):
    return _run_sub("l2ffn", build_l2ffn_program, prep_l2ffn_inputs, x, inputs)





def emit_top2_renorm(cx, wg, bg_row, combT, g_f, identf_tok):
    p = cx.p
    comb_done = []
    for tt in range(NTT):
        pb = B_MM0 + (tt % 2)
        p.mm(cx.bank[pb][:, 0:8], cx.ones_row_f[0:1, :], bg_row,
             True, False, deps=[g_f, cx.bank_war[pb]] + cx.ones_tok, inc=0)
        mt = None
        for dt in range(DT):
            mt = p.mm(cx.bank[pb][:, 0:8],
                      cx.lnxt[:, dt * T + tt * 128:dt * T + tt * 128 + 128],
                      wg[:, dt * 8:(dt + 1) * 8],
                      start=False, stop=(dt == DT - 1), deps=[],
                      inc=1 if dt == DT - 1 else 0)
        g8 = cx.mr_b[:, 0:8]
        m8 = cx.rstd_b[:, 0:8]
        m8b = cx.rstd_b[:, 8:16]
        t0 = p.op("dve", lambda E, pb=pb: E.tensor_copy(g8, cx.bank[pb][:, 0:8]),
                  deps=[mt, cx.g8_war])
        cx.bank_war[pb] = t0
        t1 = p.op("dve", lambda E: E.reduce_max(cx.ncol[:, 0:1], g8, axis=AX.X), deps=[t0])
        t2 = p.op("dve", lambda E: E.tensor_single_scalar(m8, g8, cx.ncol[:, 0:1], ALU.is_ge),
                  deps=[t1, cx.m8_war])
        t3 = p.op("dve", lambda E: E.scalar_tensor_tensor(m8b, m8, -1e30, g8, ALU.mult, ALU.add),
                  deps=[t2])
        t4 = p.op("dve", lambda E: E.reduce_max(cx.ncol[:, 1:2], m8b, axis=AX.X), deps=[t3])
        t5 = p.op("dve", lambda E: E.tensor_single_scalar(m8b, m8b, cx.ncol[:, 1:2], ALU.is_ge),
                  deps=[t4])
        t6 = p.op("dve", lambda E: E.tensor_sub(cx.ncol[:, 2:3], cx.ncol[:, 1:2], cx.ncol[:, 0:1]),
                  deps=[t4])
        t7 = p.op("act", lambda E: E.activation(cx.ncol[:, 3:4], cx.ncol[:, 2:3], AF.Exp),
                  deps=[t6])
        t8 = p.op("dve", lambda E: E.tensor_scalar_add(cx.ncol[:, 3:4], cx.ncol[:, 3:4], 1.0),
                  deps=[t7])
        t9 = p.op("dve", lambda E: E.reciprocal(cx.ncol[:, 2:3], cx.ncol[:, 3:4]), deps=[t8])
        t10 = p.op("dve", lambda E: E.tensor_scalar_mul(cx.ncol[:, 2:3], cx.ncol[:, 2:3], 0.5),
                   deps=[t9])
        t11 = p.op("dve", lambda E: E.tensor_scalar(cx.ncol[:, 3:4], cx.ncol[:, 2:3], -1.0, 0.5,
                                                    ALU.mult, ALU.add), deps=[t10])
        t12 = p.op("dve", lambda E: E.tensor_single_scalar(m8, m8, cx.ncol[:, 2:3], ALU.mult),
                   deps=[t10, t5])
        t13 = p.op("dve", lambda E: E.tensor_single_scalar(m8b, m8b, cx.ncol[:, 3:4], ALU.mult),
                   deps=[t11])
        t14 = p.op("dve", lambda E: E.tensor_add(m8, m8, m8b), deps=[t12, t13])
        cx.g8_war = t14
        trb = B_SC0 if tt % 2 == 0 else B_SC1
        tr = p.op("pe", lambda E, trb=trb: E.transpose(cx.bank[trb][0:8, 0:128], m8, cx.identf[:]),
                  deps=[t14, cx.bank_war[trb], identf_tok])
        ev = p.op("act", lambda E, trb=trb, tt=tt: E.copy(combT[0:8, tt * 128:(tt + 1) * 128],
                                                          cx.bank[trb][0:8, 0:128]), deps=[tr])
        cx.bank_war[trb] = ev
        cx.m8_war = ev
        comb_done.append(ev)
    return comb_done



def emit_l1ffn(cx, hbm, src, dst):
    p, nc = cx.p, cx.nc
    acc = cx.aa.f(0, 16 * 1024)
    H1 = cx.aa.b(65536, DT * CH)
    combT = cx.aa.f(73728, 2 * T)
    sel = cx.aa.f(81920, 1024)
    wgv = cx.aa.b(86016, DT * 8)
    w1b = [cx.aw.b(0, DT * 1024), cx.aw.b(16384, DT * 1024), cx.aw.b(32768, DT * 1024)]
    cx.g8_war = None
    cx.m8_war = None
    cx.combB_war = None
    cx.apl_war = None
    cx.b1_war = None
    cx.h1_war = [None] * DT

    lastln = []
    for c in range(NCH):
        lastln = emit_ln_chunk(cx, src, c)
    ln_all = list(lastln) + list(cx.ln_last_use or [])
    cx.apl_war = lastln[-1]

    p.dma("pool", wgv[:, 0:64], hbm["wg"], deps=ln_all)
    p.dma("pool", cx.brow[0:1, 0:8], hbm["bg_row"])
    p.dma("pool", sel[0:8, :], hbm["sel"])
    p.dma("pool", cx.identf[:], hbm["identf"])
    g_f = p.fence("pool")
    comb_done = emit_top2_renorm(cx, wgv, cx.brow[0:1, 0:8], combT, g_f, g_f)

    acc_ld = []
    for dt in range(DT):
        acc_ld.append(p.dma("sp", acc[:, dt * 2048:(dt + 1) * 2048],
                            src[dt * 128:(dt + 1) * 128, :],
                            deps=(cx.apply_last or []) if dt == 0 else ()))
    iters = [(e, h) for e in range(8) for h in range(2)]
    s_w1, s_w2, s_sp = 0, 1, 2
    cx.load_w(w1b[s_w1], hbm["w1T"][0][:, 0:1024], 1024, deps=[g_f])
    cx.load_w(w1b[s_w2], hbm["w2T"][0][0:1024, :], 1024)
    p.dma("pool", cx.bcols[:, 8:24], hbm["b1col"][0])
    p.dma("pool", cx.bcols[:, 24:32], hbm["b2col"][0])
    we_f = p.fence("pool")
    spare_free = g_f
    w2slot_free = None
    for it, (e, half) in enumerate(iters):
        if True:
            w1s = w1b[s_w1]
            w2s = w1b[s_w2]
            mm_last = None
            mm1_last = None
            for c in range(NCH):
                sb = cx.bank[B_BC2]
                bc = p.mm(sb[:, 0:CH], sel[0:8, e * 128:(e + 1) * 128],
                          combT[0:8, c * CH:(c + 1) * CH], True, True,
                          deps=[comb_done[-1], cx.bank_war[B_BC2]])
                cbe = p.op("act", lambda E: E.copy(cx.rstd_b[:, 0:CH], sb[:, 0:CH]),
                           deps=[bc, cx.combB_war, cx.m8_war])
                cx.bank_war[B_BC2] = cbe
                cbe_t = cbe
                h1_done = []
                for ht in range(DT):
                    pb = B_SC0 + (ht % 2)
                    mt = None
                    for dt in range(DT):
                        mt = p.mm(cx.bank[pb][:],
                                  w1s[:, dt * 1024 + ht * 128:dt * 1024 + ht * 128 + 128],
                                  cx.lnxt[:, dt * T + c * CH:dt * T + c * CH + CH],
                                  start=(dt == 0), stop=(dt == DT - 1),
                                  deps=[we_f, cx.bank_war[pb]],
                                  inc=1 if dt == DT - 1 else 0)
                    he = p.op("act", lambda E, pb=pb, ht=ht: E.activation(
                        H1[:, ht * CH:(ht + 1) * CH], cx.bank[pb][:], AF.Relu,
                        bias=cx.bcols[:, 8 + half * 8 + ht:9 + half * 8 + ht]),
                        deps=[mt, cx.h1_war[ht]])
                    cx.bank_war[pb] = he
                    h1_done.append(he)
                for ot in range(DT):
                    pb = B_MM0 + (ot % 2)
                    mt = None
                    for ht in range(DT):
                        mt = p.mm(cx.bank[pb][:],
                                  w2s[:, ht * 1024 + ot * 128:ht * 1024 + ot * 128 + 128],
                                  H1[:, ht * CH:(ht + 1) * CH],
                                  start=(ht == 0), stop=(ht == DT - 1),
                                  deps=[cx.bank_war[pb]] + ([h1_done[-1]] if ht == 0 else []),
                                  inc=1 if ht == DT - 1 else 0)
                    tmp = cx.apl[:, (ot % 2) * CH:((ot % 2) + 1) * CH]
                    if half == 0:
                        u1 = p.op("dve", lambda E, pb=pb, ot=ot, t=tmp: E.scalar_tensor_tensor(
                            t, cx.bank[pb][:], cx.bcols[:, 24 + ot:25 + ot], cx.rstd_b[:, 0:CH],
                            ALU.add, ALU.mult), deps=[mt, cbe_t, cx.apl_war])
                    else:
                        u1 = p.op("dve", lambda E, pb=pb, t=tmp: E.tensor_mul(
                            t, cx.bank[pb][:], cx.rstd_b[:, 0:CH]), deps=[mt, cbe_t, cx.apl_war])
                    cx.bank_war[pb] = u1
                    asl = acc[:, ot * 2048 + c * CH:ot * 2048 + c * CH + CH]
                    u2 = p.op("dve", lambda E, a=asl, t=tmp: E.tensor_add(a, a, t),
                              deps=[u1] + ([acc_ld[-1]] if e == 0 and half == 0 and c == 0 and ot == 0 else []))
                    cx.apl_war = u2
                    mm_last = u2
                cx.combB_war = mm_last
                cx.h1_war = [cx.bank_war[B_MM0]] * DT
                if c == NCH - 1:
                    mm1_last = h1_done[-1]
            if it + 1 < len(iters):
                ne, nh = iters[it + 1]
                cx.load_w(w1b[s_sp], hbm["w1T"][ne][:, nh * 1024:(nh + 1) * 1024], 1024,
                          deps=[spare_free])
                cx.load_w(w1b[s_w1], hbm["w2T"][ne][nh * 1024:(nh + 1) * 1024, :], 1024,
                          deps=[mm1_last])
                p.dma("pool", cx.bcols[:, 8:24], hbm["b1col"][ne], deps=[mm_last])
                p.dma("pool", cx.bcols[:, 24:32], hbm["b2col"][ne])
                we_f = p.fence("pool")
            spare_free = mm_last
            cx.b1_war = mm_last
            s_w1, s_w2, s_sp = s_sp, s_w1, s_w2
    fin = []
    bar2 = p.now()
    for dt in range(DT):
        fin.append(p.dma("sp", dst[dt * 128:(dt + 1) * 128, :], acc[:, dt * 2048:(dt + 1) * 2048],
                         deps=bar2))
    return fin


def prep_l1ffn_inputs(inputs):
    sel = np.zeros((8, 8 * 128), dtype=np.float32)
    for e in range(8):
        sel[e, e * 128:(e + 1) * 128] = 1.0
    W1 = np.asarray(inputs["L1_ffn_W1"])
    W2 = np.asarray(inputs["L1_ffn_W2"])
    b1 = np.asarray(inputs["L1_ffn_b1"])
    b2 = np.asarray(inputs["L1_ffn_b2"])
    return {
        "w1T": np.ascontiguousarray(W1.transpose(0, 2, 1)),
        "w2T": np.ascontiguousarray(W2.transpose(0, 2, 1)),
        "b1col": np.stack([col_layout(b1[e]) for e in range(8)]),
        "b2col": np.stack([col_layout(b2[e]) for e in range(8)]),
        "wg": np.ascontiguousarray(
            np.asarray(inputs["L1_ffn_Wg"], np.float32).T.reshape(8, 128, 8).transpose(1, 0, 2).reshape(128, 64)),
        "bg_row": np.asarray(inputs["L1_ffn_bg"], np.float32).reshape(1, 8),
        "sel": sel,
        "identf": np.eye(128, dtype=np.float32),
    }


def build_l1ffn_program():
    nc = bass.Bass("TRN2", target_bir_lowering=False)
    es = ExitStack()
    hbm = _mk_io(nc, [("w1T", [8, D, 2048], F32), ("w2T", [8, 2048, D], F32),
                      ("b1col", [8, 128, 16], F32), ("b2col", [8, 128, 8], F32),
                      ("wg", [128, 64], F32), ("bg_row", [1, 8], F32),
                      ("sel", [8, 8 * 128], F32), ("identf", [128, 128], F32)])
    cx = Ctx(nc, es)
    emit_l1ffn(cx, hbm, hbm["xT"], hbm["yT"])
    cx.p.emit()
    es.close()
    return nc


def run_sublayer_l1ffn(x, inputs):
    return _run_sub("l1ffn", build_l1ffn_program, prep_l1ffn_inputs, x, inputs)





def emit_l3ffn(cx, hbm, src, dst):
    p, nc = cx.p, cx.nc
    acc = cx.aa.f(0, 16 * 1024)
    G = cx.aa.b(65536, DT * CH)
    utmp = cx.aa.b(73728, 2 * CH)
    combT = cx.aa.f(75776, T)
    sel = cx.aa.f(83968, 1024)
    wgv = cx.aa.b(88064, DT * 8)
    w1s = cx.aw.b(0, DT * 1024)
    w3s = cx.aw.b(16384, DT * 1024)
    w2s = cx.aw.b(32768, DT * 1024)
    cx.g8_war = None
    cx.m8_war = None
    cx.combB_war = None
    cx.apl_war = None
    cx.g_war = [None] * DT
    cx.ut_war = [None, None]

    lastln = []
    for c in range(NCH):
        lastln = emit_ln_chunk(cx, src, c)
    ln_all = list(lastln) + list(cx.ln_last_use or [])
    cx.apl_war = lastln[-1]

    p.dma("pool", wgv[:, 0:64], hbm["wg"], deps=ln_all)
    p.dma("pool", cx.brow[0:1, 0:8], hbm["bg_row"])
    p.dma("pool", sel[0:8, :], hbm["sel"])
    p.dma("pool", cx.identf[:], hbm["identf"])
    g_f = p.fence("pool")
    comb_done = emit_top2_renorm(cx, wgv, cx.brow[0:1, 0:8], combT, g_f, g_f)

    acc_ld = []
    for dt in range(DT):
        acc_ld.append(p.dma("sp", acc[:, dt * 2048:(dt + 1) * 2048],
                            src[dt * 128:(dt + 1) * 128, :],
                            deps=(cx.apply_last or []) if dt == 0 else ()))
    QH = 512
    NQ = 2048 // QH
    slots = [{"w1": cx.aw.b(0, DT * QH), "w3": cx.aw.b(8192, DT * QH),
              "w2": cx.aw.b(16384, 4 * 1024)},
             {"w1": cx.aw.b(24576, DT * QH), "w3": cx.aw.b(32768, DT * QH),
              "w2": cx.aw.b(40960, 4 * 1024)}]
    iters = [(e, q) for e in range(9) for q in range(NQ)]

    def wsrc(e, name):
        if e == 8:
            return hbm["s" + name][0]
        return hbm[name][e]

    def load_iter(it, deps):
        e, q = iters[it]
        sl = slots[it % 2]
        cx.load_w(sl["w1"], wsrc(e, "w1")[:, q * QH:(q + 1) * QH], QH, deps=deps)
        cx.load_w(sl["w3"], wsrc(e, "w3")[:, q * QH:(q + 1) * QH], QH)
        cx.load_w(sl["w2"], wsrc(e, "w2")[q * QH:(q + 1) * QH, :], 1024)
        return p.fence("pool")

    slot_free = [g_f, g_f]
    we_f = load_iter(0, [g_f])
    for it, (e, q) in enumerate(iters):
        shared = (e == 8)
        sl = slots[it % 2]
        w1s, w3s, w2s = sl["w1"], sl["w3"], sl["w2"]
        mm_last = None
        for c in range(NCH):
            if not shared:
                sb = cx.bank[B_BC2]
                bc = p.mm(sb[:, 0:CH], sel[0:8, e * 128:(e + 1) * 128],
                          combT[0:8, c * CH:(c + 1) * CH], True, True,
                          deps=[comb_done[-1], cx.bank_war[B_BC2]])
                cbe = p.op("act", lambda E: E.copy(cx.rstd_b[:, 0:CH], sb[:, 0:CH]),
                           deps=[bc, cx.combB_war, cx.m8_war])
                cx.bank_war[B_BC2] = cbe
            else:
                cbe = None
            eo_done = []
            for ht in range(4):
                pb = B_SC0 + (ht % 2)
                mt = None
                for dt in range(DT):
                    mt = p.mm(cx.bank[pb][:],
                              w1s[:, dt * QH + ht * 128:dt * QH + ht * 128 + 128],
                              cx.lnxt[:, dt * T + c * CH:dt * T + c * CH + CH],
                              start=(dt == 0), stop=(dt == DT - 1),
                              deps=[we_f, cx.bank_war[pb]],
                              inc=1 if dt == DT - 1 else 0)
                ge = p.op("act", lambda E, pb=pb, ht=ht: E.activation(
                    G[:, ht * CH:(ht + 1) * CH], cx.bank[pb][:], AF.Silu),
                    deps=[mt, cx.g_war[ht]])
                cx.bank_war[pb] = ge
                mt2 = None
                for dt in range(DT):
                    mt2 = p.mm(cx.bank[pb][:],
                               w3s[:, dt * QH + ht * 128:dt * QH + ht * 128 + 128],
                               cx.lnxt[:, dt * T + c * CH:dt * T + c * CH + CH],
                               start=(dt == 0), stop=(dt == DT - 1),
                               deps=[ge] if dt == 0 else [],
                               inc=1 if dt == DT - 1 else 0)
                ue = p.op("act", lambda E, pb=pb, ht=ht: E.copy(
                    utmp[:, (ht % 2) * CH:((ht % 2) + 1) * CH], cx.bank[pb][:]),
                    deps=[mt2, cx.ut_war[ht % 2]])
                cx.bank_war[pb] = ue
                me = p.op("dve", lambda E, ht=ht: E.tensor_mul(
                    G[:, ht * CH:(ht + 1) * CH], G[:, ht * CH:(ht + 1) * CH],
                    utmp[:, (ht % 2) * CH:((ht % 2) + 1) * CH]), deps=[ue, ge])
                cx.ut_war[ht % 2] = me
                eo_done.append(me)
            for ot in range(DT):
                pb = B_MM0 + (ot % 2)
                mt = None
                for ht in range(4):
                    mt = p.mm(cx.bank[pb][:],
                              w2s[:, ht * 1024 + ot * 128:ht * 1024 + ot * 128 + 128],
                              G[:, ht * CH:(ht + 1) * CH],
                              start=(ht == 0), stop=(ht == 3),
                              deps=[cx.bank_war[pb]] + ([eo_done[-1]] if ht == 0 else []),
                              inc=1 if ht == 3 else 0)
                asl = acc[:, ot * 2048 + c * CH:ot * 2048 + c * CH + CH]
                if shared:
                    u2 = p.op("dve", lambda E, pb=pb, a=asl: E.scalar_tensor_tensor(
                        a, cx.bank[pb][:], 0.5, a, ALU.mult, ALU.add),
                        deps=[mt, cx.apl_war] + ([acc_ld[-1]] if it == 0 else []))
                    cx.bank_war[pb] = u2
                else:
                    tmp = cx.apl[:, (ot % 2) * CH:((ot % 2) + 1) * CH]
                    u1 = p.op("dve", lambda E, pb=pb, t=tmp: E.tensor_mul(
                        t, cx.bank[pb][:], cx.rstd_b[:, 0:CH]), deps=[mt, cbe, cx.apl_war])
                    cx.bank_war[pb] = u1
                    u2 = p.op("dve", lambda E, a=asl, t=tmp: E.tensor_add(a, a, t),
                              deps=[u1] + ([acc_ld[-1]] if it == 0 and c == 0 and ot == 0 else []))
                    cx.apl_war = u2
                mm_last = u2
            cx.combB_war = mm_last
            cx.g_war = [cx.bank_war[B_MM0]] * DT
        slot_free[it % 2] = mm_last
        if it + 1 < len(iters):
            we_f = load_iter(it + 1, [slot_free[(it + 1) % 2]])
    fin = []
    bar2 = p.now()
    for dt in range(DT):
        fin.append(p.dma("sp", dst[dt * 128:(dt + 1) * 128, :], acc[:, dt * 2048:(dt + 1) * 2048],
                         deps=bar2))
    return fin


def prep_l3ffn_inputs(inputs):
    sel = np.zeros((8, 8 * 128), dtype=np.float32)
    for e in range(8):
        sel[e, e * 128:(e + 1) * 128] = 1.0
    return {
        "w1": np.ascontiguousarray(inputs["L3_ffn_w1"]),
        "w3": np.ascontiguousarray(inputs["L3_ffn_w3"]),
        "w2": np.ascontiguousarray(inputs["L3_ffn_w2"]),
        "sw1": np.ascontiguousarray(inputs["L3_ffn_sw1"]),
        "sw3": np.ascontiguousarray(inputs["L3_ffn_sw3"]),
        "sw2": np.ascontiguousarray(inputs["L3_ffn_sw2"]),
        "wg": np.ascontiguousarray(
            np.asarray(inputs["L3_ffn_router_w"], np.float32).T.reshape(8, 128, 8).transpose(1, 0, 2).reshape(128, 64)),
        "bg_row": np.asarray(inputs["L3_ffn_experts_bias"], np.float32).reshape(1, 8),
        "sel": sel,
        "identf": np.eye(128, dtype=np.float32),
    }


def build_l3ffn_program():
    nc = bass.Bass("TRN2", target_bir_lowering=False)
    es = ExitStack()
    hbm = _mk_io(nc, [("w1", [8, D, 2048], F32), ("w3", [8, D, 2048], F32),
                      ("w2", [8, 2048, D], F32), ("sw1", [1, D, 2048], F32),
                      ("sw3", [1, D, 2048], F32), ("sw2", [1, 2048, D], F32),
                      ("wg", [128, 64], F32), ("bg_row", [1, 8], F32),
                      ("sel", [8, 8 * 128], F32), ("identf", [128, 128], F32)])
    cx = Ctx(nc, es)
    emit_l3ffn(cx, hbm, hbm["xT"], hbm["yT"])
    cx.p.emit()
    es.close()
    return nc


def run_sublayer_l3ffn(x, inputs):
    return _run_sub("l3ffn", build_l3ffn_program, prep_l3ffn_inputs, x, inputs)





def emit_l3attn(cx, hbm, src, dst, stop_after=None):
    p, nc = cx.p, cx.nc
    NKV = 4
    KT = cx.aa.b(0, NKV * T)
    V = cx.aa.b(16384, NTT * 512)
    QC = cx.aa.b(32768, DT * CH)
    OC = cx.aa.b(40960, DT * CH)
    AT = cx.aa.b(49152, 2 * CH)
    cosb = cx.aa.f(51200, CH)
    sinb = cx.aa.f(53248, CH)
    wq3 = cx.aa.b(55296, CH)
    hsqp = cx.aa.b(56320, CH)
    maskW = cx.aa.f(57344, 128)
    dirsrow = cx.aa.b(57856, 32)
    dirscol = cx.aa.b(57920, 8)
    wqp = cx.aw.b(0, DT * 1024)
    wkp = cx.aw.b(16384, DT * 512)
    wvp = cx.aw.b(24576, DT * 512)
    wop = cx.aw.b(32768, DT * 1024)
    SCALE = 1.0 / float(np.sqrt(128.0))
    cx.apl_war = None
    cx.hsqp_war = None
    cx.wq3_war = None

    lastln = []
    for c in range(NCH):
        lastln = emit_ln_chunk(cx, src, c)
    ln_all = list(lastln) + list(cx.ln_last_use or [])
    cx.apl_war = lastln[-1]

    cx.load_w(wqp, hbm["wq"], 1024)
    cx.load_w(wkp, hbm["wk"], 512)
    cx.load_w(wvp, hbm["wv"], 512)
    cx.load_w(wop, hbm["wo"], 1024)
    p.dma("pool", cx.p2t[:], hbm["p3t"])
    p.dma("pool", cx.identf[:], hbm["identf"])
    p.dma("pool", cx.m128[:, 0:128], hbm["maskC"])
    p.dma("pool", maskW, hbm["maskW"])
    p.dma("pool", dirsrow[0:1, 0:24], hbm["dirsrow"])
    p.dma("pool", dirscol[0:3, 0:8], hbm["dirscol"])
    p.dma("pool", cx.srow[0:1, 20:28], hbm["hs_row"])
    w_f = p.fence("pool")
    hs2 = p.op("dve", lambda E: E.tensor_scalar_mul(cx.srow[0:1, 28:36], cx.srow[0:1, 20:28],
                                                    float(np.sqrt(128.0))), deps=[w_f])

    kv_done = []
    cs_tok = None
    for c in range(NCH):
        p.dma("pool", cosb, hbm["cosT"][:, c * CH:(c + 1) * CH], deps=[cs_tok, cx.apl_war])
        p.dma("pool", sinb, hbm["sinT"][:, c * CH:(c + 1) * CH])
        cs_f = p.fence("pool")
        for kh in range(NKV):
            pb = B_MM0 + (kh % 2)
            mt = None
            for dt in range(DT):
                mt = p.mm(cx.bank[pb][:], wkp[:, dt * 512 + kh * 128:dt * 512 + kh * 128 + 128],
                          cx.lnxt[:, dt * T + c * CH:dt * T + c * CH + CH],
                          start=(dt == 0), stop=(dt == DT - 1),
                          deps=[w_f, cx.bank_war[pb]] + (ln_all if c == 0 and kh == 0 else []),
                          inc=1 if dt == DT - 1 else 0)
            sl = KT[:, kh * T + c * CH:kh * T + c * CH + CH]
            e = fm_evac(cx, pb, sl, deps=[mt])
            cx.bank_war[pb] = e
            rt = _rope_tile(cx, sl, B_SC0 if kh % 2 == 0 else B_SC1, cosb, sinb, [e, cs_f])
            kv_done.append(rt)
        cs_tok = kv_done[-1]
    v_done = []
    for tt in range(NTT):
        pb = B_MM0 + (tt % 2)
        mt = None
        for dt in range(DT):
            mt = p.mm(cx.bank[pb][:],
                      cx.lnxt[:, dt * T + tt * 128:dt * T + tt * 128 + 128],
                      wvp[:, dt * 512:(dt + 1) * 512],
                      start=(dt == 0), stop=(dt == DT - 1),
                      deps=[cx.bank_war[pb]] if dt == 0 else [],
                      inc=1 if dt == DT - 1 else 0)
        e = fm_evac(cx, pb, V[:, tt * 512:(tt + 1) * 512], deps=[mt])
        cx.bank_war[pb] = e
        v_done.append(e)

    if stop_after == "B":
        bar = p.now()
        for dtl in range(DT):
            p.dma("sp", dst[dtl * 128:(dtl + 1) * 128, :], src[dtl * 128:(dtl + 1) * 128, :],
                  deps=bar)
        return []
    exf_prev = [None, None]
    st_prev = [None, None]
    qc_war = []
    at_war = [None, None]
    ncol_war = None
    rrow_war = None
    rstdb_war = None
    for c in range(1 if stop_after in ("one", "C1", "C2") else (2 if stop_after == "two" else NCH)):
        p.dma("pool", cosb, hbm["cosT"][:, c * CH:(c + 1) * CH], deps=[cs_tok, cx.apl_war])
        p.dma("pool", sinb, hbm["sinT"][:, c * CH:(c + 1) * CH])
        cs_f = p.fence("pool")
        q_done = []
        for ot in range(DT):
            pb = B_MM0 + (ot % 2)
            mt = None
            for dt in range(DT):
                mt = p.mm(cx.bank[pb][:], wqp[:, dt * 1024 + ot * 128:dt * 1024 + ot * 128 + 128],
                          cx.lnxt[:, dt * T + c * CH:dt * T + c * CH + CH],
                          start=(dt == 0), stop=(dt == DT - 1),
                          deps=[cx.bank_war[pb]] + (qc_war + v_done[-2:] if dt == 0 else []),
                          inc=1 if dt == DT - 1 else 0)
            sl = QC[:, ot * CH:(ot + 1) * CH]
            e = fm_evac(cx, pb, sl, deps=[mt])
            cx.bank_war[pb] = e
            rt = _rope_tile(cx, sl, B_SC0 if ot % 2 == 0 else B_SC1, cosb, sinb, [e, cs_f])
            q_done.append(rt)
        qc_war = []
        cs_tok = q_done[-1]
        if stop_after == "C1":
            break
        tk_lo = max(0, 4 * c - 4)
        tk_hi = 4 * (c + 1)
        tks = list(range(tk_lo, tk_hi))
        oc_toks = []
        for h in range(8):
            kh = h // 2
            qp = p.mm(cx.bank[B_ST1][0:1, :], dirscol[0:3, h:h + 1], QC[0:3, h * CH:(h + 1) * CH],
                      True, True, deps=[q_done[-1], cx.bank_war[B_ST1], w_f])
            hq = p.op("act", lambda E, h=h: E.mul(hsqp[0:1, 0:CH], cx.bank[B_ST1][0:1, :],
                                                  cx.srow[0:1, 28 + h:29 + h]),
                      deps=[qp, hs2, cx.hsqp_war])
            cx.bank_war[B_ST1] = hq
            w3m = p.mm(cx.bank[B_ST2][0:3, :], dirsrow[0:1, 3 * h:3 * h + 3], hsqp[0:1, 0:CH],
                       True, True, deps=[hq, cx.bank_war[B_ST2]])
            w3e = p.op("act", lambda E: E.copy(wq3[0:3, 0:CH], cx.bank[B_ST2][0:3, :]),
                       deps=[w3m, cx.wq3_war])
            cx.bank_war[B_ST2] = w3e

            def score_mm(tk, pb, extra_deps):
                m1 = p.mm(cx.bank[pb][:], KT[:, kh * T + tk * 128:kh * T + tk * 128 + 128],
                          QC[:, h * CH:(h + 1) * CH], True, False,
                          deps=extra_deps + [cx.bank_war[pb]], inc=0)
                return p.mm(cx.bank[pb][:], KT[0:3, kh * T + tk * 128:kh * T + tk * 128 + 128],
                            wq3[0:3, 0:CH], False, True, deps=[w3e])

            mx_t = []
            for i, tk in enumerate(tks):
                pb = B_SC0 + (i % 2)
                mt = score_mm(tk, pb, [kv_done[-1]] if i < 2 else [])
                rd = p.op("dve", lambda E, pb=pb, i=i: E.reduce_max(
                    cx.m128[:, 128 + i:129 + i], cx.bank[pb][:], axis=AX.X), deps=[mt, ncol_war])
                cx.bank_war[pb] = rd
                mx_t.append(rd)
            mxc = p.op("dve", lambda E: E.reduce_max(cx.ncol[:, 5:6],
                                                     cx.m128[:, 128:128 + len(tks)], axis=AX.X),
                       deps=[mx_t[-1]])
            tr = p.op("pe", lambda E: E.transpose(cx.bank[B_BC2][0:1, 0:128], cx.ncol[:, 5:6],
                                                  cx.identf[:]), deps=[mxc, cx.bank_war[B_BC2]])
            gmx = p.op("dve", lambda E: E.reduce_max(cx.srow[0:1, 16:17],
                                                     cx.bank[B_BC2][0:1, 0:128], axis=AX.X), deps=[tr])
            cx.bank_war[B_BC2] = gmx
            gm2 = p.op("dve", lambda E: E.tensor_scalar_max(cx.srow[0:1, 16:17], cx.srow[0:1, 16:17],
                                                            0.0), deps=[gmx])
            ngm = p.op("dve", lambda E: E.tensor_scalar_mul(cx.srow[0:1, 17:18], cx.srow[0:1, 16:17],
                                                            -SCALE), deps=[gm2])
            bcm = p.mm(cx.bank[B_BC2][:, 0:1], cx.ones_row_f[0:1, :], cx.srow[0:1, 17:18], True, True,
                       deps=[ngm])
            nmc = p.op("dve", lambda E: E.tensor_copy(cx.ncol[:, 6:7], cx.bank[B_BC2][:, 0:1]),
                       deps=[bcm, ncol_war])
            cx.bank_war[B_BC2] = nmc
            ncol_war = nmc
            ex_last = None
            av_t = rs_t = None
            for i, tk in enumerate(tks):
                pb = B_SC0 + (i % 2)
                mt = score_mm(tk, pb, [nmc])
                d0 = tk - 4 * c
                t = mt
                if d0 > 0:
                    t = p.op("dve", lambda E, pb=pb, n=d0: E.tensor_scalar_add(
                        cx.bank[pb][:, 0:n * 128], cx.bank[pb][:, 0:n * 128], -1e6), deps=[t])
                if 0 <= d0 <= 3:
                    t = p.op("dve", lambda E, pb=pb, q0=d0 * 128: E.tensor_add(
                        cx.bank[pb][:, q0:q0 + 128], cx.bank[pb][:, q0:q0 + 128],
                        cx.m128[:, 0:128]), deps=[t])
                if 0 <= d0 + 4 <= 3:
                    t = p.op("dve", lambda E, pb=pb, q0=(d0 + 4) * 128: E.tensor_add(
                        cx.bank[pb][:, q0:q0 + 128], cx.bank[pb][:, q0:q0 + 128], maskW), deps=[t])
                if d0 + 5 <= 3:
                    t = p.op("dve", lambda E, pb=pb, q0=(d0 + 5) * 128: E.tensor_scalar_add(
                        cx.bank[pb][:, q0:q0 + 512 - q0], cx.bank[pb][:, q0:512], -1e6), deps=[t])
                sE = AT[:, (i % 2) * CH:((i % 2) + 1) * CH]
                ex = p.op("act", lambda E, pb=pb, s=sE: E.activation(
                    s, cx.bank[pb][:], AF.Exp, bias=cx.ncol[:, 6:7], scale=SCALE),
                    deps=[t, at_war[i % 2]])
                cx.bank_war[pb] = ex
                ex_last = ex
                first, last = (i == 0), (i == len(tks) - 1)
                av_t = p.mm(cx.bank[B_MM0][:], V[:, tk * 512 + kh * 128:tk * 512 + kh * 128 + 128],
                            sE, first, last, deps=[ex] + ([cx.bank_war[B_MM0]] if first else []),
                            inc=1 if last else 0)
                rs_t = p.mm(cx.bank[B_ST1][0:1, :], cx.ones_col_b[:, 0:1], sE, first, last,
                            deps=[ex] + ([cx.bank_war[B_ST1]] if first else []),
                            inc=1 if last else 0)
                at_war[i % 2] = av_t
            rc = p.op("dve", lambda E: E.reciprocal(cx.rrow[0:1, 0:CH], cx.bank[B_ST1][0:1, :]),
                      deps=[rs_t, rrow_war])
            cx.bank_war[B_ST1] = rc
            b1 = p.mm(cx.bank[B_BC1][:, 0:CH], cx.ones_row_f[0:1, :], cx.rrow[0:1, 0:CH],
                      True, True, deps=[rc, cx.bank_war[B_BC1]])
            rrow_war = b1
            rbt = p.op("act", lambda E: E.copy(cx.rstd_b[:], cx.bank[B_BC1][:, 0:CH]),
                       deps=[b1, rstdb_war])
            cx.bank_war[B_BC1] = rbt
            oe = p.op("dve", lambda E, h=h: E.tensor_mul(
                OC[:, h * CH:(h + 1) * CH], cx.bank[B_MM0][:], cx.rstd_b[:]),
                deps=[av_t, rbt])
            cx.bank_war[B_MM0] = oe
            rstdb_war = oe
            oc_toks.append(oe)
            cx.hsqp_war = oe
            cx.wq3_war = oe
        if stop_after in ("C2", "C2all"):
            continue
        for ot in range(DT):
            pb = B_MM0 + (ot % 2)
            mt = None
            for dt in range(DT):
                mt = p.mm(cx.bank[pb][:], wop[:, dt * 1024 + ot * 128:dt * 1024 + ot * 128 + 128],
                          OC[:, dt * CH:(dt + 1) * CH],
                          start=(dt == 0), stop=(dt == DT - 1),
                          deps=[cx.bank_war[pb]] + (oc_toks if dt == 0 else []),
                          inc=1 if dt == DT - 1 else 0)
            eb = ot % 2
            xo = cx.exf[:, 0:CH] if eb == 0 else cx.exf2
            lx = p.dma("sp", xo, src[ot * 128:(ot + 1) * 128, c * CH:(c + 1) * CH],
                       deps=[exf_prev[eb]])
            xsl = cx.apl[:, eb * CH:(eb + 1) * CH]
            e_t = p.op("dve", lambda E, o=xsl, pb=pb, x=xo: E.tensor_add(o, cx.bank[pb][:], x),
                       deps=[mt, lx, st_prev[eb], cx.apl_war])
            cx.bank_war[pb] = e_t
            exf_prev[eb] = e_t
            qc_war.append(e_t)
            st_prev[eb] = p.dma("sp", dst[ot * 128:(ot + 1) * 128, c * CH:(c + 1) * CH],
                                xsl, deps=[e_t])
            cx.apl_war = st_prev[eb]
        cx.apply_last = (cx.apply_last or []) + [t for t in st_prev if t]
    if stop_after in ("one", "C1", "C2", "two", "C2all"):
        bar = p.now()
        for dtl in range(DT):
            p.dma("sp", dst[dtl * 128:(dtl + 1) * 128, :], src[dtl * 128:(dtl + 1) * 128, :],
                  deps=bar)
        return []
    return [t for t in st_prev if t]


def prep_l3attn_inputs(inputs):
    import itertools
    hd = 128
    inv = 1.0 / (10000.0 ** (np.arange(0, hd, 2, dtype=np.float32) / hd))
    t = np.arange(T, dtype=np.float32)
    freqs = np.concatenate([np.outer(inv, t), np.outer(inv, t)], axis=0)
    cosT = np.cos(freqs).astype(np.float32)
    sinT = np.sin(freqs).astype(np.float32)
    P = np.zeros((128, 128), dtype=np.float32)
    for cp in range(64):
        P[cp, cp + 64] = -1.0
    for cp in range(64, 128):
        P[cp, cp - 64] = 1.0
    i_ = np.arange(128, dtype=np.float32)
    diffT = i_[None, :] - i_[:, None]
    maskC = np.where(diffT >= 0, 0.0, -1e6).astype(np.float32)
    maskW = np.where(diffT < 0, 0.0, -1e6).astype(np.float32)
    tg = np.array(list(itertools.product([-1.0, 1.0], repeat=3)), dtype=np.float32)
    tg = tg / np.linalg.norm(tg, axis=1, keepdims=True)
    dirs = tg[:8]
    return {
        "wq": np.ascontiguousarray(inputs["L3_attn_Wq"]),
        "wk": np.ascontiguousarray(inputs["L3_attn_Wk"]),
        "wv": np.ascontiguousarray(inputs["L3_attn_Wv"]),
        "wo": np.ascontiguousarray(inputs["L3_attn_Wo"]),
        "p3t": np.ascontiguousarray(P.T),
        "cosT": cosT, "sinT": sinT,
        "maskC": maskC, "maskW": maskW,
        "dirsrow": np.ascontiguousarray(dirs.reshape(1, 24)),
        "dirscol": np.ascontiguousarray(dirs.T),
        "hs_row": np.asarray(inputs["L3_attn_head_scales"], np.float32).reshape(1, 8),
        "identf": np.eye(128, dtype=np.float32),
    }


def build_l3attn_program(stop_after=None):
    nc = bass.Bass("TRN2", target_bir_lowering=False)
    es = ExitStack()
    hbm = _mk_io(nc, [("wq", [D, D], F32), ("wk", [D, 512], F32), ("wv", [D, 512], F32),
                      ("wo", [D, D], F32), ("p3t", [128, 128], F32),
                      ("cosT", [128, T], F32), ("sinT", [128, T], F32),
                      ("maskC", [128, 128], F32), ("maskW", [128, 128], F32),
                      ("dirsrow", [1, 24], F32), ("dirscol", [3, 8], F32),
                      ("hs_row", [1, 8], F32), ("identf", [128, 128], F32)])
    cx = Ctx(nc, es)
    emit_l3attn(cx, hbm, hbm["xT"], hbm["yT"], stop_after=stop_after)
    cx.p.emit()
    es.close()
    return nc


def run_sublayer_l3attn(x, inputs):
    return _run_sub("l3attn", build_l3attn_program, prep_l3attn_inputs, x, inputs)


_pre_cache_store = {}


def _pre(name, prep, inputs):
    key = (name, id(inputs))
    if key not in _pre_cache_store:
        _pre_cache_store.clear()
        _pre_cache_store[key] = prep(inputs)
    return _pre_cache_store[key]


CHAIN = [
    ("l0ffn", build_l0ffn_program, prep_l0ffn_inputs),
    ("l0attn", build_l0attn_program, prep_l0attn_inputs),
    ("l0ffn", build_l0ffn_program, prep_l0ffn_inputs),
    ("l1ffn", build_l1ffn_program, prep_l1ffn_inputs),
    ("l1attn", build_l1attn_program, prep_l1attn_inputs),
    ("l1ffn", build_l1ffn_program, prep_l1ffn_inputs),
    ("l2ffn", build_l2ffn_program, prep_l2ffn_inputs),
    ("l2attn", build_l2attn_program, prep_l2attn_inputs),
    ("l2ffn", build_l2ffn_program, prep_l2ffn_inputs),
    ("l3ffn", build_l3ffn_program, prep_l3ffn_inputs),
    ("l3attn", build_l3attn_program, prep_l3attn_inputs),
    ("l3ffn", build_l3ffn_program, prep_l3ffn_inputs),
]





_SUBLAYER_SPECS = {
    "l0ffn": ([("w0T", [D, D], F32), ("b0col", [128, 8], F32), ("alpha1", [1, 1], F32)],
              prep_l0ffn_inputs),
    "l0attn": ([("wqkv", [D, 3 * D], F32), ("wproj", [D, D], F32),
                ("bqkv_col", [128, 24], F32), ("bv_row", [1, 1024], F32),
                ("bproj_col", [128, 8], F32), ("diffT", [128, 128], F32),
                ("trilT", [128, 128], F32), ("identf", [128, 128], F32),
                ("gamma1", [1, 1], F32)], prep_l0attn_inputs),
    "l1ffn": ([("w1T", [8, D, 2048], F32), ("w2T", [8, 2048, D], F32),
               ("b1col", [8, 128, 16], F32), ("b2col", [8, 128, 8], F32),
               ("wg", [128, 64], F32), ("bg_row", [1, 8], F32),
               ("sel", [8, 8 * 128], F32), ("identf", [128, 128], F32)], prep_l1ffn_inputs),
    "l1attn": ([("wq", [D, D], F32), ("wkv", [D, D], F32), ("wo", [D, D], F32),
                ("kw_col", [128, 8], F32), ("maskC", [128, 128], F32),
                ("identf", [128, 128], F32)], prep_l1attn_inputs),
    "l2ffn": ([("wqkv", [D, 3 * D], F32), ("bqkv_col", [128, 24], F32),
               ("bv_row", [1, 1024], F32), ("wgate", [128, 64], F32),
               ("bg_row", [1, 8], F32), ("wexp", [8, D, D], F32),
               ("bexp_col", [8, 128, 8], F32), ("sel", [8, 8 * 128], F32),
               ("identf", [128, 128], F32)], prep_l2ffn_inputs),
    "l2attn": ([("wqkvT", [D, 3 * D], F32), ("woutT", [D, D], F32),
                ("bout_col", [128, 8], F32), ("cosT", [128, T], F32),
                ("sinT", [128, T], F32), ("p2t", [128, 128], F32),
                ("identf", [128, 128], F32)], prep_l2attn_inputs),
    "l3ffn": ([("w1", [8, D, 2048], F32), ("w3", [8, D, 2048], F32),
               ("w2", [8, 2048, D], F32), ("sw1", [1, D, 2048], F32),
               ("sw3", [1, D, 2048], F32), ("sw2", [1, 2048, D], F32),
               ("wg", [128, 64], F32), ("bg_row", [1, 8], F32),
               ("sel", [8, 8 * 128], F32), ("identf", [128, 128], F32)], prep_l3ffn_inputs),
    "l3attn": ([("wq", [D, D], F32), ("wk", [D, 512], F32), ("wv", [D, 512], F32),
                ("wo", [D, D], F32), ("p3t", [128, 128], F32),
                ("cosT", [128, T], F32), ("sinT", [128, T], F32),
                ("maskC", [128, 128], F32), ("maskW", [128, 128], F32),
                ("dirsrow", [1, 24], F32), ("dirscol", [3, 8], F32),
                ("hs_row", [1, 8], F32), ("identf", [128, 128], F32)], prep_l3attn_inputs),
}

_FUSED_SEQ = ["l0ffn", "l0attn", "l0ffn", "l1ffn", "l1attn", "l1ffn",
              "l2ffn", "l2attn", "l2ffn", "l3ffn", "l3attn", "l3ffn"]

_EMITTERS = {}


def _emit_sublayer(name, cx, hbm, src, dst):
    if name == "l0ffn":
        wq, wtok = emit_l0ffn_prep(cx, hbm)
        emit_l0ffn(cx, wq, src, dst, wtok)
    elif name == "l0attn":
        emit_l0attn(cx, hbm, src, dst)
    elif name == "l1ffn":
        emit_l1ffn(cx, hbm, src, dst)
    elif name == "l1attn":
        emit_l1attn(cx, hbm, src, dst)
    elif name == "l2ffn":
        emit_l2ffn(cx, hbm, src, dst)
    elif name == "l2attn":
        emit_l2attn(cx, hbm, src, dst)
    elif name == "l3ffn":
        emit_l3ffn(cx, hbm, src, dst)
    elif name == "l3attn":
        emit_l3attn(cx, hbm, src, dst)
    else:
        raise KeyError(name)


def build_fused_program():
    nc = bass.Bass("TRN2", target_bir_lowering=False)
    es = ExitStack()
    xT = nc.dram_tensor("xT", [D, T], F32, kind="ExternalInput").ap()
    yT = nc.dram_tensor("yT", [D, T], F32, kind="ExternalOutput").ap()
    xs = nc.dram_tensor("xs", [D, T], F32, kind="Internal").ap()
    hbms = {}
    for name, (specs, _prep) in _SUBLAYER_SPECS.items():
        hv = {}
        for key, shape, dt in specs:
            hv[key] = nc.dram_tensor(f"{name}_{key}", shape, dt, kind="ExternalInput").ap()
        hbms[name] = hv
    cx = Ctx(nc, es)
    n = len(_FUSED_SEQ)
    for i, name in enumerate(_FUSED_SEQ):
        src = xT if i == 0 else xs
        dst = yT if i == n - 1 else xs
        bar = cx.p.now()
        cx.sub_bar = bar
        cx.ln_last_use = bar
        _emit_sublayer(name, cx, hbms[name], src, dst)
    cx.p.emit()
    es.close()
    return nc


def kernel(**inputs):
    x = np.asarray(inputs["L0_attn_x"], dtype=np.float32)
    nc = _cached("fused", build_fused_program)
    pre = {}
    for name, (_specs, prep) in _SUBLAYER_SPECS.items():
        pv = prep(inputs)
        for k, v in pv.items():
            pre[f"{name}_{k}"] = np.ascontiguousarray(np.asarray(v, dtype=np.float32))
    in_maps = [dict(pre, xT=np.ascontiguousarray(x[c].T)) for c in range(NCORES)]
    res = run_bass_kernel_spmd(nc, in_maps, core_ids=list(range(NCORES)))
    return np.ascontiguousarray(
        np.stack([r["yT"].T for r in res.results])).astype(np.float32)


def kernel_unfused(**inputs):
    x = np.asarray(inputs["L0_attn_x"], dtype=np.float32)
    xT = [np.ascontiguousarray(x[c].T) for c in range(NCORES)]
    pres = {}
    for name, build, prep in CHAIN:
        if name not in pres:
            pres[name] = prep(inputs)
    for name, build, prep in CHAIN:
        nc = _cached(name, build)
        in_maps = [dict(pres[name], xT=xT[c]) for c in range(NCORES)]
        res = run_bass_kernel_spmd(nc, in_maps, core_ids=list(range(NCORES)))
        xT = [np.ascontiguousarray(r["yT"]) for r in res.results]
    return np.ascontiguousarray(np.stack([t.T for t in xT])).astype(np.float32)


# version 07 (final)
# speedup vs baseline: 1.1191x; 1.0012x over previous
import numpy as np
from collections import defaultdict
from contextlib import ExitStack

import concourse.bass as bass
import concourse.mybir as mybir
from concourse.bass_utils import run_bass_kernel_spmd

F32 = mybir.dt.float32
BF16 = mybir.dt.bfloat16
I8 = mybir.dt.int8
AF = mybir.ActivationFunctionType
ALU = mybir.AluOpType
AX = mybir.AxisListType

D = 1024
T = 2048
NCORES = 8
CH = 512
NCH = T // CH
DT = D // 128
NTT = T // 128
EPS = 1e-5

ENG_ATTR = {"pe": "tensor", "act": "scalar", "dve": "vector", "pool": "gpsimd", "sp": "sync"}
ENG_ORDER = ["pe", "act", "dve", "pool", "sp"]


class Prog:

    def __init__(self, nc, es):
        self.nc = nc
        self.es = es
        self.q = {e: [] for e in ENG_ORDER}
        self.cnt = defaultdict(int)
        self.mark = {e: defaultdict(int) for e in ENG_ORDER}
        self.semh = {}

    def _sem(self, name):
        if name not in self.semh:
            self.semh[name] = self.es.enter_context(self.nc.semaphore(name))
        return self.semh[name]

    def op(self, eng, fn, deps=(), sem=None, inc=1):
        dl = []
        for d in deps or ():
            if d is None:
                continue
            s, v = d
            if self.mark[eng][s] < v:
                dl.append((s, v))
                self.mark[eng][s] = v
        tok = None
        if inc:
            if sem is None:
                sem = "s_" + eng
            self.cnt[sem] += inc
            tok = (sem, self.cnt[sem])
            self._sem(sem)
        for s, _v in dl:
            self._sem(s)
        self.q[eng].append((dl, fn, (sem, inc) if inc else None))
        return tok

    def dma(self, eng, out, in_, deps=(), **kw):
        sem = "d_" + eng
        return self.op(eng, lambda E: E.dma_start(out, in_, **kw), deps, sem=sem, inc=16)

    def fence(self, eng):
        if not hasattr(self, "_fence_scr"):
            self._fence_scr = self.nc.alloc_sbuf_tensor("fence_scr", [1, 8], F32)
        tok_dma = ("d_" + eng, self.cnt["d_" + eng])
        scr = self._fence_scr
        return self.op(eng, lambda E: E.memset(scr[0:1, 0:1], 0.0), deps=[tok_dma], inc=1)

    def mm(self, out, lhsT, rhs, start, stop, deps=(), inc=1, tile_position=None):
        def fn(E, out=out, lhsT=lhsT, rhs=rhs, start=start, stop=stop, tp=tile_position):
            kw = {"skip_group_check": True}
            if tp is not None:
                kw["tile_position"] = tp
            return E.matmul(out, lhsT, rhs, start=start, stop=stop, **kw)
        return self.op("pe", fn, deps, inc=inc)

    def now(self):
        return [(s, v) for s, v in self.cnt.items()]

    def finalize(self):
        toks = [(s, v) for s, v in self.cnt.items() if s.startswith("d_")]
        self.op("sp", lambda E: None, toks, inc=0)

    def emit(self):
        self.finalize()
        with self.nc.Block() as blk:
            for e in ENG_ORDER:
                if not self.q[e]:
                    continue
                section = getattr(blk, ENG_ATTR[e])

                def body(E, e=e):
                    for dl, fn, incinfo in self.q[e]:
                        for s, v in dl:
                            E.wait_ge(self.semh[s], v)
                        ins = fn(E)
                        if incinfo is not None:
                            sem, amt = incinfo
                            ins.then_inc(self.semh[sem], amt)

                section(body)


class Arena:

    def __init__(self, nc, name, nbytes):
        self.h = nc.alloc_sbuf_tensor(name, [128, nbytes], I8)
        self.vf = self.h.bitcast(F32)
        self.vb = self.h.bitcast(BF16)
        self.nbytes = nbytes

    def f(self, off, n):
        assert off % 4 == 0 and off + 4 * n <= self.nbytes
        return self.vf[:, off // 4:off // 4 + n]

    def b(self, off, n):
        assert off % 2 == 0 and off + 2 * n <= self.nbytes
        return self.vb[:, off // 2:off // 2 + n]


B_ST1, B_ST2, B_BC1, B_BC2, B_MM0, B_MM1, B_SC0, B_SC1 = range(8)


class Ctx:

    def __init__(self, nc, es):
        self.nc = nc
        self.p = Prog(nc, es)
        p = self.p
        self.ones_row_f = alloc(nc, "ones_row_f", [1, 128], F32)
        self.ones_blk = alloc(nc, "ones_blk", [128, 64], F32)
        self.ones_row_b = alloc(nc, "ones_row_b", [1, 128], BF16)
        self.ones_col_b = alloc(nc, "ones_col_b", [128, 1], BF16)
        self.ones_col_f = alloc(nc, "ones_col_f", [128, 1], F32)
        toks = [p.op("pool", lambda E, t=t: E.memset(t[:], 1.0))
                for t in (self.ones_row_f, self.ones_row_b, self.ones_col_b, self.ones_col_f,
                          self.ones_blk)]
        self.ones_tok = toks
        self.cc_map = {}
        self.xb = alloc(nc, "c_xb", [128, DT * CH], BF16)
        self.as8 = Arena(nc, "c_as8", DT * CH * 2)
        self.sq = self.as8.b(0, DT * CH)
        self.apl = self.as8.f(0, 2 * CH)
        self.exf = alloc(nc, "c_exf", [128, CH], F32)
        self.exf2 = self.xb.bitcast(F32)[:, 0:CH]
        self.rows = alloc(nc, "c_rows", [1, 4 * CH], F32)
        self.srow = alloc(nc, "c_srow", [1, 64], F32)
        self.rrow = alloc(nc, "c_rrow", [1, CH], F32)
        self.rstd_b = alloc(nc, "c_rstdb", [128, CH], F32)
        self.mr_b = alloc(nc, "c_mrb", [128, CH], F32)
        self.ncol = alloc(nc, "c_ncol", [128, 8], F32)
        self.bcols = alloc(nc, "c_bcols", [128, 48], F32)
        self.brow = alloc(nc, "c_brow", [1, 1024], F32)
        self.m128 = alloc(nc, "c_m128", [128, 256], F32)
        self.identf = alloc(nc, "c_identf", [128, 128], F32)
        self.p2t = alloc(nc, "c_p2t", [128, 128], BF16)
        self.lnxt = alloc(nc, "c_lnxt", [128, DT * T], BF16)
        self.aw = Arena(nc, "arena_w", 49152)
        self.aa = Arena(nc, "arena_a", 90112)
        self.bank = [nc.alloc_psum_tensor(f"bank{i}", [128, CH], F32) for i in range(8)]
        self.bank_war = [None] * 8
        self.mxc_war = None
        self.ln_last_use = None
        self.bc_last_use = None
        self.apply_last = None
        self.rows_war = None

    def cc(self, val):
        if val not in self.cc_map:
            t = alloc(self.nc, f"cc_{len(self.cc_map)}", [128, 1], F32)
            tok = self.p.op("pool", lambda E, t=t, v=val: E.memset(t[:], float(v)))
            self.cc_map[val] = (t, tok)
        return self.cc_map[val]


    def bcast_row(self, row_ap, n, bank_i, deps, evac_to=None, evac_eng="act", evac_deps=()):
        p = self.p
        bk = self.bank[bank_i]
        t = p.mm(bk[:, 0:n], self.ones_row_f[0:1, :], row_ap, True, True,
                 deps=list(deps) + self.ones_tok)
        if evac_to is None:
            return bk[:, 0:n], t
        if evac_eng == "act":
            t2 = p.op("act", lambda E: E.copy(evac_to, bk[:, 0:n]), deps=[t] + list(evac_deps))
        else:
            t2 = p.op("dve", lambda E: E.tensor_copy(evac_to, bk[:, 0:n]), deps=[t] + list(evac_deps))
        return evac_to, t2

    def load_w(self, dstv, hbm_ap, ncols, deps=(), eng="pool"):
        ntiles = hbm_ap.shape[0] // 128
        i = hbm_ap.rearrange("(a p) m -> p a m", p=128)
        o = dstv.rearrange("p (a m) -> p a m", m=ncols)
        return self.p.dma(eng, o, i, deps=deps)

    def load_bias_col(self, hbm_ap, ncols, col_off, deps=(), scale=None):
        p = self.p
        dst = self.bcols[:, col_off:col_off + ncols]
        p.dma("pool", dst, hbm_ap, deps=deps)
        t = p.fence("pool")
        if scale is not None:
            t = p.op("dve", lambda E: E.tensor_scalar_mul(dst, dst, float(scale)), deps=[t])
        return t


def alloc(nc, name, shape, dtype):
    return nc.alloc_sbuf_tensor(name, shape, dtype)



def emit_ln_chunk(cx, src, c, keep_deps=()):
    p, nc = cx.p, cx.nc
    R = cx.rows
    st1, st2 = cx.bank[B_ST1], cx.bank[B_ST2]
    prev = (list(cx.ln_last_use or []) + list(cx.apply_last or [])
            + list(keep_deps) + cx.ones_tok)
    srcr = src.rearrange("(a p) m -> p a m", p=128)
    p.dma("pool", cx.xb.rearrange("p (a m) -> p a m", m=CH),
          srcr[:, :, c * CH:(c + 1) * CH], deps=prev)
    xb_f = p.fence("pool")
    sq_t = [None] * DT
    mmA_t = mmB_t = None
    for dt in range(DT):
        sl = cx.sq[:, dt * CH:(dt + 1) * CH]
        xbs = cx.xb[:, dt * CH:(dt + 1) * CH]
        sq_t[dt] = p.op("dve", lambda E, o=sl, i=xbs: E.tensor_mul(o, i, i),
                        deps=[xb_f] + (prev if dt == 0 else []))
        mmA_t = p.mm(st1[0:1, :], cx.ones_col_b[:, 0:1], xbs,
                     start=(dt == 0), stop=(dt == DT - 1),
                     deps=[xb_f] + (prev if dt == 0 else []),
                     inc=1 if dt == DT - 1 else 0)
        mmB_t = p.mm(st2[0:1, :], cx.ones_col_b[:, 0:1], sl,
                     start=(dt == 0), stop=(dt == DT - 1),
                     deps=[sq_t[dt]], inc=1 if dt == DT - 1 else 0)
    rows_war = list(cx.rows_war or [])
    m_t = p.op("dve", lambda E: E.tensor_scalar_mul(R[0:1, 0:CH], st1[0:1, :], 1.0 / D),
               deps=[mmA_t] + rows_war)
    msq_t = p.op("dve", lambda E: E.tensor_mul(R[0:1, CH:2 * CH], R[0:1, 0:CH], R[0:1, 0:CH]),
                 deps=[m_t])
    v1_t = p.op("dve", lambda E: E.tensor_scalar_mul(R[0:1, 2 * CH:3 * CH], st2[0:1, :], 1.0 / D),
                deps=[mmB_t])
    var_t = p.op("dve", lambda E: E.tensor_sub(R[0:1, 2 * CH:3 * CH], R[0:1, 2 * CH:3 * CH],
                                               R[0:1, CH:2 * CH]),
                 deps=[msq_t, v1_t])
    epsc, epst = cx.cc(EPS)
    ln_t = p.op("act", lambda E: E.activation(R[0:1, CH:2 * CH], R[0:1, 2 * CH:3 * CH], AF.Ln,
                                              bias=epsc[0:1, 0:1]),
                deps=[var_t, epst] + rows_war)
    rstd_t = p.op("act", lambda E: E.activation(R[0:1, 3 * CH:4 * CH], R[0:1, CH:2 * CH], AF.Exp,
                                                scale=-0.5),
                  deps=[ln_t])
    mr_t = p.op("dve", lambda E: E.tensor_mul(R[0:1, 0:CH], R[0:1, 0:CH], R[0:1, 3 * CH:4 * CH]),
                deps=[rstd_t])
    bprev = list(cx.bc_last_use or [])
    apply_war = list(cx.apply_last or [])
    _, e1 = cx.bcast_row(R[0:1, 3 * CH:4 * CH], CH, B_BC1, [rstd_t] + bprev,
                         evac_to=cx.rstd_b[:], evac_deps=apply_war)
    _, e2 = cx.bcast_row(R[0:1, 0:CH], CH, B_BC2, [mr_t] + bprev,
                         evac_to=cx.mr_b[:], evac_deps=apply_war)
    cx.bc_last_use = [e1, e2]
    cx.rows_war = [("s_pe", p.cnt["s_pe"])]
    out_t = []
    ap_t = [None, None]
    for dt in range(DT):
        asl = cx.apl[:, (dt % 2) * CH:((dt % 2) + 1) * CH]
        xbs = cx.xb[:, dt * CH:(dt + 1) * CH]
        t1 = p.op("dve", lambda E, o=asl, i=xbs: E.tensor_mul(o, i, cx.rstd_b[:]),
                  deps=[xb_f, e1, ap_t[dt % 2]])
        osl = cx.lnxt[:, dt * T + c * CH:dt * T + c * CH + CH]
        t2 = p.op("dve", lambda E, o=osl, i=asl: E.tensor_sub(o, i, cx.mr_b[:]),
                  deps=[t1, e2])
        ap_t[dt % 2] = t2
        out_t.append(t2)
    cx.apply_last = out_t
    cx.ln_last_use = out_t + [mmA_t, mmB_t, m_t, v1_t]
    return out_t


def fm_evac(cx, bank_i, dst_ap, bias_col=None, deps=(), scale=None):
    p = cx.p
    bk = cx.bank[bank_i]
    n = dst_ap.shape[-1]
    if bias_col is not None:
        assert scale is None
        fn = lambda E: E.add(dst_ap, bk[:, 0:n], bias_col)
    elif scale is not None:
        fn = lambda E: E.activation(dst_ap, bk[:, 0:n], AF.Copy, scale=float(scale))
    else:
        fn = lambda E: E.copy(dst_ap, bk[:, 0:n])
    return p.op("act", fn, deps=deps)



def emit_l0ffn_prep(cx, hbm):
    p, nc = cx.p, cx.nc
    bar = list(getattr(cx, "sub_bar", []) or [])
    wq = cx.aw.b(0, DT * 1024)
    wstage = cx.aw.f(16384, DT * 1024)
    red = cx.ncol
    R = cx.srow
    ts3 = cx.aa.f(0, 3 * 1024)
    ps = cx.bank[B_ST1]

    cx.load_w(wstage, hbm["w0T"], 1024, deps=bar)
    p.dma("pool", cx.bcols[:, 0:8], hbm["b0col"])
    p.dma("pool", R[0:1, 0:1], hbm["alpha1"])
    ld_f = p.fence("pool")
    rt = []
    for i in range(DT):
        rt.append(p.op("dve", lambda E, i=i: E.tensor_reduce(
            red[:, i:i + 1], wstage[:, i * 1024:(i + 1) * 1024], AX.X, ALU.add,
            apply_absolute_value=True), deps=[ld_f]))
    sm = p.mm(ps[0:1, 8:16], cx.ones_col_f[:, 0:1], red[:, 0:8], True, True,
              deps=rt + cx.ones_tok)
    tot = p.op("dve", lambda E: E.reduce_sum(R[0:1, 1:2], ps[0:1, 8:16], axis=AX.X), deps=[sm])
    d1 = p.op("dve", lambda E: E.tensor_scalar_mul(R[0:1, 2:3], R[0:1, 1:2], 0.7 / (D * 1024.0)), deps=[tot])
    d2 = p.op("dve", lambda E: E.tensor_scalar_mul(R[0:1, 3:4], R[0:1, 2:3], -1.0), deps=[d1])
    d3 = p.op("dve", lambda E: E.tensor_scalar_mul(R[0:1, 4:5], R[0:1, 0:1], 0.5), deps=[ld_f])
    bc = p.mm(ps[:, 0:3], cx.ones_row_f[0:1, :], R[0:1, 2:5], True, True,
              deps=[d1, d2, d3, tot] + cx.ones_tok)
    be = p.op("dve", lambda E: E.tensor_copy(red[:, 1:4], ps[:, 0:3]), deps=[bc])
    toks = []
    tprev = None
    for i in range(DT):
        w_sl = wstage[:, i * 1024:(i + 1) * 1024]
        a_t = p.op("dve", lambda E, w=w_sl: E.tensor_single_scalar(ts3[:, 0:1024], w, red[:, 1:2], ALU.is_gt),
                   deps=[be, tprev])
        b_t = p.op("dve", lambda E, w=w_sl: E.tensor_single_scalar(ts3[:, 1024:2048], w, red[:, 2:3], ALU.is_lt),
                   deps=[be])
        c_t = p.op("dve", lambda E: E.tensor_sub(ts3[:, 2048:3072], ts3[:, 0:1024], ts3[:, 1024:2048]),
                   deps=[a_t, b_t])
        o_t = p.op("dve", lambda E, i=i: E.tensor_single_scalar(
            wq[:, i * 1024:(i + 1) * 1024], ts3[:, 2048:3072], red[:, 3:4], ALU.mult), deps=[c_t])
        tprev = o_t
        toks.append(o_t)
    bh = p.op("dve", lambda E: E.tensor_scalar_mul(cx.bcols[:, 0:8], cx.bcols[:, 0:8], 0.5), deps=[ld_f])
    return wq, toks + [bh, d3]


def emit_l0ffn(cx, wq, src, dst, wtok):
    p = cx.p
    ev_prev = [None, None]
    st_prev = [None, None]
    exf_prev = [None, None]
    extra = list(wtok)
    for c in range(NCH):
        ln_t = emit_ln_chunk(cx, src, c, keep_deps=extra)
        extra = []
        for ot in range(DT):
            pb = B_MM0 + (ot % 2)
            eb = ot % 2
            xo = cx.exf[:, 0:CH]
            lx = p.dma("sp", xo, src[ot * 128:(ot + 1) * 128, c * CH:(c + 1) * CH],
                       deps=[exf_prev[0], exf_prev[1]])
            mt = None
            for dt in range(DT):
                mt = p.mm(cx.bank[pb][:],
                          wq[:, dt * 1024 + ot * 128:dt * 1024 + ot * 128 + 128],
                          cx.lnxt[:, dt * T + c * CH:dt * T + c * CH + CH],
                          start=(dt == 0), stop=(dt == DT - 1),
                          deps=([ln_t[dt]] if dt > 0 else [ln_t[0], ev_prev[ot % 2]]),
                          inc=1 if dt == DT - 1 else 0)
            xsl = cx.apl[:, eb * CH:(eb + 1) * CH]
            e_t = p.op("dve", lambda E, o=xsl, pm=cx.bank[pb], bo=ot, x=xo: E.scalar_tensor_tensor(
                o, pm[:], cx.bcols[:, bo:bo + 1], x, ALU.add, ALU.add),
                deps=[mt, lx, st_prev[eb], cx.apply_last[-1] if cx.apply_last else None])
            ev_prev[ot % 2] = e_t
            exf_prev[eb] = e_t
            extra.append(e_t)
            st_prev[eb] = p.dma("sp", dst[ot * 128:(ot + 1) * 128, c * CH:(c + 1) * CH],
                                xsl, deps=[e_t])
        cx.apply_last = (cx.apply_last or []) + [t for t in st_prev if t]
    return [t for t in st_prev if t]



def emit_l0attn(cx, hbm, src, dst, stop_after=None):
    p, nc = cx.p, cx.nc
    KT = cx.aa.b(0, DT * T)
    V = cx.aa.b(32768, NTT * 1024)
    AT = cx.aa.b(65536, NTT * CH)
    QC = cx.aa.b(81920, DT * CH)
    OC = QC
    wqp = cx.aw.b(0, DT * 1024)
    wkp = cx.aw.b(16384, DT * 1024)
    wvp = cx.aw.b(32768, DT * 1024)
    wproj = cx.aw.b(16384, DT * 1024)

    lastln = []
    for c in range(NCH):
        lastln = emit_ln_chunk(cx, src, c)
    ln_all = list(lastln) + list(cx.ln_last_use or [])

    if stop_after == "A":
        for dtl in range(DT):
            p.dma("sp", dst[dtl * 128:(dtl + 1) * 128, :], src[dtl * 128:(dtl + 1) * 128, :],
                  deps=ln_all)
        return []
    for part, dstv in ((0, wqp), (1, wkp), (2, wvp)):
        cx.load_w(dstv, hbm["wqkv"][:, part * 1024:(part + 1) * 1024], 1024)
    p.dma("pool", cx.bcols[:, 8:32], hbm["bqkv_col"])
    p.dma("pool", cx.brow[0:1, 0:1024], hbm["bv_row"])
    p.dma("pool", cx.m128[:, 0:128], hbm["diffT"])
    p.dma("pool", cx.m128[:, 128:256], hbm["trilT"])
    p.dma("pool", cx.identf[:], hbm["identf"])
    p.dma("pool", cx.srow[0:1, 8:9], hbm["gamma1"])
    w_f = p.fence("pool")
    lg = p.op("act", lambda E: E.activation(cx.srow[0:1, 9:10], cx.srow[0:1, 8:9], AF.Ln), deps=[w_f])
    bcg = p.mm(cx.bank[B_SC0][:, 0:1], cx.ones_row_f[0:1, :], cx.srow[0:1, 9:10], True, True,
               deps=[lg] + cx.ones_tok)
    lgc = p.op("dve", lambda E: E.tensor_copy(cx.ncol[:, 4:5], cx.bank[B_SC0][:, 0:1]), deps=[bcg])
    mk1 = p.op("dve", lambda E: E.tensor_single_scalar(cx.m128[:, 0:128], cx.m128[:, 0:128],
                                                       cx.ncol[:, 4:5], ALU.mult), deps=[lgc, w_f])
    mk2 = p.op("act", lambda E: E.activation(cx.m128[:, 0:128], cx.m128[:, 0:128], AF.Exp), deps=[mk1])
    mask_t = p.op("dve", lambda E: E.tensor_mul(cx.m128[:, 0:128], cx.m128[:, 0:128],
                                                cx.m128[:, 128:256]), deps=[mk2])

    kv_done = []
    for c in range(NCH):
        for ot in range(DT):
            pb = B_MM0 + (ot % 2)
            war = [t for t in kv_done[-2:]]
            mt = None
            for dt in range(DT):
                mt = p.mm(cx.bank[pb][:], wkp[:, dt * 1024 + ot * 128:dt * 1024 + ot * 128 + 128],
                          cx.lnxt[:, dt * T + c * CH:dt * T + c * CH + CH],
                          start=(dt == 0), stop=(dt == DT - 1),
                          deps=[w_f] + (ln_all + war if dt == 0 else []),
                          inc=1 if dt == DT - 1 else 0)
            kv_done.append(fm_evac(cx, pb, KT[:, ot * T + c * CH:ot * T + c * CH + CH],
                                   bias_col=cx.bcols[:, 16 + ot:17 + ot], deps=[mt]))
    v_done = []
    for tt in range(NTT):
        for vo in range(2):
            pb = B_MM0 + (vo % 2)
            war = [v_done[-2] if len(v_done) >= 2 else (kv_done[-2 + vo] if len(kv_done) >= 2 else None)]
            bi = p.mm(cx.bank[pb][:], cx.ones_row_f[0:1, :], cx.brow[0:1, vo * 512:(vo + 1) * 512],
                      True, False, deps=[w_f] + war + cx.ones_tok, inc=0)
            mt = None
            for dt in range(DT):
                mt = p.mm(cx.bank[pb][:],
                          cx.lnxt[:, dt * T + tt * 128:dt * T + tt * 128 + 128],
                          wvp[:, dt * 1024 + vo * 512:dt * 1024 + vo * 512 + 512],
                          start=False, stop=(dt == DT - 1), deps=[],
                          inc=1 if dt == DT - 1 else 0)
            v_done.append(fm_evac(cx, pb, V[:, tt * 1024 + vo * 512:tt * 1024 + vo * 512 + 512],
                                  deps=[mt]))
    wp_war = kv_done
    cx.load_w(wproj, hbm["wproj"], 1024, deps=wp_war)
    p.dma("pool", cx.bcols[:, 32:40], hbm["bproj_col"])
    wp_f = p.fence("pool")

    if stop_after == "B":
        for dtl in range(DT):
            p.dma("sp", dst[dtl * 128:(dtl + 1) * 128, :], src[dtl * 128:(dtl + 1) * 128, :],
                  deps=v_done + kv_done + [wp_f])
        return []
    SCALE = 1.0 / 32.0
    qc_war = []
    at_war = []
    exf_prev = [None, None]
    st_prev = [None, None]
    rb_war = []
    av_evac_war = []
    prev_ex = []
    for c in range(1 if stop_after == "one" else NCH):
        q_done = []
        for ot in range(DT):
            pb = B_MM0 + (ot % 2)
            war = qc_war + [q_done[-2] if len(q_done) >= 2 else None]
            mt = None
            for dt in range(DT):
                mt = p.mm(cx.bank[pb][:], wqp[:, dt * 1024 + ot * 128:dt * 1024 + ot * 128 + 128],
                          cx.lnxt[:, dt * T + c * CH:dt * T + c * CH + CH],
                          start=(dt == 0), stop=(dt == DT - 1),
                          deps=(war + v_done[-2:] if dt == 0 else []),
                          inc=1 if dt == DT - 1 else 0)
            q_done.append(fm_evac(cx, pb, QC[:, ot * CH:(ot + 1) * CH],
                                  bias_col=cx.bcols[:, 8 + ot:9 + ot], deps=[mt]))
        qc_war = []
        if stop_after == "C1":
            break
        mx_t = []
        for tk in range(NTT):
            pb = B_SC0 + (tk % 2)
            mt = None
            for ft in range(DT):
                mt = p.mm(cx.bank[pb][:], KT[:, ft * T + tk * 128:ft * T + tk * 128 + 128],
                          QC[:, ft * CH:(ft + 1) * CH],
                          start=(ft == 0), stop=(ft == DT - 1),
                          deps=([kv_done[-1], q_done[-1], mask_t] if tk < 2 and ft == 0 else
                                ([mx_t[tk - 2]] if ft == 0 else [])),
                          inc=1 if ft == DT - 1 else 0)
            mx_t.append(p.op("dve", lambda E, tk=tk, pb=pb: E.reduce_max(
                cx.m128[:, 128 + tk:129 + tk], cx.bank[pb][:], axis=AX.X),
                deps=[mt, mask_t] if tk == 0 else [mt]))
        mxc = p.op("dve", lambda E: E.reduce_max(cx.ncol[:, 6:7], cx.m128[:, 128:144], axis=AX.X),
                   deps=mx_t)
        tr = p.op("pe", lambda E: E.transpose(cx.bank[B_BC2][0:1, 0:128], cx.ncol[:, 6:7], cx.identf[:]),
                  deps=[mxc] + rb_war)
        gmx = p.op("dve", lambda E: E.reduce_max(cx.srow[0:1, 10:11], cx.bank[B_BC2][0:1, 0:128], axis=AX.X),
                   deps=[tr])
        gm2 = p.op("dve", lambda E: E.tensor_scalar_max(cx.srow[0:1, 10:11], cx.srow[0:1, 10:11], 0.0),
                   deps=[gmx])
        ngm = p.op("dve", lambda E: E.tensor_scalar_mul(cx.srow[0:1, 11:12], cx.srow[0:1, 10:11],
                                                        -SCALE), deps=[gm2])
        bcm = p.mm(cx.bank[B_BC2][:, 0:1], cx.ones_row_f[0:1, :], cx.srow[0:1, 11:12], True, True,
                   deps=[ngm])
        nmc = p.op("dve", lambda E: E.tensor_copy(cx.ncol[:, 7:8], cx.bank[B_BC2][:, 0:1]),
                   deps=[bcm] + prev_ex)
        if stop_after == "C2":
            break
        ex_t = []
        for tk in range(NTT):
            pb = B_SC0 + (tk % 2)
            mt = None
            for ft in range(DT):
                mt = p.mm(cx.bank[pb][:], KT[:, ft * T + tk * 128:ft * T + tk * 128 + 128],
                          QC[:, ft * CH:(ft + 1) * CH],
                          start=(ft == 0), stop=(ft == DT - 1),
                          deps=([nmc] + at_war if ft == 0 and tk < 2 else
                                ([ex_t[tk - 2]] if ft == 0 else [])),
                          inc=1 if ft == DT - 1 else 0)
            dtok = mt
            if c * 4 <= tk < (c + 1) * 4:
                qb = (tk - c * 4) * 128
                dtok = p.op("dve", lambda E, pb=pb, qb=qb: E.tensor_mul(
                    cx.bank[pb][:, qb:qb + 128], cx.bank[pb][:, qb:qb + 128], cx.m128[:, 0:128]),
                    deps=[mt])
            ex_t.append(p.op("act", lambda E, pb=pb, tk=tk: E.activation(
                AT[:, tk * CH:(tk + 1) * CH], cx.bank[pb][:], AF.Exp,
                bias=cx.ncol[:, 7:8], scale=SCALE), deps=[dtok]))
        at_war = []
        prev_ex = [ex_t[-1]]
        if stop_after == "C3":
            break
        rs_t = None
        for tk in range(NTT):
            rs_t = p.mm(cx.bank[B_ST1][0:1, :], cx.ones_col_b[:, 0:1], AT[:, tk * CH:(tk + 1) * CH],
                        start=(tk == 0), stop=(tk == NTT - 1),
                        deps=[ex_t[tk]] + (rb_war if tk == 0 else []),
                        inc=1 if tk == NTT - 1 else 0)
        rc = p.op("dve", lambda E: E.reciprocal(cx.rrow[0:1, 0:CH], cx.bank[B_ST1][0:1, :]),
                  deps=[rs_t] + rb_war)
        _, rbt = cx.bcast_row(cx.rrow[0:1, 0:CH], CH, B_BC1, [rc], evac_to=cx.rstd_b[:],
                              evac_deps=av_evac_war)
        rb_war = [rbt]
        av_done = []
        av_evac_war = []
        for ot in range(DT):
            pb = B_MM0 + (ot % 2)
            war = [av_done[-2] if len(av_done) >= 2 else q_done[-1]]
            mt = None
            for tk in range(NTT):
                mt = p.mm(cx.bank[pb][:], V[:, tk * 1024 + ot * 128:tk * 1024 + ot * 128 + 128],
                          AT[:, tk * CH:(tk + 1) * CH],
                          start=(tk == 0), stop=(tk == NTT - 1),
                          deps=(war + [ex_t[-1]] if tk == 0 else []),
                          inc=1 if tk == NTT - 1 else 0)
            e = p.op("dve", lambda E, pb=pb, ot=ot: E.tensor_mul(
                OC[:, ot * CH:(ot + 1) * CH], cx.bank[pb][:], cx.rstd_b[:]),
                deps=[mt, rbt])
            av_done.append(e)
            at_war.append(e)
            av_evac_war.append(e)
        if stop_after == "C4":
            continue
        proj_ev = [None, None]
        for ot in range(DT):
            pb = B_MM0 + (ot % 2)
            eb = ot % 2
            xo = cx.exf[:, 0:CH] if eb == 0 else cx.exf2
            lx = p.dma("sp", xo, src[ot * 128:(ot + 1) * 128, c * CH:(c + 1) * CH],
                       deps=[exf_prev[eb]])
            mt = None
            for dt in range(DT):
                mt = p.mm(cx.bank[pb][:], wproj[:, dt * 1024 + ot * 128:dt * 1024 + ot * 128 + 128],
                          OC[:, dt * CH:(dt + 1) * CH],
                          start=(dt == 0), stop=(dt == DT - 1),
                          deps=([wp_f, proj_ev[ot % 2]] + av_done if dt == 0 else []),
                          inc=1 if dt == DT - 1 else 0)
            xsl = cx.apl[:, eb * CH:(eb + 1) * CH]
            e_t = p.op("dve", lambda E, o=xsl, pb=pb, bo=32 + ot, x=xo: E.scalar_tensor_tensor(
                o, cx.bank[pb][:], cx.bcols[:, bo:bo + 1], x, ALU.add, ALU.add),
                deps=[mt, lx, st_prev[eb]])
            exf_prev[eb] = e_t
            proj_ev[ot % 2] = e_t
            qc_war.append(e_t)
            st_prev[eb] = p.dma("sp", dst[ot * 128:(ot + 1) * 128, c * CH:(c + 1) * CH],
                                xsl, deps=[e_t])
        cx.apply_last = (cx.apply_last or []) + [t for t in st_prev if t]
    if stop_after in ("one",):
        bar = p.now()
        for dtl in range(DT):
            p.dma("sp", dst[dtl * 128:(dtl + 1) * 128, :], src[dtl * 128:(dtl + 1) * 128, :],
                  deps=bar)
        return []
    if stop_after and stop_after.startswith("C"):
        bar = p.now()
        for dtl in range(DT):
            p.dma("sp", dst[dtl * 128:(dtl + 1) * 128, :], src[dtl * 128:(dtl + 1) * 128, :],
                  deps=bar)
        return []
    return [t for t in st_prev if t]



def col_layout(v):
    v = np.asarray(v).reshape(-1)
    n = v.shape[0]
    return np.ascontiguousarray(v.reshape(n // 128, 128).T)


def prep_l0ffn_inputs(inputs):
    return {
        "w0T": np.ascontiguousarray(inputs["L0_ffn_weight"].T),
        "b0col": col_layout(inputs["L0_ffn_bias"]),
        "alpha1": np.asarray(inputs["L0_ffn_alpha"], np.float32).reshape(1, 1),
    }


def prep_l0attn_inputs(inputs):
    i_ = np.arange(128, dtype=np.float32)
    diffT = i_[None, :] - i_[:, None]
    trilT = (diffT >= 0).astype(np.float32)
    bqkv = np.asarray(inputs["L0_attn_bqkv"], np.float32)
    return {
        "wqkv": np.ascontiguousarray(inputs["L0_attn_Wqkv"]),
        "wproj": np.ascontiguousarray(inputs["L0_attn_Wproj"]),
        "bqkv_col": np.ascontiguousarray(
            np.concatenate([col_layout(bqkv[0:1024]), col_layout(bqkv[1024:2048]),
                            col_layout(bqkv[2048:3072])], axis=1)),
        "bv_row": np.ascontiguousarray(bqkv[2048:3072].reshape(1, 1024)),
        "bproj_col": col_layout(inputs["L0_attn_bproj"]),
        "diffT": np.ascontiguousarray(diffT),
        "trilT": np.ascontiguousarray(trilT),
        "identf": np.eye(128, dtype=np.float32),
        "gamma1": np.asarray(inputs["L0_attn_gamma"], np.float32).reshape(1, 1),
    }



def _mk_io(nc, names_shapes):
    hbm = {"xT": nc.dram_tensor("xT", [D, T], F32, kind="ExternalInput").ap(),
           "yT": nc.dram_tensor("yT", [D, T], F32, kind="ExternalOutput").ap()}
    for name, shape, dt in names_shapes:
        hbm[name] = nc.dram_tensor(name, shape, dt, kind="ExternalInput").ap()
    return hbm


def build_l0ffn_program():
    nc = bass.Bass("TRN2", target_bir_lowering=False)
    es = ExitStack()
    hbm = _mk_io(nc, [("w0T", [D, D], F32), ("b0col", [128, 8], F32), ("alpha1", [1, 1], F32)])
    cx = Ctx(nc, es)
    wq, wtok = emit_l0ffn_prep(cx, hbm)
    emit_l0ffn(cx, wq, hbm["xT"], hbm["yT"], wtok)
    cx.p.emit()
    es.close()
    return nc


def build_l0attn_program(stop_after=None):
    nc = bass.Bass("TRN2", target_bir_lowering=False)
    es = ExitStack()
    hbm = _mk_io(nc, [("wqkv", [D, 3 * D], F32), ("wproj", [D, D], F32),
                      ("bqkv_col", [128, 24], F32), ("bv_row", [1, 1024], F32),
                      ("bproj_col", [128, 8], F32), ("diffT", [128, 128], F32),
                      ("trilT", [128, 128], F32), ("identf", [128, 128], F32),
                      ("gamma1", [1, 1], F32)])
    cx = Ctx(nc, es)
    emit_l0attn(cx, hbm, hbm["xT"], hbm["yT"], stop_after=stop_after)
    cx.p.emit()
    es.close()
    return nc


_prog_cache = {}


def _cached(name, builder):
    if name not in _prog_cache:
        _prog_cache[name] = builder()
    return _prog_cache[name]


def _run_sub(progname, builder, prep, x, inputs):
    nc = _cached(progname, builder)
    pre = prep(inputs)
    in_maps = [dict(pre, xT=np.ascontiguousarray(x[c].T)) for c in range(NCORES)]
    res = run_bass_kernel_spmd(nc, in_maps, core_ids=list(range(NCORES)))
    return np.stack([r["yT"].T for r in res.results])


def run_sublayer_l0ffn(x, inputs):
    return _run_sub("l0ffn", build_l0ffn_program, prep_l0ffn_inputs, x, inputs)


def run_sublayer_l0attn(x, inputs):
    return _run_sub("l0attn", build_l0attn_program, prep_l0attn_inputs, x, inputs)





def mha_core(cx, c, KT, V, QC, OC, AT, scale, causal, start_deps):
    p = cx.p
    bw = cx.bank_war
    oc_toks = []
    tk_hi = min(NTT, 4 * (c + 1)) if causal else NTT
    ms_t = p.op("dve", lambda E: E.memset(cx.m128[:, 128:160], 0.0), deps=[cx.ncol_war, cx.mxc_war])

    def lhsE(hp, tk):
        return KT[0:64, hp * T + tk * 128:hp * T + tk * 128 + 128]

    def lhsO(hp, tk):
        return KT[64:128, hp * T + tk * 128:hp * T + tk * 128 + 128]

    def rhsE(hp):
        return QC[0:64, hp * CH:(hp + 1) * CH]

    def rhsO(hp):
        return QC[64:128, hp * CH:(hp + 1) * CH]

    def pass1_step(hp, tk, extra):
        mE = p.mm(cx.bank[B_ST1][:], lhsE(hp, tk), rhsE(hp), True, True,
                  deps=extra + [bw[B_ST1]])
        rdE = p.op("dve", lambda E, tk=tk: E.reduce_max(
            cx.m128[:, 128 + tk:129 + tk], cx.bank[B_ST1][:], axis=AX.X), deps=[mE, ms_t])
        bw[B_ST1] = rdE
        mO = p.mm(cx.bank[B_ST2][:], lhsO(hp, tk), rhsO(hp), True, True,
                  deps=extra + [bw[B_ST2]], tile_position=(64, 0))
        rdO = p.op("dve", lambda E, tk=tk: E.reduce_max(
            cx.m128[:, 144 + tk:145 + tk], cx.bank[B_ST2][:], axis=AX.X), deps=[mO])
        bw[B_ST2] = rdO
        return rdO

    def combine(hp, rd_last):
        mxc = p.op("dve", lambda E: E.reduce_max(
            cx.ncol[:, 4:5], cx.m128[:, 128:160], axis=AX.X),
            deps=[rd_last, cx.ncol_war])
        cx.mxc_war = mxc
        tr = p.op("pe", lambda E: E.transpose(cx.bank[B_BC2][0:1, 0:128],
                                              cx.ncol[:, 4:5],
                                              cx.identf[:]), deps=[mxc, bw[B_BC2]])
        gmx = p.op("dve", lambda E: E.reduce_max(cx.srow[0:1, 16:17], cx.bank[B_BC2][0:1, 0:128],
                                                 axis=AX.X), deps=[tr])
        bw[B_BC2] = gmx
        ngm = p.op("dve", lambda E: E.tensor_scalar(cx.srow[0:1, 17:18], cx.srow[0:1, 16:17],
                                                    0.0, -float(scale), ALU.max, ALU.mult),
                   deps=[gmx])
        bcm = p.mm(cx.bank[B_BC2][:, 0:1], cx.ones_row_f[0:1, :], cx.srow[0:1, 17:18], True, True,
                   deps=[ngm])
        nmc = p.op("dve", lambda E: E.tensor_copy(cx.ncol[:, 6:7],
                                                  cx.bank[B_BC2][:, 0:1]), deps=[bcm])
        bw[B_BC2] = nmc
        cx.ncol_war = nmc
        return nmc

    rd_last = None
    for tk in range(tk_hi):
        rd_last = pass1_step(0, tk, start_deps if tk < 2 else [])
    prev_rd = rd_last

    for hp in range(DT):
        nmc = combine(hp, prev_rd)
        av_e = av_o = rs_e = rs_o = None
        for tk in range(tk_hi):
            bE = B_SC0 if tk % 2 == 0 else B_SC1
            mE = p.mm(cx.bank[bE][:], lhsE(hp, tk), rhsE(hp), True, True,
                      deps=[bw[bE], nmc])
            mO = None
            if hp + 1 < DT:
                prev_rd = pass1_step(hp + 1, tk, [nmc])
            mO = p.mm(cx.bank[B_SC1 if tk % 2 == 0 else B_SC0][:], lhsO(hp, tk), rhsO(hp),
                      True, True, deps=[bw[B_SC1 if tk % 2 == 0 else B_SC0]],
                      tile_position=(64, 0))
            bO = B_SC1 if tk % 2 == 0 else B_SC0
            if causal and tk >= 4 * c:
                nfull = tk - 4 * c
                for b, mtok in ((bE, mE), (bO, mO)):
                    t = mtok
                    if nfull > 0:
                        t = p.op("dve", lambda E, b=b, n=nfull: E.tensor_scalar_add(
                            cx.bank[b][:, 0:n * 128], cx.bank[b][:, 0:n * 128], -1e6), deps=[t])
                    if nfull < 4:
                        t = p.op("dve", lambda E, b=b, q0=nfull * 128: E.tensor_add(
                            cx.bank[b][:, q0:q0 + 128], cx.bank[b][:, q0:q0 + 128],
                            cx.m128[:, 0:128]), deps=[t])
                    if b == bE:
                        mE = t
                    else:
                        mO = t
            sE = AT[:, (2 * (tk % 2)) * CH:(2 * (tk % 2) + 1) * CH]
            sO = AT[:, (2 * (tk % 2) + 1) * CH:(2 * (tk % 2) + 2) * CH]
            eE = p.op("act", lambda E, b=bE, s=sE: E.activation(
                s, cx.bank[b][:], AF.Exp, bias=cx.ncol[:, 6:7], scale=float(scale)),
                deps=[mE, cx.at_war[2 * (tk % 2)]])
            bw[bE] = eE
            eO = p.op("act", lambda E, b=bO, s=sO: E.activation(
                s, cx.bank[b][:], AF.Exp, bias=cx.ncol[:, 6:7], scale=float(scale)),
                deps=[mO, cx.at_war[2 * (tk % 2) + 1]])
            bw[bO] = eO
            first, last = (tk == 0), (tk == tk_hi - 1)
            av_e = p.mm(cx.bank[B_MM0][0:64, :], V[:, tk * 1024 + 2 * hp * 64:tk * 1024 + 2 * hp * 64 + 64],
                        sE, first, False, deps=[eE] + ([bw[B_MM0]] if first else []),
                        inc=0)
            rs_e = p.mm(cx.bank[B_MM0][64:65, :], cx.ones_col_b[:, 0:1], sE, first, last,
                        deps=[], tile_position=(0, 64), inc=1 if last else 0)
            av_o = p.mm(cx.bank[B_MM1][64:128, :],
                        V[:, tk * 1024 + (2 * hp + 1) * 64:tk * 1024 + (2 * hp + 1) * 64 + 64],
                        sO, first, False, deps=[eO] + ([bw[B_MM1]] if first else []),
                        tile_position=(0, 64), inc=0)
            rs_o = p.mm(cx.bank[B_MM1][0:1, :], cx.ones_col_b[:, 0:1], sO, first, last,
                        deps=[], inc=1 if last else 0)
            cx.at_war[2 * (tk % 2)] = rs_e
            cx.at_war[2 * (tk % 2) + 1] = rs_o
        rcE = p.op("dve", lambda E: E.reciprocal(cx.mr_b[64:65, 0:CH], cx.bank[B_MM0][64:65, :]),
                   deps=[rs_e, cx.rrow_war])
        rcO = p.op("dve", lambda E: E.reciprocal(cx.mr_b[0:1, 0:CH], cx.bank[B_MM1][0:1, :]),
                   deps=[rs_o])
        b1 = p.mm(cx.bank[B_BC1][0:64, 0:CH], cx.ones_blk[64:65, 0:64], cx.mr_b[64:65, 0:CH],
                  True, True, deps=[rcE, bw[B_BC1]], tile_position=(64, 0))
        b2 = p.mm(cx.bank[B_BC1][64:128, 0:CH], cx.ones_blk[0:1, 0:64], cx.mr_b[0:1, 0:CH],
                  True, True, deps=[rcO], tile_position=(0, 64))
        cx.rrow_war = b2
        rbt = p.op("act", lambda E: E.copy(cx.rstd_b[:], cx.bank[B_BC1][:, 0:CH]),
                   deps=[b1, b2, cx.rstdb_war])
        bw[B_BC1] = rbt
        oe = p.op("dve", lambda E, hp=hp: E.tensor_mul(
            OC[0:64, hp * CH:(hp + 1) * CH], cx.bank[B_MM0][0:64, :], cx.rstd_b[0:64, :]),
            deps=[rs_e, rbt])
        bw[B_MM0] = oe
        oo = p.op("dve", lambda E, hp=hp: E.tensor_mul(
            OC[64:128, hp * CH:(hp + 1) * CH], cx.bank[B_MM1][64:128, :], cx.rstd_b[64:128, :]),
            deps=[rs_o, rbt])
        bw[B_MM1] = oo
        cx.rstdb_war = oo
        oc_toks += [oe, oo]
    return oc_toks


def emit_l1attn(cx, hbm, src, dst, debug=False):
    p, nc = cx.p, cx.nc
    KT = cx.aa.b(0, DT * T)
    V = cx.aa.b(32768, NTT * 1024)
    AT = cx.aa.b(65536, 4 * CH)
    QC = cx.aa.b(69632, DT * CH)
    OC = cx.aa.b(77824, DT * CH)
    wq = cx.aw.b(0, DT * 1024)
    wkv = cx.aw.b(16384, DT * 1024)
    wo = cx.aw.b(32768, DT * 1024)
    cx.ncol_war = None
    cx.rrow_war = None
    cx.rstdb_war = None
    cx.at_war = [None] * 4

    lastln = []
    for c in range(NCH):
        lastln = emit_ln_chunk(cx, src, c)
    ln_all = list(lastln) + list(cx.ln_last_use or [])

    for name, dstv in (("wq", wq), ("wkv", wkv), ("wo", wo)):
        cx.load_w(dstv, hbm[name], 1024)
    p.dma("pool", cx.bcols[:, 8:16], hbm["kw_col"])
    p.dma("pool", cx.m128[:, 0:128], hbm["maskC"])
    p.dma("pool", cx.identf[:], hbm["identf"])
    w_f = p.fence("pool")
    kw1 = p.op("dve", lambda E: E.tensor_scalar_add(cx.bcols[:, 8:16], cx.bcols[:, 8:16], 1.0),
               deps=[w_f])

    kv_done = []
    for c in range(NCH):
        for ot in range(DT):
            pb = B_MM0 + (ot % 2)
            mt = None
            for dt in range(DT):
                mt = p.mm(cx.bank[pb][:], wkv[:, dt * 1024 + ot * 128:dt * 1024 + ot * 128 + 128],
                          cx.lnxt[:, dt * T + c * CH:dt * T + c * CH + CH],
                          start=(dt == 0), stop=(dt == DT - 1),
                          deps=[w_f, cx.bank_war[pb]] + (ln_all if c == 0 and ot == 0 else []),
                          inc=1 if dt == DT - 1 else 0)
            e = p.op("act", lambda E, pb=pb, ot=ot, c=c: E.mul(
                KT[:, ot * T + c * CH:ot * T + c * CH + CH], cx.bank[pb][:],
                cx.bcols[:, 8 + ot:9 + ot]), deps=[mt, kw1])
            cx.bank_war[pb] = e
            kv_done.append(e)
    v_done = []
    for tt in range(NTT):
        for vo in range(2):
            pb = B_MM0 + (vo % 2)
            mt = None
            for dt in range(DT):
                mt = p.mm(cx.bank[pb][:],
                          cx.lnxt[:, dt * T + tt * 128:dt * T + tt * 128 + 128],
                          wkv[:, dt * 1024 + vo * 512:dt * 1024 + vo * 512 + 512],
                          start=(dt == 0), stop=(dt == DT - 1),
                          deps=[cx.bank_war[pb]] if dt == 0 else [],
                          inc=1 if dt == DT - 1 else 0)
            e = fm_evac(cx, pb, V[:, tt * 1024 + vo * 512:tt * 1024 + vo * 512 + 512], deps=[mt])
            cx.bank_war[pb] = e
            v_done.append(e)

    exf_prev = [None, None]
    st_prev = [None, None]
    qc_war = []
    for c in range(1 if debug else NCH):
        q_done = []
        for ot in range(DT):
            pb = B_MM0 + (ot % 2)
            mt = None
            for dt in range(DT):
                mt = p.mm(cx.bank[pb][:], wq[:, dt * 1024 + ot * 128:dt * 1024 + ot * 128 + 128],
                          cx.lnxt[:, dt * T + c * CH:dt * T + c * CH + CH],
                          start=(dt == 0), stop=(dt == DT - 1),
                          deps=[cx.bank_war[pb]] + (qc_war + v_done[-2:] if dt == 0 else []),
                          inc=1 if dt == DT - 1 else 0)
            e = fm_evac(cx, pb, QC[:, ot * CH:(ot + 1) * CH], deps=[mt])
            cx.bank_war[pb] = e
            q_done.append(e)
        qc_war = []
        oc_toks = mha_core(cx, c, KT, V, QC, OC, AT, 1.0 / 32.0, True,
                           [kv_done[-1], q_done[-1]])
        if debug:
            bar = p.now()
            for nm, buf, wid in (("d_kt", KT, DT * T), ("d_v", V, NTT * 1024),
                                 ("d_qc", QC, DT * CH), ("d_oc", OC, DT * CH),
                                 ("d_at", AT, 4 * CH)):
                o = nc.dram_tensor(nm, [128, wid], F32, kind="ExternalOutput").ap()
                p.dma("pool", o, buf, deps=bar)
            for nm, buf in (("d_ncol", cx.ncol[:]), ("d_rrow", cx.rrow[:]),
                            ("d_m128", cx.m128[:]), ("d_lnxt", cx.lnxt[:])):
                o = nc.dram_tensor(nm, list(buf.shape), F32, kind="ExternalOutput").ap()
                p.dma("pool", o, buf, deps=bar)
            for dtl in range(DT):
                p.dma("sp", dst[dtl * 128:(dtl + 1) * 128, :], src[dtl * 128:(dtl + 1) * 128, :],
                      deps=bar)
            return []
        for ot in range(DT):
            pb = B_MM0 + (ot % 2)
            eb = ot % 2
            xo = cx.exf[:, 0:CH] if eb == 0 else cx.exf2
            lx = p.dma("sp", xo, src[ot * 128:(ot + 1) * 128, c * CH:(c + 1) * CH],
                       deps=[exf_prev[eb]])
            mt = None
            for dt in range(DT):
                mt = p.mm(cx.bank[pb][:], wo[:, dt * 1024 + ot * 128:dt * 1024 + ot * 128 + 128],
                          OC[:, dt * CH:(dt + 1) * CH],
                          start=(dt == 0), stop=(dt == DT - 1),
                          deps=[cx.bank_war[pb]] + (oc_toks if dt == 0 else []),
                          inc=1 if dt == DT - 1 else 0)
            xsl = cx.apl[:, eb * CH:(eb + 1) * CH]
            e_t = p.op("dve", lambda E, o=xsl, pb=pb, x=xo: E.tensor_add(o, cx.bank[pb][:], x),
                       deps=[mt, lx, st_prev[eb]])
            cx.bank_war[pb] = e_t
            exf_prev[eb] = e_t
            qc_war.append(e_t)
            st_prev[eb] = p.dma("sp", dst[ot * 128:(ot + 1) * 128, c * CH:(c + 1) * CH],
                                xsl, deps=[e_t])
        cx.apply_last = (cx.apply_last or []) + [t for t in st_prev if t]
    return [t for t in st_prev if t]


def prep_l1attn_inputs(inputs):
    i_ = np.arange(128, dtype=np.float32)
    diffT = i_[None, :] - i_[:, None]
    maskC = np.where(diffT >= 0, 0.0, -1e6).astype(np.float32)
    return {
        "wq": np.ascontiguousarray(inputs["L1_attn_w_q"]),
        "wkv": np.ascontiguousarray(inputs["L1_attn_w_kv"]),
        "wo": np.ascontiguousarray(inputs["L1_attn_w_o"]),
        "kw_col": col_layout(np.asarray(inputs["L1_attn_k_weights"]).reshape(-1)),
        "maskC": maskC,
        "identf": np.eye(128, dtype=np.float32),
    }


def build_l1attn_program(debug=False):
    nc = bass.Bass("TRN2", target_bir_lowering=False)
    es = ExitStack()
    hbm = _mk_io(nc, [("wq", [D, D], F32), ("wkv", [D, D], F32), ("wo", [D, D], F32),
                      ("kw_col", [128, 8], F32), ("maskC", [128, 128], F32),
                      ("identf", [128, 128], F32)])
    cx = Ctx(nc, es)
    emit_l1attn(cx, hbm, hbm["xT"], hbm["yT"], debug=debug)
    cx.p.emit()
    es.close()
    return nc


def run_sublayer_l1attn(x, inputs):
    return _run_sub("l1attn", build_l1attn_program, prep_l1attn_inputs, x, inputs)





def _rope_tile(cx, kt_slice, psum_bank, cosb, sinb, deps):
    p = cx.p
    rot = p.mm(cx.bank[psum_bank][:], cx.p2t[:], kt_slice, True, True,
               deps=deps + [cx.bank_war[psum_bank]])
    t1 = p.op("dve", lambda E: E.tensor_mul(cx.apl[:, 0:CH], kt_slice, cosb), deps=deps + [cx.apl_war])
    t2 = p.op("dve", lambda E: E.tensor_mul(cx.apl[:, CH:2 * CH], cx.bank[psum_bank][:], sinb),
              deps=[rot, t1])
    cx.bank_war[psum_bank] = t2
    t3 = p.op("dve", lambda E: E.tensor_add(kt_slice, cx.apl[:, 0:CH], cx.apl[:, CH:2 * CH]),
              deps=[t2])
    cx.apl_war = t3
    return t3


def emit_l2attn(cx, hbm, src, dst):
    p, nc = cx.p, cx.nc
    KT = cx.aa.b(0, DT * T)
    V = cx.aa.b(32768, NTT * 1024)
    QC = cx.aa.b(65536, DT * CH)
    OC = cx.aa.b(73728, DT * CH)
    AT = cx.aa.b(81920, 4 * CH)
    cosb = cx.aa.f(86016, CH)
    sinb = cx.aa.f(88064, CH)
    wq = cx.aw.b(0, DT * 1024)
    wk = cx.aw.b(16384, DT * 1024)
    wv = cx.aw.b(32768, DT * 1024)
    wo = cx.aw.b(16384, DT * 1024)
    cx.ncol_war = None
    cx.rrow_war = None
    cx.rstdb_war = None
    cx.apl_war = None
    cx.at_war = [None] * 4

    lastln = []
    for c in range(NCH):
        lastln = emit_ln_chunk(cx, src, c)
    ln_all = list(lastln) + list(cx.ln_last_use or [])
    cx.apl_war = lastln[-1]

    for part, dstv in ((0, wq), (1, wk), (2, wv)):
        cx.load_w(dstv, hbm["wqkvT"][:, part * 1024:(part + 1) * 1024], 1024)
    p.dma("pool", cx.p2t[:], hbm["p2t"])
    p.dma("pool", cx.identf[:], hbm["identf"])
    p.dma("pool", cx.bcols[:, 32:40], hbm["bout_col"])
    w_f = p.fence("pool")

    kv_done = []
    cs_tok = None
    for c in range(NCH):
        p.dma("pool", cosb, hbm["cosT"][:, c * CH:(c + 1) * CH],
              deps=[cs_tok, cx.apl_war])
        p.dma("pool", sinb, hbm["sinT"][:, c * CH:(c + 1) * CH])
        cs_f = p.fence("pool")
        for ot in range(DT):
            pb = B_MM0 + (ot % 2)
            mt = None
            for dt in range(DT):
                mt = p.mm(cx.bank[pb][:], wk[:, dt * 1024 + ot * 128:dt * 1024 + ot * 128 + 128],
                          cx.lnxt[:, dt * T + c * CH:dt * T + c * CH + CH],
                          start=(dt == 0), stop=(dt == DT - 1),
                          deps=[w_f, cx.bank_war[pb]] + (ln_all if c == 0 and ot == 0 else []),
                          inc=1 if dt == DT - 1 else 0)
            sl = KT[:, ot * T + c * CH:ot * T + c * CH + CH]
            e = fm_evac(cx, pb, sl, deps=[mt])
            cx.bank_war[pb] = e
            rt = _rope_tile(cx, sl, B_SC0 if ot % 2 == 0 else B_SC1, cosb, sinb, [e, cs_f])
            kv_done.append(rt)
        cs_tok = kv_done[-1]
    v_done = []
    for tt in range(NTT):
        for vo in range(2):
            pb = B_MM0 + (vo % 2)
            mt = None
            for dt in range(DT):
                mt = p.mm(cx.bank[pb][:],
                          cx.lnxt[:, dt * T + tt * 128:dt * T + tt * 128 + 128],
                          wv[:, dt * 1024 + vo * 512:dt * 1024 + vo * 512 + 512],
                          start=(dt == 0), stop=(dt == DT - 1),
                          deps=[cx.bank_war[pb]] if dt == 0 else [],
                          inc=1 if dt == DT - 1 else 0)
            e = fm_evac(cx, pb, V[:, tt * 1024 + vo * 512:tt * 1024 + vo * 512 + 512], deps=[mt])
            cx.bank_war[pb] = e
            v_done.append(e)
    cx.load_w(wo, hbm["woutT"], 1024, deps=kv_done)
    wo_f = p.fence("pool")

    exf_prev = [None, None]
    st_prev = [None, None]
    qc_war = []
    for c in range(NCH):
        p.dma("pool", cosb, hbm["cosT"][:, c * CH:(c + 1) * CH], deps=[cs_tok, cx.apl_war])
        p.dma("pool", sinb, hbm["sinT"][:, c * CH:(c + 1) * CH])
        cs_f = p.fence("pool")
        q_done = []
        for ot in range(DT):
            pb = B_MM0 + (ot % 2)
            mt = None
            for dt in range(DT):
                mt = p.mm(cx.bank[pb][:], wq[:, dt * 1024 + ot * 128:dt * 1024 + ot * 128 + 128],
                          cx.lnxt[:, dt * T + c * CH:dt * T + c * CH + CH],
                          start=(dt == 0), stop=(dt == DT - 1),
                          deps=[cx.bank_war[pb]] + (qc_war + v_done[-2:] if dt == 0 else []),
                          inc=1 if dt == DT - 1 else 0)
            sl = QC[:, ot * CH:(ot + 1) * CH]
            e = fm_evac(cx, pb, sl, deps=[mt])
            cx.bank_war[pb] = e
            rt = _rope_tile(cx, sl, B_SC0 if ot % 2 == 0 else B_SC1, cosb, sinb, [e, cs_f])
            q_done.append(rt)
        qc_war = []
        cs_tok = q_done[-1]
        oc_toks = mha_core(cx, c, KT, V, QC, OC, AT, 0.125, False,
                           [kv_done[-1], q_done[-1]])
        for ot in range(DT):
            pb = B_MM0 + (ot % 2)
            mt = None
            for dt in range(DT):
                mt = p.mm(cx.bank[pb][:], wo[:, dt * 1024 + ot * 128:dt * 1024 + ot * 128 + 128],
                          OC[:, dt * CH:(dt + 1) * CH],
                          start=(dt == 0), stop=(dt == DT - 1),
                          deps=[wo_f, cx.bank_war[pb]] + (oc_toks if dt == 0 else []),
                          inc=1 if dt == DT - 1 else 0)
            eb = ot % 2
            xo = cx.exf[:, 0:CH] if eb == 0 else cx.exf2
            lx = p.dma("sp", xo, src[ot * 128:(ot + 1) * 128, c * CH:(c + 1) * CH],
                       deps=[exf_prev[eb]])
            xsl = cx.apl[:, eb * CH:(eb + 1) * CH]
            e_t = p.op("dve", lambda E, o=xsl, pb=pb, bo=32 + ot, x=xo: E.scalar_tensor_tensor(
                o, cx.bank[pb][:], cx.bcols[:, bo:bo + 1], x, ALU.add, ALU.add),
                deps=[mt, lx, st_prev[eb], cx.apl_war])
            cx.bank_war[pb] = e_t
            exf_prev[eb] = e_t
            qc_war.append(e_t)
            st_prev[eb] = p.dma("sp", dst[ot * 128:(ot + 1) * 128, c * CH:(c + 1) * CH],
                                xsl, deps=[e_t])
            cx.apl_war = st_prev[eb]
        cx.apply_last = (cx.apply_last or []) + [t for t in st_prev if t]
    return [t for t in st_prev if t]


def prep_l2attn_inputs(inputs):
    hd = 64
    inv = 1.0 / (10000.0 ** (np.arange(0, hd, 2, dtype=np.float32) / hd))
    t = np.arange(T, dtype=np.float32)
    freqs = np.concatenate([np.outer(inv, t), np.outer(inv, t)], axis=0)
    cosT = np.cos(freqs).astype(np.float32)
    sinT = np.sin(freqs).astype(np.float32)
    cosT2 = np.ascontiguousarray(np.concatenate([cosT, cosT], axis=0))
    sinT2 = np.ascontiguousarray(np.concatenate([sinT, sinT], axis=0))
    P = np.zeros((64, 64), dtype=np.float32)
    for cp in range(32):
        P[cp, 2 * cp + 1] = -1.0
    for cp in range(32, 64):
        P[cp, 2 * (cp - 32)] = 1.0
    P2T = np.zeros((128, 128), dtype=np.float32)
    P2T[0:64, 0:64] = P.T
    P2T[64:128, 64:128] = P.T
    return {
        "wqkvT": np.ascontiguousarray(inputs["L2_attn_w_qkv"].T),
        "woutT": np.ascontiguousarray(inputs["L2_attn_w_out"].T),
        "bout_col": col_layout(inputs["L2_attn_b_out"]),
        "cosT": cosT2, "sinT": sinT2, "p2t": P2T,
        "identf": np.eye(128, dtype=np.float32),
    }


def build_l2attn_program():
    nc = bass.Bass("TRN2", target_bir_lowering=False)
    es = ExitStack()
    hbm = _mk_io(nc, [("wqkvT", [D, 3 * D], F32), ("woutT", [D, D], F32),
                      ("bout_col", [128, 8], F32), ("cosT", [128, T], F32),
                      ("sinT", [128, T], F32), ("p2t", [128, 128], F32),
                      ("identf", [128, 128], F32)])
    cx = Ctx(nc, es)
    emit_l2attn(cx, hbm, hbm["xT"], hbm["yT"])
    cx.p.emit()
    es.close()
    return nc


def run_sublayer_l2attn(x, inputs):
    return _run_sub("l2attn", build_l2attn_program, prep_l2attn_inputs, x, inputs)





def emit_l2ffn(cx, hbm, src, dst):
    p, nc = cx.p, cx.nc
    cx.g8_war = None
    cx.m8_war = None
    cx.combB_war = None
    KT = cx.aa.b(0, DT * T)
    V = cx.aa.b(32768, NTT * 1024)
    QC = cx.aa.b(65536, DT * CH)
    OC = cx.aa.b(73728, DT * CH)
    AT = cx.aa.b(81920, 4 * CH)
    sel = cx.aa.f(86016, 1024)
    acc = cx.aa.f(0, 16 * 1024)
    combT = cx.aa.f(73728, 2 * T)
    wq = cx.aw.b(0, DT * 1024)
    wk = cx.aw.b(16384, DT * 1024)
    wv = cx.aw.b(32768, DT * 1024)
    wg = cx.aw.b(0, DT * 8)
    we = [cx.aw.b(off, DT * 1024) for off in (0, 16384)]
    cx.ncol_war = None
    cx.rrow_war = None
    cx.rstdb_war = None
    cx.apl_war = None
    cx.at_war = [None] * 4

    lastln = []
    for c in range(NCH):
        lastln = emit_ln_chunk(cx, src, c)
    ln_all = list(lastln) + list(cx.ln_last_use or [])
    cx.apl_war = lastln[-1]

    for part, dstv in ((0, wq), (1, wk), (2, wv)):
        cx.load_w(dstv, hbm["wqkv"][:, part * 1024:(part + 1) * 1024], 1024)
    p.dma("pool", cx.bcols[:, 8:32], hbm["bqkv_col"])
    p.dma("pool", cx.brow[0:1, 0:1024], hbm["bv_row"])
    p.dma("pool", cx.identf[:], hbm["identf"])
    w_f = p.fence("pool")

    kv_done = []
    for c in range(NCH):
        for ot in range(DT):
            pb = B_MM0 + (ot % 2)
            mt = None
            for dt in range(DT):
                mt = p.mm(cx.bank[pb][:], wk[:, dt * 1024 + ot * 128:dt * 1024 + ot * 128 + 128],
                          cx.lnxt[:, dt * T + c * CH:dt * T + c * CH + CH],
                          start=(dt == 0), stop=(dt == DT - 1),
                          deps=[w_f, cx.bank_war[pb]] + (ln_all if c == 0 and ot == 0 else []),
                          inc=1 if dt == DT - 1 else 0)
            e = fm_evac(cx, pb, KT[:, ot * T + c * CH:ot * T + c * CH + CH],
                        bias_col=cx.bcols[:, 16 + ot:17 + ot], deps=[mt])
            cx.bank_war[pb] = e
            kv_done.append(e)
    v_done = []
    for tt in range(NTT):
        for vo in range(2):
            pb = B_MM0 + (vo % 2)
            bi = p.mm(cx.bank[pb][:], cx.ones_row_f[0:1, :], cx.brow[0:1, vo * 512:(vo + 1) * 512],
                      True, False, deps=[w_f, cx.bank_war[pb]] + cx.ones_tok, inc=0)
            mt = None
            for dt in range(DT):
                mt = p.mm(cx.bank[pb][:],
                          cx.lnxt[:, dt * T + tt * 128:dt * T + tt * 128 + 128],
                          wv[:, dt * 1024 + vo * 512:dt * 1024 + vo * 512 + 512],
                          start=False, stop=(dt == DT - 1), deps=[],
                          inc=1 if dt == DT - 1 else 0)
            e = fm_evac(cx, pb, V[:, tt * 1024 + vo * 512:tt * 1024 + vo * 512 + 512], deps=[mt])
            cx.bank_war[pb] = e
            v_done.append(e)

    exf_prev = [None, None]
    st_prev = [None, None]
    qc_war = []
    attn_stores = []
    for c in range(NCH):
        q_done = []
        for ot in range(DT):
            pb = B_MM0 + (ot % 2)
            mt = None
            for dt in range(DT):
                mt = p.mm(cx.bank[pb][:], wq[:, dt * 1024 + ot * 128:dt * 1024 + ot * 128 + 128],
                          cx.lnxt[:, dt * T + c * CH:dt * T + c * CH + CH],
                          start=(dt == 0), stop=(dt == DT - 1),
                          deps=[cx.bank_war[pb]] + (qc_war + v_done[-2:] if dt == 0 else []),
                          inc=1 if dt == DT - 1 else 0)
            e = fm_evac(cx, pb, QC[:, ot * CH:(ot + 1) * CH],
                        bias_col=cx.bcols[:, 8 + ot:9 + ot], deps=[mt])
            cx.bank_war[pb] = e
            q_done.append(e)
        qc_war = []
        oc_toks = mha_core(cx, c, KT, V, QC, OC, AT, 0.125, False,
                           [kv_done[-1], q_done[-1]])
        for ot in range(DT):
            eb = ot % 2
            xo = cx.exf[:, 0:CH] if eb == 0 else cx.exf2
            lx = p.dma("sp", xo, src[ot * 128:(ot + 1) * 128, c * CH:(c + 1) * CH],
                       deps=[exf_prev[eb]])
            xsl = cx.apl[:, (ot % 2) * CH:((ot % 2) + 1) * CH]
            e_t = p.op("dve", lambda E, o=xsl, ot=ot, x=xo: E.scalar_tensor_tensor(
                o, OC[:, ot * CH:(ot + 1) * CH], 0.5, x, ALU.mult, ALU.add),
                deps=[oc_toks[2 * ot], oc_toks[2 * ot + 1], lx, st_prev[ot % 2], cx.apl_war])
            exf_prev[eb] = e_t
            cx.apl_war = e_t
            qc_war.append(e_t)
            st_prev[ot % 2] = p.dma("sp", dst[ot * 128:(ot + 1) * 128, c * CH:(c + 1) * CH],
                                    xsl, deps=[e_t])
            attn_stores.append(st_prev[ot % 2])
        cx.apply_last = (cx.apply_last or []) + [t for t in st_prev if t]

    bar = p.now()
    p.dma("pool", cx.aw.b(0, 64)[:, 0:64], hbm["wgate"], deps=bar)
    p.dma("pool", cx.brow[0:1, 0:8], hbm["bg_row"], deps=bar)
    p.dma("pool", sel[0:8, :], hbm["sel"], deps=bar)
    g_f = p.fence("pool")
    comb_done = []
    gscr = cx.rrow
    for tt in range(NTT):
        pb = B_MM0 + (tt % 2)
        bi = p.mm(cx.bank[pb][:, 0:8], cx.ones_row_f[0:1, :], cx.brow[0:1, 0:8],
                  True, False, deps=[g_f, cx.bank_war[pb]] + cx.ones_tok, inc=0)
        mt = None
        for dt in range(DT):
            mt = p.mm(cx.bank[pb][:, 0:8],
                      cx.lnxt[:, dt * T + tt * 128:dt * T + tt * 128 + 128],
                      wg[:, dt * 8:(dt + 1) * 8],
                      start=False, stop=(dt == DT - 1), deps=[],
                      inc=1 if dt == DT - 1 else 0)
        s0 = cx.ncol[:, 0:1]
        g8 = cx.mr_b[:, 0:8]
        t1 = p.op("dve", lambda E, pb=pb: E.reduce_max(s0, cx.bank[pb][:, 0:8], axis=AX.X,
                                                       negate=True), deps=[mt, cx.g8_war])
        t2 = p.op("act", lambda E, pb=pb: E.activation(g8, cx.bank[pb][:, 0:8], AF.Exp, bias=s0),
                  deps=[t1])
        cx.bank_war[pb] = t2
        t3 = p.op("dve", lambda E: E.reduce_sum(cx.ncol[:, 1:2], g8, axis=AX.X), deps=[t2])
        t4 = p.op("dve", lambda E: E.reciprocal(cx.ncol[:, 2:3], cx.ncol[:, 1:2]), deps=[t3])
        t5 = p.op("dve", lambda E: E.tensor_single_scalar(g8, g8, cx.ncol[:, 2:3], ALU.mult),
                  deps=[t4])
        t6 = p.op("dve", lambda E: E.reduce_max(cx.ncol[:, 3:4], g8, axis=AX.X), deps=[t5])
        m8 = cx.rstd_b[:, 0:8]
        t7 = p.op("dve", lambda E: E.tensor_single_scalar(m8, g8, cx.ncol[:, 3:4], ALU.is_ge),
                  deps=[t6, cx.m8_war])
        m8b = cx.rstd_b[:, 8:16]
        t8 = p.op("dve", lambda E: E.scalar_tensor_tensor(m8b, m8, -1e30, g8, ALU.mult, ALU.add),
                  deps=[t7])
        t9 = p.op("dve", lambda E: E.reduce_max(cx.ncol[:, 4:5], m8b, axis=AX.X), deps=[t8])
        t10 = p.op("dve", lambda E: E.tensor_single_scalar(m8b, m8b, cx.ncol[:, 4:5], ALU.is_ge),
                   deps=[t9])
        t11 = p.op("dve", lambda E: E.tensor_add(m8, m8, m8b), deps=[t10])
        t12 = p.op("dve", lambda E: E.tensor_mul(m8, m8, g8), deps=[t11])
        t13 = p.op("dve", lambda E: E.tensor_scalar_mul(m8, m8, 0.5), deps=[t12])
        cx.g8_war = t13
        trb = B_SC0 if tt % 2 == 0 else B_SC1
        tr = p.op("pe", lambda E, trb=trb: E.transpose(cx.bank[trb][0:8, 0:128], m8, cx.identf[:]),
                  deps=[t13, cx.bank_war[trb], w_f])
        ev = p.op("act", lambda E, trb=trb, tt=tt: E.copy(combT[0:8, tt * 128:(tt + 1) * 128],
                                                          cx.bank[trb][0:8, 0:128]), deps=[tr])
        cx.bank_war[trb] = ev
        cx.m8_war = ev
        comb_done.append(ev)

    acc_ld = []
    for dt in range(DT):
        acc_ld.append(p.dma("sp", acc[:, dt * 2048:(dt + 1) * 2048],
                            dst[dt * 128:(dt + 1) * 128, :], deps=attn_stores if dt == 0 else ()))
    we_tok = [None, None]
    for e in range(8):
        wb = e % 2
        cx.load_w(we[wb], hbm["wexp"][e], 1024,
                  deps=[g_f] + (comb_done if e < 2 else []) + ([we_tok[wb]] if we_tok[wb] else []))
        p.dma("pool", cx.bcols[:, 40:48], hbm["bexp_col"][e], deps=[cx.combB_war])
        we_f = p.fence("pool")
        mm_last = []
        for c in range(NCH):
            sb = cx.bank[B_BC2]
            bc = p.mm(sb[:, 0:CH], sel[0:8, e * 128:(e + 1) * 128], combT[0:8, c * CH:(c + 1) * CH],
                      True, True, deps=[comb_done[-1], cx.bank_war[B_BC2]])
            cbe = p.op("act", lambda E: E.copy(cx.rstd_b[:, 0:CH], sb[:, 0:CH]),
                       deps=[bc, cx.combB_war, cx.m8_war])
            cx.bank_war[B_BC2] = cbe
            for ot in range(DT):
                pb = B_MM0 + (ot % 2)
                mt = None
                for dt in range(DT):
                    mt = p.mm(cx.bank[pb][:], we[wb][:, dt * 1024 + ot * 128:dt * 1024 + ot * 128 + 128],
                              cx.lnxt[:, dt * T + c * CH:dt * T + c * CH + CH],
                              start=(dt == 0), stop=(dt == DT - 1),
                              deps=[we_f, cx.bank_war[pb]] + ([acc_ld[-1]] if e == 0 and ot == 0 and c == 0 else []),
                              inc=1 if dt == DT - 1 else 0)
                tmp = cx.apl[:, (ot % 2) * CH:((ot % 2) + 1) * CH]
                u1 = p.op("dve", lambda E, pb=pb, ot=ot, t=tmp: E.scalar_tensor_tensor(
                    t, cx.bank[pb][:], cx.bcols[:, 40 + ot:41 + ot], cx.rstd_b[:, 0:CH],
                    ALU.add, ALU.mult), deps=[mt, cbe, cx.apl_war])
                cx.bank_war[pb] = u1
                asl = acc[:, ot * 2048 + c * CH:ot * 2048 + c * CH + CH]
                u2 = p.op("dve", lambda E, a=asl, t=tmp: E.tensor_add(a, a, t), deps=[u1])
                cx.apl_war = u2
                mm_last.append(u2)
        we_tok[wb] = mm_last[-1]
        cx.combB_war = mm_last[-1]
        cx.m8_war = mm_last[-1]
    fin = []
    bar2 = p.now()
    for dt in range(DT):
        fin.append(p.dma("sp", dst[dt * 128:(dt + 1) * 128, :], acc[:, dt * 2048:(dt + 1) * 2048],
                         deps=bar2))
    return fin


def prep_l2ffn_inputs(inputs):
    bqkv = np.asarray(inputs["L2_ffn_b_qkv"], np.float32)
    sel = np.zeros((8, 8 * 128), dtype=np.float32)
    for e in range(8):
        sel[e, e * 128:(e + 1) * 128] = 1.0
    be = np.stack([col_layout(np.asarray(inputs["L2_ffn_b_experts"])[e]) for e in range(8)])
    return {
        "wqkv": np.ascontiguousarray(inputs["L2_ffn_W_qkv"]),
        "bqkv_col": np.ascontiguousarray(
            np.concatenate([col_layout(bqkv[0:1024]), col_layout(bqkv[1024:2048]),
                            col_layout(bqkv[2048:3072])], axis=1)),
        "bv_row": np.ascontiguousarray(bqkv[2048:3072].reshape(1, 1024)),
        "wgate": np.ascontiguousarray(
            np.asarray(inputs["L2_ffn_W_gate"], np.float32).reshape(8, 128, 8).transpose(1, 0, 2).reshape(128, 64)),
        "bg_row": np.asarray(inputs["L2_ffn_b_gate"], np.float32).reshape(1, 8),
        "wexp": np.ascontiguousarray(inputs["L2_ffn_W_experts"]),
        "bexp_col": be,
        "sel": sel,
        "identf": np.eye(128, dtype=np.float32),
    }


def build_l2ffn_program():
    nc = bass.Bass("TRN2", target_bir_lowering=False)
    es = ExitStack()
    hbm = _mk_io(nc, [("wqkv", [D, 3 * D], F32), ("bqkv_col", [128, 24], F32),
                      ("bv_row", [1, 1024], F32), ("wgate", [128, 64], F32),
                      ("bg_row", [1, 8], F32), ("wexp", [8, D, D], F32),
                      ("bexp_col", [8, 128, 8], F32), ("sel", [8, 8 * 128], F32),
                      ("identf", [128, 128], F32)])
    cx = Ctx(nc, es)
    emit_l2ffn(cx, hbm, hbm["xT"], hbm["yT"])
    cx.p.emit()
    es.close()
    return nc


def run_sublayer_l2ffn(x, inputs):
    return _run_sub("l2ffn", build_l2ffn_program, prep_l2ffn_inputs, x, inputs)





def emit_top2_renorm(cx, wg, bg_row, combT, g_f, identf_tok):
    p = cx.p
    comb_done = []
    for tt in range(NTT):
        pb = B_MM0 + (tt % 2)
        p.mm(cx.bank[pb][:, 0:8], cx.ones_row_f[0:1, :], bg_row,
             True, False, deps=[g_f, cx.bank_war[pb]] + cx.ones_tok, inc=0)
        mt = None
        for dt in range(DT):
            mt = p.mm(cx.bank[pb][:, 0:8],
                      cx.lnxt[:, dt * T + tt * 128:dt * T + tt * 128 + 128],
                      wg[:, dt * 8:(dt + 1) * 8],
                      start=False, stop=(dt == DT - 1), deps=[],
                      inc=1 if dt == DT - 1 else 0)
        g8 = cx.mr_b[:, 0:8]
        m8 = cx.rstd_b[:, 0:8]
        m8b = cx.rstd_b[:, 8:16]
        t0 = p.op("dve", lambda E, pb=pb: E.tensor_copy(g8, cx.bank[pb][:, 0:8]),
                  deps=[mt, cx.g8_war])
        cx.bank_war[pb] = t0
        t1 = p.op("dve", lambda E: E.reduce_max(cx.ncol[:, 0:1], g8, axis=AX.X), deps=[t0])
        t2 = p.op("dve", lambda E: E.tensor_single_scalar(m8, g8, cx.ncol[:, 0:1], ALU.is_ge),
                  deps=[t1, cx.m8_war])
        t3 = p.op("dve", lambda E: E.scalar_tensor_tensor(m8b, m8, -1e30, g8, ALU.mult, ALU.add),
                  deps=[t2])
        t4 = p.op("dve", lambda E: E.reduce_max(cx.ncol[:, 1:2], m8b, axis=AX.X), deps=[t3])
        t5 = p.op("dve", lambda E: E.tensor_single_scalar(m8b, m8b, cx.ncol[:, 1:2], ALU.is_ge),
                  deps=[t4])
        t6 = p.op("dve", lambda E: E.tensor_sub(cx.ncol[:, 2:3], cx.ncol[:, 1:2], cx.ncol[:, 0:1]),
                  deps=[t4])
        t7 = p.op("act", lambda E: E.activation(cx.ncol[:, 3:4], cx.ncol[:, 2:3], AF.Exp),
                  deps=[t6])
        t8 = p.op("dve", lambda E: E.tensor_scalar_add(cx.ncol[:, 3:4], cx.ncol[:, 3:4], 1.0),
                  deps=[t7])
        t9 = p.op("dve", lambda E: E.reciprocal(cx.ncol[:, 2:3], cx.ncol[:, 3:4]), deps=[t8])
        t10 = p.op("dve", lambda E: E.tensor_scalar_mul(cx.ncol[:, 2:3], cx.ncol[:, 2:3], 0.5),
                   deps=[t9])
        t11 = p.op("dve", lambda E: E.tensor_scalar(cx.ncol[:, 3:4], cx.ncol[:, 2:3], -1.0, 0.5,
                                                    ALU.mult, ALU.add), deps=[t10])
        t12 = p.op("dve", lambda E: E.tensor_single_scalar(m8, m8, cx.ncol[:, 2:3], ALU.mult),
                   deps=[t10, t5])
        t13 = p.op("dve", lambda E: E.tensor_single_scalar(m8b, m8b, cx.ncol[:, 3:4], ALU.mult),
                   deps=[t11])
        t14 = p.op("dve", lambda E: E.tensor_add(m8, m8, m8b), deps=[t12, t13])
        cx.g8_war = t14
        trb = B_SC0 if tt % 2 == 0 else B_SC1
        tr = p.op("pe", lambda E, trb=trb: E.transpose(cx.bank[trb][0:8, 0:128], m8, cx.identf[:]),
                  deps=[t14, cx.bank_war[trb], identf_tok])
        ev = p.op("act", lambda E, trb=trb, tt=tt: E.copy(combT[0:8, tt * 128:(tt + 1) * 128],
                                                          cx.bank[trb][0:8, 0:128]), deps=[tr])
        cx.bank_war[trb] = ev
        cx.m8_war = ev
        comb_done.append(ev)
    return comb_done



def emit_l1ffn(cx, hbm, src, dst):
    p, nc = cx.p, cx.nc
    acc = cx.aa.f(0, 16 * 1024)
    H1 = cx.aa.b(65536, DT * CH)
    combT = cx.aa.f(73728, 2 * T)
    sel = cx.aa.f(81920, 1024)
    wgv = cx.aa.b(86016, DT * 8)
    w1b = [cx.aw.b(0, DT * 1024), cx.aw.b(16384, DT * 1024), cx.aw.b(32768, DT * 1024)]
    cx.g8_war = None
    cx.m8_war = None
    cx.combB_war = None
    cx.apl_war = None
    cx.b1_war = None
    cx.h1_war = [None] * DT

    lastln = []
    for c in range(NCH):
        lastln = emit_ln_chunk(cx, src, c)
    ln_all = list(lastln) + list(cx.ln_last_use or [])
    cx.apl_war = lastln[-1]

    p.dma("pool", wgv[:, 0:64], hbm["wg"], deps=ln_all)
    p.dma("pool", cx.brow[0:1, 0:8], hbm["bg_row"])
    p.dma("pool", sel[0:8, :], hbm["sel"])
    p.dma("pool", cx.identf[:], hbm["identf"])
    g_f = p.fence("pool")
    comb_done = emit_top2_renorm(cx, wgv, cx.brow[0:1, 0:8], combT, g_f, g_f)

    acc_ld = []
    for dt in range(DT):
        acc_ld.append(p.dma("sp", acc[:, dt * 2048:(dt + 1) * 2048],
                            src[dt * 128:(dt + 1) * 128, :],
                            deps=(cx.apply_last or []) if dt == 0 else ()))
    iters = [(e, h) for e in range(8) for h in range(2)]
    s_w1, s_w2, s_sp = 0, 1, 2
    cx.load_w(w1b[s_w1], hbm["w1T"][0][:, 0:1024], 1024, deps=[g_f])
    cx.load_w(w1b[s_w2], hbm["w2T"][0][0:1024, :], 1024)
    p.dma("pool", cx.bcols[:, 8:24], hbm["b1col"][0])
    p.dma("pool", cx.bcols[:, 24:32], hbm["b2col"][0])
    we_f = p.fence("pool")
    spare_free = g_f
    w2slot_free = None
    for it, (e, half) in enumerate(iters):
        if True:
            w1s = w1b[s_w1]
            w2s = w1b[s_w2]
            mm_last = None
            mm1_last = None
            for c in range(NCH):
                sb = cx.bank[B_BC2]
                bc = p.mm(sb[:, 0:CH], sel[0:8, e * 128:(e + 1) * 128],
                          combT[0:8, c * CH:(c + 1) * CH], True, True,
                          deps=[comb_done[-1], cx.bank_war[B_BC2]])
                cbe = p.op("act", lambda E: E.copy(cx.rstd_b[:, 0:CH], sb[:, 0:CH]),
                           deps=[bc, cx.combB_war, cx.m8_war])
                cx.bank_war[B_BC2] = cbe
                cbe_t = cbe
                h1_done = []
                for ht in range(DT):
                    pb = B_SC0 + (ht % 2)
                    mt = None
                    for dt in range(DT):
                        mt = p.mm(cx.bank[pb][:],
                                  w1s[:, dt * 1024 + ht * 128:dt * 1024 + ht * 128 + 128],
                                  cx.lnxt[:, dt * T + c * CH:dt * T + c * CH + CH],
                                  start=(dt == 0), stop=(dt == DT - 1),
                                  deps=[we_f, cx.bank_war[pb]],
                                  inc=1 if dt == DT - 1 else 0)
                    he = p.op("act", lambda E, pb=pb, ht=ht: E.activation(
                        H1[:, ht * CH:(ht + 1) * CH], cx.bank[pb][:], AF.Relu,
                        bias=cx.bcols[:, 8 + half * 8 + ht:9 + half * 8 + ht]),
                        deps=[mt, cx.h1_war[ht]])
                    cx.bank_war[pb] = he
                    h1_done.append(he)
                for ot in range(DT):
                    pb = B_MM0 + (ot % 2)
                    mt = None
                    for ht in range(DT):
                        mt = p.mm(cx.bank[pb][:],
                                  w2s[:, ht * 1024 + ot * 128:ht * 1024 + ot * 128 + 128],
                                  H1[:, ht * CH:(ht + 1) * CH],
                                  start=(ht == 0), stop=(ht == DT - 1),
                                  deps=[cx.bank_war[pb]] + ([h1_done[-1]] if ht == 0 else []),
                                  inc=1 if ht == DT - 1 else 0)
                    tmp = cx.apl[:, (ot % 2) * CH:((ot % 2) + 1) * CH]
                    if half == 0:
                        u1 = p.op("dve", lambda E, pb=pb, ot=ot, t=tmp: E.scalar_tensor_tensor(
                            t, cx.bank[pb][:], cx.bcols[:, 24 + ot:25 + ot], cx.rstd_b[:, 0:CH],
                            ALU.add, ALU.mult), deps=[mt, cbe_t, cx.apl_war])
                    else:
                        u1 = p.op("dve", lambda E, pb=pb, t=tmp: E.tensor_mul(
                            t, cx.bank[pb][:], cx.rstd_b[:, 0:CH]), deps=[mt, cbe_t, cx.apl_war])
                    cx.bank_war[pb] = u1
                    asl = acc[:, ot * 2048 + c * CH:ot * 2048 + c * CH + CH]
                    u2 = p.op("dve", lambda E, a=asl, t=tmp: E.tensor_add(a, a, t),
                              deps=[u1] + ([acc_ld[-1]] if e == 0 and half == 0 and c == 0 and ot == 0 else []))
                    cx.apl_war = u2
                    mm_last = u2
                cx.combB_war = mm_last
                cx.h1_war = [cx.bank_war[B_MM0]] * DT
                if c == NCH - 1:
                    mm1_last = h1_done[-1]
            if it + 1 < len(iters):
                ne, nh = iters[it + 1]
                cx.load_w(w1b[s_sp], hbm["w1T"][ne][:, nh * 1024:(nh + 1) * 1024], 1024,
                          deps=[spare_free])
                cx.load_w(w1b[s_w1], hbm["w2T"][ne][nh * 1024:(nh + 1) * 1024, :], 1024,
                          deps=[mm1_last])
                p.dma("pool", cx.bcols[:, 8:24], hbm["b1col"][ne], deps=[mm_last])
                p.dma("pool", cx.bcols[:, 24:32], hbm["b2col"][ne])
                we_f = p.fence("pool")
            spare_free = mm_last
            cx.b1_war = mm_last
            s_w1, s_w2, s_sp = s_sp, s_w1, s_w2
    fin = []
    bar2 = p.now()
    for dt in range(DT):
        fin.append(p.dma("sp", dst[dt * 128:(dt + 1) * 128, :], acc[:, dt * 2048:(dt + 1) * 2048],
                         deps=bar2))
    return fin


def prep_l1ffn_inputs(inputs):
    sel = np.zeros((8, 8 * 128), dtype=np.float32)
    for e in range(8):
        sel[e, e * 128:(e + 1) * 128] = 1.0
    W1 = np.asarray(inputs["L1_ffn_W1"])
    W2 = np.asarray(inputs["L1_ffn_W2"])
    b1 = np.asarray(inputs["L1_ffn_b1"])
    b2 = np.asarray(inputs["L1_ffn_b2"])
    return {
        "w1T": np.ascontiguousarray(W1.transpose(0, 2, 1)),
        "w2T": np.ascontiguousarray(W2.transpose(0, 2, 1)),
        "b1col": np.stack([col_layout(b1[e]) for e in range(8)]),
        "b2col": np.stack([col_layout(b2[e]) for e in range(8)]),
        "wg": np.ascontiguousarray(
            np.asarray(inputs["L1_ffn_Wg"], np.float32).T.reshape(8, 128, 8).transpose(1, 0, 2).reshape(128, 64)),
        "bg_row": np.asarray(inputs["L1_ffn_bg"], np.float32).reshape(1, 8),
        "sel": sel,
        "identf": np.eye(128, dtype=np.float32),
    }


def build_l1ffn_program():
    nc = bass.Bass("TRN2", target_bir_lowering=False)
    es = ExitStack()
    hbm = _mk_io(nc, [("w1T", [8, D, 2048], F32), ("w2T", [8, 2048, D], F32),
                      ("b1col", [8, 128, 16], F32), ("b2col", [8, 128, 8], F32),
                      ("wg", [128, 64], F32), ("bg_row", [1, 8], F32),
                      ("sel", [8, 8 * 128], F32), ("identf", [128, 128], F32)])
    cx = Ctx(nc, es)
    emit_l1ffn(cx, hbm, hbm["xT"], hbm["yT"])
    cx.p.emit()
    es.close()
    return nc


def run_sublayer_l1ffn(x, inputs):
    return _run_sub("l1ffn", build_l1ffn_program, prep_l1ffn_inputs, x, inputs)





def emit_l3ffn(cx, hbm, src, dst):
    p, nc = cx.p, cx.nc
    acc = cx.aa.f(0, 16 * 1024)
    G = cx.aa.b(65536, DT * CH)
    utmp = cx.aa.b(73728, 2 * CH)
    combT = cx.aa.f(75776, T)
    sel = cx.aa.f(83968, 1024)
    wgv = cx.aa.b(88064, DT * 8)
    w1s = cx.aw.b(0, DT * 1024)
    w3s = cx.aw.b(16384, DT * 1024)
    w2s = cx.aw.b(32768, DT * 1024)
    cx.g8_war = None
    cx.m8_war = None
    cx.combB_war = None
    cx.apl_war = None
    cx.g_war = [None] * DT
    cx.ut_war = [None, None]

    lastln = []
    for c in range(NCH):
        lastln = emit_ln_chunk(cx, src, c)
    ln_all = list(lastln) + list(cx.ln_last_use or [])
    cx.apl_war = lastln[-1]

    p.dma("pool", wgv[:, 0:64], hbm["wg"], deps=ln_all)
    p.dma("pool", cx.brow[0:1, 0:8], hbm["bg_row"])
    p.dma("pool", sel[0:8, :], hbm["sel"])
    p.dma("pool", cx.identf[:], hbm["identf"])
    g_f = p.fence("pool")
    comb_done = emit_top2_renorm(cx, wgv, cx.brow[0:1, 0:8], combT, g_f, g_f)

    acc_ld = []
    for dt in range(DT):
        acc_ld.append(p.dma("sp", acc[:, dt * 2048:(dt + 1) * 2048],
                            src[dt * 128:(dt + 1) * 128, :],
                            deps=(cx.apply_last or []) if dt == 0 else ()))
    QH = 512
    NQ = 2048 // QH
    slots = [{"w1": cx.aw.b(0, DT * QH), "w3": cx.aw.b(8192, DT * QH),
              "w2": cx.aw.b(16384, 4 * 1024)},
             {"w1": cx.aw.b(24576, DT * QH), "w3": cx.aw.b(32768, DT * QH),
              "w2": cx.aw.b(40960, 4 * 1024)}]
    iters = [(e, q) for e in range(9) for q in range(NQ)]

    def wsrc(e, name):
        if e == 8:
            return hbm["s" + name][0]
        return hbm[name][e]

    def load_iter(it, deps):
        e, q = iters[it]
        sl = slots[it % 2]
        cx.load_w(sl["w1"], wsrc(e, "w1")[:, q * QH:(q + 1) * QH], QH, deps=deps)
        cx.load_w(sl["w3"], wsrc(e, "w3")[:, q * QH:(q + 1) * QH], QH)
        cx.load_w(sl["w2"], wsrc(e, "w2")[q * QH:(q + 1) * QH, :], 1024)
        return p.fence("pool")

    slot_free = [g_f, g_f]
    we_f = load_iter(0, [g_f])
    for it, (e, q) in enumerate(iters):
        shared = (e == 8)
        sl = slots[it % 2]
        w1s, w3s, w2s = sl["w1"], sl["w3"], sl["w2"]
        mm_last = None
        for c in range(NCH):
            if not shared:
                sb = cx.bank[B_BC2]
                bc = p.mm(sb[:, 0:CH], sel[0:8, e * 128:(e + 1) * 128],
                          combT[0:8, c * CH:(c + 1) * CH], True, True,
                          deps=[comb_done[-1], cx.bank_war[B_BC2]])
                cbe = p.op("act", lambda E: E.copy(cx.rstd_b[:, 0:CH], sb[:, 0:CH]),
                           deps=[bc, cx.combB_war, cx.m8_war])
                cx.bank_war[B_BC2] = cbe
            else:
                cbe = None
            eo_done = []
            for ht in range(4):
                pb = B_SC0 + (ht % 2)
                mt = None
                for dt in range(DT):
                    mt = p.mm(cx.bank[pb][:],
                              w1s[:, dt * QH + ht * 128:dt * QH + ht * 128 + 128],
                              cx.lnxt[:, dt * T + c * CH:dt * T + c * CH + CH],
                              start=(dt == 0), stop=(dt == DT - 1),
                              deps=[we_f, cx.bank_war[pb]],
                              inc=1 if dt == DT - 1 else 0)
                ge = p.op("act", lambda E, pb=pb, ht=ht: E.activation(
                    G[:, ht * CH:(ht + 1) * CH], cx.bank[pb][:], AF.Silu),
                    deps=[mt, cx.g_war[ht]])
                cx.bank_war[pb] = ge
                mt2 = None
                for dt in range(DT):
                    mt2 = p.mm(cx.bank[pb][:],
                               w3s[:, dt * QH + ht * 128:dt * QH + ht * 128 + 128],
                               cx.lnxt[:, dt * T + c * CH:dt * T + c * CH + CH],
                               start=(dt == 0), stop=(dt == DT - 1),
                               deps=[ge] if dt == 0 else [],
                               inc=1 if dt == DT - 1 else 0)
                ue = p.op("act", lambda E, pb=pb, ht=ht: E.copy(
                    utmp[:, (ht % 2) * CH:((ht % 2) + 1) * CH], cx.bank[pb][:]),
                    deps=[mt2, cx.ut_war[ht % 2]])
                cx.bank_war[pb] = ue
                me = p.op("dve", lambda E, ht=ht: E.tensor_mul(
                    G[:, ht * CH:(ht + 1) * CH], G[:, ht * CH:(ht + 1) * CH],
                    utmp[:, (ht % 2) * CH:((ht % 2) + 1) * CH]), deps=[ue, ge])
                cx.ut_war[ht % 2] = me
                eo_done.append(me)
            for ot in range(DT):
                pb = B_MM0 + (ot % 2)
                mt = None
                for ht in range(4):
                    mt = p.mm(cx.bank[pb][:],
                              w2s[:, ht * 1024 + ot * 128:ht * 1024 + ot * 128 + 128],
                              G[:, ht * CH:(ht + 1) * CH],
                              start=(ht == 0), stop=(ht == 3),
                              deps=[cx.bank_war[pb]] + ([eo_done[-1]] if ht == 0 else []),
                              inc=1 if ht == 3 else 0)
                asl = acc[:, ot * 2048 + c * CH:ot * 2048 + c * CH + CH]
                if shared:
                    u2 = p.op("dve", lambda E, pb=pb, a=asl: E.scalar_tensor_tensor(
                        a, cx.bank[pb][:], 0.5, a, ALU.mult, ALU.add),
                        deps=[mt, cx.apl_war] + ([acc_ld[-1]] if it == 0 else []))
                    cx.bank_war[pb] = u2
                else:
                    tmp = cx.apl[:, (ot % 2) * CH:((ot % 2) + 1) * CH]
                    u1 = p.op("dve", lambda E, pb=pb, t=tmp: E.tensor_mul(
                        t, cx.bank[pb][:], cx.rstd_b[:, 0:CH]), deps=[mt, cbe, cx.apl_war])
                    cx.bank_war[pb] = u1
                    u2 = p.op("dve", lambda E, a=asl, t=tmp: E.tensor_add(a, a, t),
                              deps=[u1] + ([acc_ld[-1]] if it == 0 and c == 0 and ot == 0 else []))
                    cx.apl_war = u2
                mm_last = u2
            cx.combB_war = mm_last
            cx.g_war = [cx.bank_war[B_MM0]] * DT
        slot_free[it % 2] = mm_last
        if it + 1 < len(iters):
            we_f = load_iter(it + 1, [slot_free[(it + 1) % 2]])
    fin = []
    bar2 = p.now()
    for dt in range(DT):
        fin.append(p.dma("sp", dst[dt * 128:(dt + 1) * 128, :], acc[:, dt * 2048:(dt + 1) * 2048],
                         deps=bar2))
    return fin


def prep_l3ffn_inputs(inputs):
    sel = np.zeros((8, 8 * 128), dtype=np.float32)
    for e in range(8):
        sel[e, e * 128:(e + 1) * 128] = 1.0
    return {
        "w1": np.ascontiguousarray(inputs["L3_ffn_w1"]),
        "w3": np.ascontiguousarray(inputs["L3_ffn_w3"]),
        "w2": np.ascontiguousarray(inputs["L3_ffn_w2"]),
        "sw1": np.ascontiguousarray(inputs["L3_ffn_sw1"]),
        "sw3": np.ascontiguousarray(inputs["L3_ffn_sw3"]),
        "sw2": np.ascontiguousarray(inputs["L3_ffn_sw2"]),
        "wg": np.ascontiguousarray(
            np.asarray(inputs["L3_ffn_router_w"], np.float32).T.reshape(8, 128, 8).transpose(1, 0, 2).reshape(128, 64)),
        "bg_row": np.asarray(inputs["L3_ffn_experts_bias"], np.float32).reshape(1, 8),
        "sel": sel,
        "identf": np.eye(128, dtype=np.float32),
    }


def build_l3ffn_program():
    nc = bass.Bass("TRN2", target_bir_lowering=False)
    es = ExitStack()
    hbm = _mk_io(nc, [("w1", [8, D, 2048], F32), ("w3", [8, D, 2048], F32),
                      ("w2", [8, 2048, D], F32), ("sw1", [1, D, 2048], F32),
                      ("sw3", [1, D, 2048], F32), ("sw2", [1, 2048, D], F32),
                      ("wg", [128, 64], F32), ("bg_row", [1, 8], F32),
                      ("sel", [8, 8 * 128], F32), ("identf", [128, 128], F32)])
    cx = Ctx(nc, es)
    emit_l3ffn(cx, hbm, hbm["xT"], hbm["yT"])
    cx.p.emit()
    es.close()
    return nc


def run_sublayer_l3ffn(x, inputs):
    return _run_sub("l3ffn", build_l3ffn_program, prep_l3ffn_inputs, x, inputs)





def emit_l3attn(cx, hbm, src, dst, stop_after=None):
    p, nc = cx.p, cx.nc
    NKV = 4
    KT = cx.aa.b(0, NKV * T)
    V = cx.aa.b(16384, NTT * 512)
    QC = cx.aa.b(32768, DT * CH)
    OC = cx.aa.b(40960, DT * CH)
    AT = cx.aa.b(49152, 2 * CH)
    cosb = cx.aa.f(51200, CH)
    sinb = cx.aa.f(53248, CH)
    wq3 = cx.aa.b(55296, CH)
    hsqp = cx.aa.b(56320, CH)
    maskW = cx.aa.f(57344, 128)
    dirsrow = cx.aa.b(57856, 32)
    dirscol = cx.aa.b(57920, 8)
    wqp = cx.aw.b(0, DT * 1024)
    wkp = cx.aw.b(16384, DT * 512)
    wvp = cx.aw.b(24576, DT * 512)
    wop = cx.aw.b(32768, DT * 1024)
    SCALE = 1.0 / float(np.sqrt(128.0))
    cx.apl_war = None
    cx.hsqp_war = None
    cx.wq3_war = None

    lastln = []
    for c in range(NCH):
        lastln = emit_ln_chunk(cx, src, c)
    ln_all = list(lastln) + list(cx.ln_last_use or [])
    cx.apl_war = lastln[-1]

    cx.load_w(wqp, hbm["wq"], 1024)
    cx.load_w(wkp, hbm["wk"], 512)
    cx.load_w(wvp, hbm["wv"], 512)
    cx.load_w(wop, hbm["wo"], 1024)
    p.dma("pool", cx.p2t[:], hbm["p3t"])
    p.dma("pool", cx.identf[:], hbm["identf"])
    p.dma("pool", cx.m128[:, 0:128], hbm["maskC"])
    p.dma("pool", maskW, hbm["maskW"])
    p.dma("pool", dirsrow[0:1, 0:24], hbm["dirsrow"])
    p.dma("pool", dirscol[0:3, 0:8], hbm["dirscol"])
    p.dma("pool", cx.srow[0:1, 20:28], hbm["hs_row"])
    w_f = p.fence("pool")
    hs2 = p.op("dve", lambda E: E.tensor_scalar_mul(cx.srow[0:1, 28:36], cx.srow[0:1, 20:28],
                                                    float(np.sqrt(128.0))), deps=[w_f])

    kv_done = []
    cs_tok = None
    for c in range(NCH):
        p.dma("pool", cosb, hbm["cosT"][:, c * CH:(c + 1) * CH], deps=[cs_tok, cx.apl_war])
        p.dma("pool", sinb, hbm["sinT"][:, c * CH:(c + 1) * CH])
        cs_f = p.fence("pool")
        for kh in range(NKV):
            pb = B_MM0 + (kh % 2)
            mt = None
            for dt in range(DT):
                mt = p.mm(cx.bank[pb][:], wkp[:, dt * 512 + kh * 128:dt * 512 + kh * 128 + 128],
                          cx.lnxt[:, dt * T + c * CH:dt * T + c * CH + CH],
                          start=(dt == 0), stop=(dt == DT - 1),
                          deps=[w_f, cx.bank_war[pb]] + (ln_all if c == 0 and kh == 0 else []),
                          inc=1 if dt == DT - 1 else 0)
            sl = KT[:, kh * T + c * CH:kh * T + c * CH + CH]
            e = fm_evac(cx, pb, sl, deps=[mt])
            cx.bank_war[pb] = e
            rt = _rope_tile(cx, sl, B_SC0 if kh % 2 == 0 else B_SC1, cosb, sinb, [e, cs_f])
            kv_done.append(rt)
        cs_tok = kv_done[-1]
    v_done = []
    for tt in range(NTT):
        pb = B_MM0 + (tt % 2)
        mt = None
        for dt in range(DT):
            mt = p.mm(cx.bank[pb][:],
                      cx.lnxt[:, dt * T + tt * 128:dt * T + tt * 128 + 128],
                      wvp[:, dt * 512:(dt + 1) * 512],
                      start=(dt == 0), stop=(dt == DT - 1),
                      deps=[cx.bank_war[pb]] if dt == 0 else [],
                      inc=1 if dt == DT - 1 else 0)
        e = fm_evac(cx, pb, V[:, tt * 512:(tt + 1) * 512], deps=[mt])
        cx.bank_war[pb] = e
        v_done.append(e)

    if stop_after == "B":
        bar = p.now()
        for dtl in range(DT):
            p.dma("sp", dst[dtl * 128:(dtl + 1) * 128, :], src[dtl * 128:(dtl + 1) * 128, :],
                  deps=bar)
        return []
    exf_prev = [None, None]
    st_prev = [None, None]
    qc_war = []
    at_war = [None, None]
    ncol_war = None
    rrow_war = None
    rstdb_war = None
    for c in range(1 if stop_after in ("one", "C1", "C2") else (2 if stop_after == "two" else NCH)):
        p.dma("pool", cosb, hbm["cosT"][:, c * CH:(c + 1) * CH], deps=[cs_tok, cx.apl_war])
        p.dma("pool", sinb, hbm["sinT"][:, c * CH:(c + 1) * CH])
        cs_f = p.fence("pool")
        q_done = []
        for ot in range(DT):
            pb = B_MM0 + (ot % 2)
            mt = None
            for dt in range(DT):
                mt = p.mm(cx.bank[pb][:], wqp[:, dt * 1024 + ot * 128:dt * 1024 + ot * 128 + 128],
                          cx.lnxt[:, dt * T + c * CH:dt * T + c * CH + CH],
                          start=(dt == 0), stop=(dt == DT - 1),
                          deps=[cx.bank_war[pb]] + (qc_war + v_done[-2:] if dt == 0 else []),
                          inc=1 if dt == DT - 1 else 0)
            sl = QC[:, ot * CH:(ot + 1) * CH]
            e = fm_evac(cx, pb, sl, deps=[mt])
            cx.bank_war[pb] = e
            rt = _rope_tile(cx, sl, B_SC0 if ot % 2 == 0 else B_SC1, cosb, sinb, [e, cs_f])
            q_done.append(rt)
        qc_war = []
        cs_tok = q_done[-1]
        if stop_after == "C1":
            break
        tk_lo = max(0, 4 * c - 4)
        tk_hi = 4 * (c + 1)
        tks = list(range(tk_lo, tk_hi))
        oc_toks = []
        for h in range(8):
            kh = h // 2
            qp = p.mm(cx.bank[B_ST1][0:1, :], dirscol[0:3, h:h + 1], QC[0:3, h * CH:(h + 1) * CH],
                      True, True, deps=[q_done[-1], cx.bank_war[B_ST1], w_f])
            hq = p.op("act", lambda E, h=h: E.mul(hsqp[0:1, 0:CH], cx.bank[B_ST1][0:1, :],
                                                  cx.srow[0:1, 28 + h:29 + h]),
                      deps=[qp, hs2, cx.hsqp_war])
            cx.bank_war[B_ST1] = hq
            w3m = p.mm(cx.bank[B_ST2][0:3, :], dirsrow[0:1, 3 * h:3 * h + 3], hsqp[0:1, 0:CH],
                       True, True, deps=[hq, cx.bank_war[B_ST2]])
            w3e = p.op("act", lambda E: E.copy(wq3[0:3, 0:CH], cx.bank[B_ST2][0:3, :]),
                       deps=[w3m, cx.wq3_war])
            cx.bank_war[B_ST2] = w3e

            def score_mm(tk, pb, extra_deps):
                m1 = p.mm(cx.bank[pb][:], KT[:, kh * T + tk * 128:kh * T + tk * 128 + 128],
                          QC[:, h * CH:(h + 1) * CH], True, False,
                          deps=extra_deps + [cx.bank_war[pb]], inc=0)
                return p.mm(cx.bank[pb][:], KT[0:3, kh * T + tk * 128:kh * T + tk * 128 + 128],
                            wq3[0:3, 0:CH], False, True, deps=[w3e])

            mx_t = []
            for i, tk in enumerate(tks):
                pb = B_SC0 + (i % 2)
                mt = score_mm(tk, pb, [kv_done[-1]] if i < 2 else [])
                rd = p.op("dve", lambda E, pb=pb, i=i: E.reduce_max(
                    cx.m128[:, 128 + i:129 + i], cx.bank[pb][:], axis=AX.X), deps=[mt, ncol_war])
                cx.bank_war[pb] = rd
                mx_t.append(rd)
            mxc = p.op("dve", lambda E: E.reduce_max(cx.ncol[:, 5:6],
                                                     cx.m128[:, 128:128 + len(tks)], axis=AX.X),
                       deps=[mx_t[-1]])
            tr = p.op("pe", lambda E: E.transpose(cx.bank[B_BC2][0:1, 0:128], cx.ncol[:, 5:6],
                                                  cx.identf[:]), deps=[mxc, cx.bank_war[B_BC2]])
            gmx = p.op("dve", lambda E: E.reduce_max(cx.srow[0:1, 16:17],
                                                     cx.bank[B_BC2][0:1, 0:128], axis=AX.X), deps=[tr])
            cx.bank_war[B_BC2] = gmx
            gm2 = p.op("dve", lambda E: E.tensor_scalar_max(cx.srow[0:1, 16:17], cx.srow[0:1, 16:17],
                                                            0.0), deps=[gmx])
            ngm = p.op("dve", lambda E: E.tensor_scalar_mul(cx.srow[0:1, 17:18], cx.srow[0:1, 16:17],
                                                            -SCALE), deps=[gm2])
            bcm = p.mm(cx.bank[B_BC2][:, 0:1], cx.ones_row_f[0:1, :], cx.srow[0:1, 17:18], True, True,
                       deps=[ngm])
            nmc = p.op("dve", lambda E: E.tensor_copy(cx.ncol[:, 6:7], cx.bank[B_BC2][:, 0:1]),
                       deps=[bcm, ncol_war])
            cx.bank_war[B_BC2] = nmc
            ncol_war = nmc
            ex_last = None
            av_t = rs_t = None
            for i, tk in enumerate(tks):
                pb = B_SC0 + (i % 2)
                mt = score_mm(tk, pb, [nmc])
                d0 = tk - 4 * c
                t = mt
                if d0 > 0:
                    t = p.op("dve", lambda E, pb=pb, n=d0: E.tensor_scalar_add(
                        cx.bank[pb][:, 0:n * 128], cx.bank[pb][:, 0:n * 128], -1e6), deps=[t])
                if 0 <= d0 <= 3:
                    t = p.op("dve", lambda E, pb=pb, q0=d0 * 128: E.tensor_add(
                        cx.bank[pb][:, q0:q0 + 128], cx.bank[pb][:, q0:q0 + 128],
                        cx.m128[:, 0:128]), deps=[t])
                if 0 <= d0 + 4 <= 3:
                    t = p.op("dve", lambda E, pb=pb, q0=(d0 + 4) * 128: E.tensor_add(
                        cx.bank[pb][:, q0:q0 + 128], cx.bank[pb][:, q0:q0 + 128], maskW), deps=[t])
                if d0 + 5 <= 3:
                    t = p.op("dve", lambda E, pb=pb, q0=(d0 + 5) * 128: E.tensor_scalar_add(
                        cx.bank[pb][:, q0:q0 + 512 - q0], cx.bank[pb][:, q0:512], -1e6), deps=[t])
                sE = AT[:, (i % 2) * CH:((i % 2) + 1) * CH]
                ex = p.op("act", lambda E, pb=pb, s=sE: E.activation(
                    s, cx.bank[pb][:], AF.Exp, bias=cx.ncol[:, 6:7], scale=SCALE),
                    deps=[t, at_war[i % 2]])
                cx.bank_war[pb] = ex
                ex_last = ex
                first, last = (i == 0), (i == len(tks) - 1)
                av_t = p.mm(cx.bank[B_MM0][:], V[:, tk * 512 + kh * 128:tk * 512 + kh * 128 + 128],
                            sE, first, last, deps=[ex] + ([cx.bank_war[B_MM0]] if first else []),
                            inc=1 if last else 0)
                rs_t = p.mm(cx.bank[B_ST1][0:1, :], cx.ones_col_b[:, 0:1], sE, first, last,
                            deps=[ex] + ([cx.bank_war[B_ST1]] if first else []),
                            inc=1 if last else 0)
                at_war[i % 2] = av_t
            rc = p.op("dve", lambda E: E.reciprocal(cx.rrow[0:1, 0:CH], cx.bank[B_ST1][0:1, :]),
                      deps=[rs_t, rrow_war])
            cx.bank_war[B_ST1] = rc
            b1 = p.mm(cx.bank[B_BC1][:, 0:CH], cx.ones_row_f[0:1, :], cx.rrow[0:1, 0:CH],
                      True, True, deps=[rc, cx.bank_war[B_BC1]])
            rrow_war = b1
            rbt = p.op("act", lambda E: E.copy(cx.rstd_b[:], cx.bank[B_BC1][:, 0:CH]),
                       deps=[b1, rstdb_war])
            cx.bank_war[B_BC1] = rbt
            oe = p.op("dve", lambda E, h=h: E.tensor_mul(
                OC[:, h * CH:(h + 1) * CH], cx.bank[B_MM0][:], cx.rstd_b[:]),
                deps=[av_t, rbt])
            cx.bank_war[B_MM0] = oe
            rstdb_war = oe
            oc_toks.append(oe)
            cx.hsqp_war = oe
            cx.wq3_war = oe
        if stop_after in ("C2", "C2all"):
            continue
        for ot in range(DT):
            pb = B_MM0 + (ot % 2)
            mt = None
            for dt in range(DT):
                mt = p.mm(cx.bank[pb][:], wop[:, dt * 1024 + ot * 128:dt * 1024 + ot * 128 + 128],
                          OC[:, dt * CH:(dt + 1) * CH],
                          start=(dt == 0), stop=(dt == DT - 1),
                          deps=[cx.bank_war[pb]] + (oc_toks if dt == 0 else []),
                          inc=1 if dt == DT - 1 else 0)
            eb = ot % 2
            xo = cx.exf[:, 0:CH] if eb == 0 else cx.exf2
            lx = p.dma("sp", xo, src[ot * 128:(ot + 1) * 128, c * CH:(c + 1) * CH],
                       deps=[exf_prev[eb]])
            xsl = cx.apl[:, eb * CH:(eb + 1) * CH]
            e_t = p.op("dve", lambda E, o=xsl, pb=pb, x=xo: E.tensor_add(o, cx.bank[pb][:], x),
                       deps=[mt, lx, st_prev[eb], cx.apl_war])
            cx.bank_war[pb] = e_t
            exf_prev[eb] = e_t
            qc_war.append(e_t)
            st_prev[eb] = p.dma("sp", dst[ot * 128:(ot + 1) * 128, c * CH:(c + 1) * CH],
                                xsl, deps=[e_t])
            cx.apl_war = st_prev[eb]
        cx.apply_last = (cx.apply_last or []) + [t for t in st_prev if t]
    if stop_after in ("one", "C1", "C2", "two", "C2all"):
        bar = p.now()
        for dtl in range(DT):
            p.dma("sp", dst[dtl * 128:(dtl + 1) * 128, :], src[dtl * 128:(dtl + 1) * 128, :],
                  deps=bar)
        return []
    return [t for t in st_prev if t]


def prep_l3attn_inputs(inputs):
    import itertools
    hd = 128
    inv = 1.0 / (10000.0 ** (np.arange(0, hd, 2, dtype=np.float32) / hd))
    t = np.arange(T, dtype=np.float32)
    freqs = np.concatenate([np.outer(inv, t), np.outer(inv, t)], axis=0)
    cosT = np.cos(freqs).astype(np.float32)
    sinT = np.sin(freqs).astype(np.float32)
    P = np.zeros((128, 128), dtype=np.float32)
    for cp in range(64):
        P[cp, cp + 64] = -1.0
    for cp in range(64, 128):
        P[cp, cp - 64] = 1.0
    i_ = np.arange(128, dtype=np.float32)
    diffT = i_[None, :] - i_[:, None]
    maskC = np.where(diffT >= 0, 0.0, -1e6).astype(np.float32)
    maskW = np.where(diffT < 0, 0.0, -1e6).astype(np.float32)
    tg = np.array(list(itertools.product([-1.0, 1.0], repeat=3)), dtype=np.float32)
    tg = tg / np.linalg.norm(tg, axis=1, keepdims=True)
    dirs = tg[:8]
    return {
        "wq": np.ascontiguousarray(inputs["L3_attn_Wq"]),
        "wk": np.ascontiguousarray(inputs["L3_attn_Wk"]),
        "wv": np.ascontiguousarray(inputs["L3_attn_Wv"]),
        "wo": np.ascontiguousarray(inputs["L3_attn_Wo"]),
        "p3t": np.ascontiguousarray(P.T),
        "cosT": cosT, "sinT": sinT,
        "maskC": maskC, "maskW": maskW,
        "dirsrow": np.ascontiguousarray(dirs.reshape(1, 24)),
        "dirscol": np.ascontiguousarray(dirs.T),
        "hs_row": np.asarray(inputs["L3_attn_head_scales"], np.float32).reshape(1, 8),
        "identf": np.eye(128, dtype=np.float32),
    }


def build_l3attn_program(stop_after=None):
    nc = bass.Bass("TRN2", target_bir_lowering=False)
    es = ExitStack()
    hbm = _mk_io(nc, [("wq", [D, D], F32), ("wk", [D, 512], F32), ("wv", [D, 512], F32),
                      ("wo", [D, D], F32), ("p3t", [128, 128], F32),
                      ("cosT", [128, T], F32), ("sinT", [128, T], F32),
                      ("maskC", [128, 128], F32), ("maskW", [128, 128], F32),
                      ("dirsrow", [1, 24], F32), ("dirscol", [3, 8], F32),
                      ("hs_row", [1, 8], F32), ("identf", [128, 128], F32)])
    cx = Ctx(nc, es)
    emit_l3attn(cx, hbm, hbm["xT"], hbm["yT"], stop_after=stop_after)
    cx.p.emit()
    es.close()
    return nc


def run_sublayer_l3attn(x, inputs):
    return _run_sub("l3attn", build_l3attn_program, prep_l3attn_inputs, x, inputs)


_pre_cache_store = {}


def _pre(name, prep, inputs):
    key = (name, id(inputs))
    if key not in _pre_cache_store:
        _pre_cache_store.clear()
        _pre_cache_store[key] = prep(inputs)
    return _pre_cache_store[key]


CHAIN = [
    ("l0ffn", build_l0ffn_program, prep_l0ffn_inputs),
    ("l0attn", build_l0attn_program, prep_l0attn_inputs),
    ("l0ffn", build_l0ffn_program, prep_l0ffn_inputs),
    ("l1ffn", build_l1ffn_program, prep_l1ffn_inputs),
    ("l1attn", build_l1attn_program, prep_l1attn_inputs),
    ("l1ffn", build_l1ffn_program, prep_l1ffn_inputs),
    ("l2ffn", build_l2ffn_program, prep_l2ffn_inputs),
    ("l2attn", build_l2attn_program, prep_l2attn_inputs),
    ("l2ffn", build_l2ffn_program, prep_l2ffn_inputs),
    ("l3ffn", build_l3ffn_program, prep_l3ffn_inputs),
    ("l3attn", build_l3attn_program, prep_l3attn_inputs),
    ("l3ffn", build_l3ffn_program, prep_l3ffn_inputs),
]





_SUBLAYER_SPECS = {
    "l0ffn": ([("w0T", [D, D], F32), ("b0col", [128, 8], F32), ("alpha1", [1, 1], F32)],
              prep_l0ffn_inputs),
    "l0attn": ([("wqkv", [D, 3 * D], F32), ("wproj", [D, D], F32),
                ("bqkv_col", [128, 24], F32), ("bv_row", [1, 1024], F32),
                ("bproj_col", [128, 8], F32), ("diffT", [128, 128], F32),
                ("trilT", [128, 128], F32), ("identf", [128, 128], F32),
                ("gamma1", [1, 1], F32)], prep_l0attn_inputs),
    "l1ffn": ([("w1T", [8, D, 2048], F32), ("w2T", [8, 2048, D], F32),
               ("b1col", [8, 128, 16], F32), ("b2col", [8, 128, 8], F32),
               ("wg", [128, 64], F32), ("bg_row", [1, 8], F32),
               ("sel", [8, 8 * 128], F32), ("identf", [128, 128], F32)], prep_l1ffn_inputs),
    "l1attn": ([("wq", [D, D], F32), ("wkv", [D, D], F32), ("wo", [D, D], F32),
                ("kw_col", [128, 8], F32), ("maskC", [128, 128], F32),
                ("identf", [128, 128], F32)], prep_l1attn_inputs),
    "l2ffn": ([("wqkv", [D, 3 * D], F32), ("bqkv_col", [128, 24], F32),
               ("bv_row", [1, 1024], F32), ("wgate", [128, 64], F32),
               ("bg_row", [1, 8], F32), ("wexp", [8, D, D], F32),
               ("bexp_col", [8, 128, 8], F32), ("sel", [8, 8 * 128], F32),
               ("identf", [128, 128], F32)], prep_l2ffn_inputs),
    "l2attn": ([("wqkvT", [D, 3 * D], F32), ("woutT", [D, D], F32),
                ("bout_col", [128, 8], F32), ("cosT", [128, T], F32),
                ("sinT", [128, T], F32), ("p2t", [128, 128], F32),
                ("identf", [128, 128], F32)], prep_l2attn_inputs),
    "l3ffn": ([("w1", [8, D, 2048], F32), ("w3", [8, D, 2048], F32),
               ("w2", [8, 2048, D], F32), ("sw1", [1, D, 2048], F32),
               ("sw3", [1, D, 2048], F32), ("sw2", [1, 2048, D], F32),
               ("wg", [128, 64], F32), ("bg_row", [1, 8], F32),
               ("sel", [8, 8 * 128], F32), ("identf", [128, 128], F32)], prep_l3ffn_inputs),
    "l3attn": ([("wq", [D, D], F32), ("wk", [D, 512], F32), ("wv", [D, 512], F32),
                ("wo", [D, D], F32), ("p3t", [128, 128], F32),
                ("cosT", [128, T], F32), ("sinT", [128, T], F32),
                ("maskC", [128, 128], F32), ("maskW", [128, 128], F32),
                ("dirsrow", [1, 24], F32), ("dirscol", [3, 8], F32),
                ("hs_row", [1, 8], F32), ("identf", [128, 128], F32)], prep_l3attn_inputs),
}

_FUSED_SEQ = ["l0ffn", "l0attn", "l0ffn", "l1ffn", "l1attn", "l1ffn",
              "l2ffn", "l2attn", "l2ffn", "l3ffn", "l3attn", "l3ffn"]

_EMITTERS = {}


def _emit_sublayer(name, cx, hbm, src, dst):
    if name == "l0ffn":
        wq, wtok = emit_l0ffn_prep(cx, hbm)
        emit_l0ffn(cx, wq, src, dst, wtok)
    elif name == "l0attn":
        emit_l0attn(cx, hbm, src, dst)
    elif name == "l1ffn":
        emit_l1ffn(cx, hbm, src, dst)
    elif name == "l1attn":
        emit_l1attn(cx, hbm, src, dst)
    elif name == "l2ffn":
        emit_l2ffn(cx, hbm, src, dst)
    elif name == "l2attn":
        emit_l2attn(cx, hbm, src, dst)
    elif name == "l3ffn":
        emit_l3ffn(cx, hbm, src, dst)
    elif name == "l3attn":
        emit_l3attn(cx, hbm, src, dst)
    else:
        raise KeyError(name)


def build_fused_program():
    nc = bass.Bass("TRN2", target_bir_lowering=False)
    es = ExitStack()
    xT = nc.dram_tensor("xT", [D, T], F32, kind="ExternalInput").ap()
    yT = nc.dram_tensor("yT", [D, T], F32, kind="ExternalOutput").ap()
    xs = nc.dram_tensor("xs", [D, T], F32, kind="Internal").ap()
    hbms = {}
    for name, (specs, _prep) in _SUBLAYER_SPECS.items():
        hv = {}
        for key, shape, dt in specs:
            hv[key] = nc.dram_tensor(f"{name}_{key}", shape, dt, kind="ExternalInput").ap()
        hbms[name] = hv
    cx = Ctx(nc, es)
    n = len(_FUSED_SEQ)
    for i, name in enumerate(_FUSED_SEQ):
        src = xT if i == 0 else xs
        dst = yT if i == n - 1 else xs
        bar = cx.p.now()
        cx.sub_bar = bar
        cx.ln_last_use = bar
        _emit_sublayer(name, cx, hbms[name], src, dst)
    cx.p.emit()
    es.close()
    return nc


def kernel(**inputs):
    x = np.asarray(inputs["L0_attn_x"], dtype=np.float32)
    nc = _cached("fused", build_fused_program)
    pre = {}
    for name, (_specs, prep) in _SUBLAYER_SPECS.items():
        pv = prep(inputs)
        for k, v in pv.items():
            pre[f"{name}_{k}"] = np.ascontiguousarray(np.asarray(v, dtype=np.float32))
    in_maps = [dict(pre, xT=np.ascontiguousarray(x[c].T)) for c in range(NCORES)]
    res = run_bass_kernel_spmd(nc, in_maps, core_ids=list(range(NCORES)))
    return np.ascontiguousarray(
        np.stack([r["yT"].T for r in res.results])).astype(np.float32)


def kernel_unfused(**inputs):
    x = np.asarray(inputs["L0_attn_x"], dtype=np.float32)
    xT = [np.ascontiguousarray(x[c].T) for c in range(NCORES)]
    pres = {}
    for name, build, prep in CHAIN:
        if name not in pres:
            pres[name] = prep(inputs)
    for name, build, prep in CHAIN:
        nc = _cached(name, build)
        in_maps = [dict(pres[name], xT=xT[c]) for c in range(NCORES)]
        res = run_bass_kernel_spmd(nc, in_maps, core_ids=list(range(NCORES)))
        xT = [np.ascontiguousarray(r["yT"]) for r in res.results]
    return np.ascontiguousarray(np.stack([t.T for t in xT])).astype(np.float32)
